# version 09
# baseline (speedup 1.0000x reference)
import math
import os as _os
from contextlib import ExitStack

import numpy as np

import concourse.bass as bass
import concourse.mybir as mybir
from concourse.bass_utils import run_bass_kernel_spmd

F32 = mybir.dt.float32
BF16 = mybir.dt.bfloat16
AF = mybir.ActivationFunctionType
ALU = mybir.AluOpType
AX = mybir.AxisListType

B, S, D, H, HD = 2, 8192, 2048, 16, 128
GROUPS = [[0, 1, 2, 3], [4, 5, 6, 7]]
TH = 2048
NT = TH // 128
E, SLOT, HFF = 16, 128, 512
ES = E * SLOT
MYH, MYE = 4, 4
ISQ = 1.0 / math.sqrt(HD)
LN_EPS = 1e-5
UW = D + 4

ENGS = ("sync", "tensor", "scalar", "vector", "gpsimd")


class Sem:
    def __init__(self, h):
        self.h = h
        self.cnt = 0


class KB:
    def __init__(self, stage=99, debug=False):
        self.stage = stage
        self.debug = debug
        self.ops = {e: [] for e in ENGS}

    def on(self, eng_name, fn):
        self.ops[eng_name].append(fn)

    def flush(self, name=None):
        ops, self.ops = self.ops, {e: [] for e in ENGS}
        with self.nc.Block() as blk:
            for e in ENGS:
                if ops[e]:
                    def body(eng, _ops=tuple(ops[e])):
                        for f in _ops:
                            f(eng)
                    getattr(blk, e)(body)

    def new_cc(self):
        s = self.cc_sems[self.cc_used]
        self.cc_used += 1
        return s

    def build(self):
        nc = bass.Bass("TRN2", target_bir_lowering=False)
        self.nc = nc

        self.x_home = nc.dram_tensor("x_home", [TH, D], F32, kind="ExternalInput")
        self.wq_my = nc.dram_tensor("wq_my", [D, MYH * HD], F32, kind="ExternalInput")
        self.wk_my = nc.dram_tensor("wk_my", [D, MYH * HD], F32, kind="ExternalInput")
        self.wv_my = nc.dram_tensor("wv_my", [D, MYH * HD], F32, kind="ExternalInput")
        self.woT = nc.dram_tensor("woT", [D, D], F32, kind="ExternalInput")
        self.bo = nc.dram_tensor("bo", [1, D], F32, kind="ExternalInput")
        self.gamma = nc.dram_tensor("gamma", [1, D], F32, kind="ExternalInput")
        self.beta = nc.dram_tensor("beta", [1, D], F32, kind="ExternalInput")
        self.scale_in = nc.dram_tensor("scale_in", [1, 1], F32, kind="ExternalInput")
        self.mu2d = nc.dram_tensor("mu2d", [D, ES], F32, kind="ExternalInput")
        self.w1_my = nc.dram_tensor("w1_my", [MYE, D, HFF], F32, kind="ExternalInput")
        self.b1_my = nc.dram_tensor("b1_my", [MYE, HFF], F32, kind="ExternalInput")
        self.w2_my = nc.dram_tensor("w2_my", [MYE, HFF, D], F32, kind="ExternalInput")
        self.b2_my = nc.dram_tensor("b2_my", [MYE, D], F32, kind="ExternalInput")
        self.y = nc.dram_tensor("y", [TH, D], F32, kind="ExternalOutput")

        self.d_xlnT_own = [nc.dram_tensor(f"d_xlnT_own{k}", [D, 512], BF16)
                           for k in range(4)]
        self.d_xlnT_g = [nc.dram_tensor(f"d_xlnT_g{k}", [4 * D, 512], BF16)
                         for k in range(4)]
        self.d_q = [nc.dram_tensor(f"d_q{h}", [HD, S], BF16) for h in range(MYH)]
        self.d_k = [nc.dram_tensor(f"d_k{h}", [HD, S], BF16) for h in range(MYH)]
        self.d_v = [nc.dram_tensor(f"d_v{h}", [S, HD], BF16) for h in range(MYH)]
        self.d_ctx_own = [nc.dram_tensor(f"d_ctx_own{h}", [HD, S], BF16)
                          for h in range(MYH)]
        self.d_ctx_g = [nc.dram_tensor(f"d_ctx_g{h}", [4 * HD, S], BF16)
                        for h in range(MYH)]
        self.d_x1 = nc.dram_tensor("d_x1", [TH, D], F32)
        self.d_xln3 = nc.dram_tensor("d_xln3", [TH, D], BF16)
        self.d_xln3T = nc.dram_tensor("d_xln3T", [D, TH], BF16)
        self.d_P2 = nc.dram_tensor("d_P2", [TH, ES], BF16)
        self.d_P2T = nc.dram_tensor("d_P2T", [ES, TH], BF16)
        self.d_U_p = [nc.dram_tensor(f"d_U_p{e}", [4 * SLOT, UW], F32)
                      for e in range(MYE)]
        self.d_U_m = [nc.dram_tensor(f"d_U_m{e}", [SLOT, UW], F32)
                      for e in range(MYE)]
        self.d_sp = [nc.dram_tensor(f"d_sp{e}", [SLOT, D], BF16)
                     for e in range(MYE)]
        self.d_sg = [nc.dram_tensor(f"d_sg{e}", [4 * SLOT, D], BF16)
                     for e in range(MYE)]

        self.dbg = {}
        if self.debug:
            def mkdbg(name, src, shape, dt):
                t = nc.dram_tensor("dbg_" + name, shape, dt, kind="ExternalOutput")
                self.dbg[name] = (t, src)
            for kk in range(4):
                mkdbg(f"xlnT_g{kk}", self.d_xlnT_g[kk], [4 * D, 512], BF16)
                mkdbg(f"xlnT_o{kk}", self.d_xlnT_own[kk], [D, 512], BF16)
            for h in range(MYH):
                mkdbg(f"q{h}", self.d_q[h], [HD, S], BF16)
                mkdbg(f"k{h}", self.d_k[h], [HD, S], BF16)
                mkdbg(f"v{h}", self.d_v[h], [S, HD], BF16)
            mkdbg("x1", self.d_x1, [TH, D], F32)
            mkdbg("xln3", self.d_xln3, [TH, D], BF16)
            mkdbg("xln3T", self.d_xln3T, [D, TH], BF16)
            mkdbg("P2", self.d_P2, [TH, ES], BF16)
            mkdbg("P2T", self.d_P2T, [ES, TH], BF16)

        with ExitStack() as ctx:
            en = ctx.enter_context
            self.BIGA = en(nc.sbuf_tensor("BIGA", [128, 32768], BF16))
            self.BIGB = en(nc.sbuf_tensor("BIGB", [128, 24576], BF16))
            self.STAGE = en(nc.sbuf_tensor("STAGE", [128, 7168], BF16))
            self.sbx = en(nc.sbuf_tensor("sbx", [128, D], F32))
            self.sbf1 = en(nc.sbuf_tensor("sbf1", [128, D], F32))
            self.sbf2 = en(nc.sbuf_tensor("sbf2", [128, D], F32))
            self.sbf3 = en(nc.sbuf_tensor("sbf3", [128, D], F32))
            self.evf = en(nc.sbuf_tensor("evf", [128, UW], F32))
            self.bc = en(nc.sbuf_tensor("bc_pool", [128, 3 * D], F32))
            self.Lt = en(nc.sbuf_tensor("Lt", [128, 512], F32))
            self.masks = en(nc.sbuf_tensor("masks", [128, 2048], BF16))
            self.ident = en(nc.sbuf_tensor("ident", [128, 128], BF16))
            self.ones_fc = en(nc.sbuf_tensor("ones_fc", [128, 1], F32))
            self.ones_fr = en(nc.sbuf_tensor("ones_fr", [1, 128], F32))
            self.ones_bc = en(nc.sbuf_tensor("ones_bc", [128, 1], BF16))
            self.ones4 = en(nc.sbuf_tensor("ones4", [128, 4], BF16))
            self.eps_ln = en(nc.sbuf_tensor("eps_ln", [128, 1], F32))
            self.scale_col = en(nc.sbuf_tensor("scale_col", [128, 1], F32))
            self.r_all = en(nc.sbuf_tensor("r_all", [128, NT], F32))
            self.c_mean = en(nc.sbuf_tensor("c_mean", [128, 1], F32))
            self.c_ssq = en(nc.sbuf_tensor("c_ssq", [128, 1], F32))
            self.c_var = en(nc.sbuf_tensor("c_var", [128, 1], F32))
            self.c_rstd = en(nc.sbuf_tensor("c_rstd", [128, 1], F32))
            self.c_tmp = en(nc.sbuf_tensor("c_tmp", [128, 1], F32))
            self.c_tmp2 = en(nc.sbuf_tensor("c_tmp2", [128, 1], F32))
            self.recip_row = en(nc.sbuf_tensor("recip_row", [1, 512], F32))
            self.g_r = en(nc.sbuf_tensor("g_row", [1, D], F32))
            self.psw = [en(nc.psum_tensor(f"pw{i}", [128, 1024], F32))
                        for i in range(3)]
            self.ps = [self.psw[i // 2][:, 512 * (i % 2):512 * (i % 2) + 512]
                       for i in range(6)]
            self.psT = en(nc.psum_tensor("psT", [128, 512], BF16))
            self.psW = en(nc.psum_tensor("psW", [128, 512], F32))
            for nm in ("gd", "gd2", "wd", "pe", "act", "dve", "pe2", "act2", "act3",
                       "la", "lb", "lc", "lx", "lr"):
                setattr(self, "s_" + nm, Sem(en(nc.semaphore("s_" + nm))))
            self.st_s = [Sem(en(nc.semaphore(f"s_sts{i}"))) for i in range(8)]
            self.cc_sems = [Sem(en(nc.semaphore(f"s_cc{i}"))) for i in range(16)]
            self.cc_used = 0

            self.phase0()
            if self.stage >= 1:
                self.phase_A0()
            if self.stage >= 2:
                self.phase_A2()
            if self.stage >= 3:
                self.phase_A3()
            if self.stage >= 4:
                self.phase_A5()
            if self.stage >= 5:
                self.phase_F0()
                self.phase_F2()
            if self.stage >= 6:
                self.phase_F3()
            if 4 <= self.stage < 6:
                self.flush("A5F3partial")
            if self.stage >= 7:
                self.phase_F4()
            if self.stage >= 8:
                self.phase_F5()

            if self.debug:
                k = self
                gd_now = k.s_gd.cnt + 16 * len(self.dbg)
                k.s_gd.cnt = gd_now
                st_tots_d = [(s.h, s.cnt) for s in self.st_s]
                cc_n_d = self.cc_used

                def g_dbg(g):
                    for i in range(cc_n_d):
                        g.wait_ge(self.cc_sems[i].h, 1)
                    for hh, cc in st_tots_d:
                        g.wait_ge(hh, cc)
                    for name, (t, src) in k.dbg.items():
                        g.dma_start(t.ap(), src.ap()).then_inc(k.s_gd.h, 16)
                    g.wait_ge(k.s_gd.h, gd_now)
                self.on("gpsimd", g_dbg)
                self.flush("dbg")

            st_tots = [(s.h, s.cnt) for s in self.st_s]
            gd_f, gd2_f, wd_f = self.s_gd.cnt, self.s_gd2.cnt, self.s_wd.cnt
            cc_n = self.cc_used

            def fin(sy):
                for hh, cc in st_tots:
                    sy.wait_ge(hh, cc)
                sy.wait_ge(self.s_gd.h, gd_f)
                sy.wait_ge(self.s_gd2.h, gd2_f)
                sy.wait_ge(self.s_wd.h, wd_f)
                for i in range(cc_n):
                    sy.wait_ge(self.cc_sems[i].h, 1)
            self.on("sync", fin)
            self.flush("fin")
        return nc

    def phase0(self):
        k = self
        k.s_gd2.cnt += 7
        gd_target = k.s_gd.cnt + 16 * 48

        def g_body(g):
            g.memset(k.masks[:, :], 1.0).then_inc(k.s_gd2.h, 1)
            g.wait_ge(k.s_gd2.h, 1)
            for r in range(4):
                g.affine_select(out=k.masks[:, 512 * r:512 * (r + 1)],
                                in_=k.masks[:, 512 * r:512 * (r + 1)],
                                compare_op=ALU.is_ge, fill=0.0, base=-128 * r,
                                pattern=[[1, 512]], channel_multiplier=-1
                                ).then_inc(k.s_gd2.h, 1)
            g.memset(k.ident[:, :], 0.0).then_inc(k.s_gd2.h, 1)
            g.wait_ge(k.s_gd2.h, 6)
            g.affine_select(out=k.ident[:, :], in_=k.ident[:, :],
                            compare_op=ALU.not_equal, fill=1.0, base=0,
                            pattern=[[-1, 128]], channel_multiplier=1)
            g.memset(k.ones_fc[:, :], 1.0)
            g.memset(k.ones_fr[:, :], 1.0)
            g.memset(k.ones_bc[:, :], 1.0)
            g.memset(k.ones4[:, :], 0.0).then_inc(k.s_gd2.h, 1)
            g.wait_ge(k.s_gd2.h, 7)
            g.memset(k.ones4[:, 0:1], 1.0)
            g.memset(k.eps_ln[:, :], LN_EPS)
            if _os.environ.get("SIMINIT", "0") == "1":
                g.memset(k.BIGA[:, :], 0.0)
                g.memset(k.STAGE[:, :], 0.0)
            for wi, w in enumerate((k.wq_my, k.wk_my, k.wv_my)):
                for dc in range(16):
                    g.dma_start(
                        k.BIGB[:, 8192 * wi + 512 * dc:8192 * wi + 512 * (dc + 1)],
                        w[128 * dc:128 * (dc + 1), :],
                    ).then_inc(k.s_gd.h, 16)
            g.wait_ge(k.s_gd.h, gd_target)
        k.s_gd.cnt += 16 * 48
        self.on("gpsimd", g_body)
        self.flush("p0")

        for src_ap, width, dst_fn in (
            (self.gamma[:, :], D, lambda j: self.bc[:, 512 * j:512 * (j + 1)]),
            (self.beta[:, :], D, lambda j: self.bc[:, D + 512 * j:D + 512 * (j + 1)]),
            (self.bo[:, :], D, lambda j: self.bc[:, 2 * D + 512 * j:2 * D + 512 * (j + 1)]),
            (self.scale_in[:, :], 1, lambda j: self.scale_col[:, :]),
        ):
            nsl = (width + 511) // 512
            k.s_lr.cnt += 16
            ld_at = k.s_lr.cnt
            pe_at, ac_at = [], []
            for j in range(nsl):
                k.s_pe.cnt += 1
                pe_at.append(k.s_pe.cnt)
                k.s_act.cnt += 1
                ac_at.append(k.s_act.cnt)

            def sy_body(sy, src_ap=src_ap, width=width):
                sy.dma_start(k.g_r[:, 0:width], src_ap).then_inc(k.s_lr.h, 16)
            self.on("sync", sy_body)

            def pe_body(t, nsl=nsl, width=width, ld_at=ld_at, ac_at=tuple(ac_at)):
                t.wait_ge(k.s_lr.h, ld_at)
                for j in range(nsl):
                    w = min(512, width - 512 * j)
                    if j > 0:
                        t.wait_ge(k.s_act.h, ac_at[j - 1])
                    t.matmul(k.psW[:, 0:w], k.ones_fr[:, :],
                             k.g_r[:, 512 * j:512 * j + w],
                             start=True, stop=True).then_inc(k.s_pe.h, 1)
            self.on("tensor", pe_body)

            def sc_body(sc, nsl=nsl, width=width, dst_fn=dst_fn,
                        pe_at=tuple(pe_at)):
                for j in range(nsl):
                    w = min(512, width - 512 * j)
                    sc.wait_ge(k.s_pe.h, pe_at[j])
                    sc.copy(dst_fn(j), k.psW[:, 0:w]).then_inc(k.s_act.h, 1)
            self.on("scalar", sc_body)
            self.flush("bcast")

    def phase_A0(self):
        k = self
        k.e_a0_ln = [0] * NT
        ld_x = [0] * NT
        red_done = [0] * NT
        meansq_done = [0] * NT
        var_done = [0] * NT
        xln_done = [0] * NT
        ssq_done = [0] * NT
        mean_done = [0] * NT
        rstd_done = [0] * NT
        tp_done = [[0] * 4 for _ in range(NT)]
        evac_done = [[0] * 4 for _ in range(NT)]
        st_done = [[0] * 4 for _ in range(NT)]

        for ti in range(NT):
            k.s_lx.cnt += 16
            ld_x[ti] = k.s_lx.cnt
            k.s_dve.cnt += 1
            red_done[ti] = k.s_dve.cnt
            k.s_act.cnt += 1
            ssq_done[ti] = k.s_act.cnt
            k.s_act.cnt += 1
            mean_done[ti] = k.s_act.cnt
            k.s_dve.cnt += 2
            meansq_done[ti] = k.s_dve.cnt
            k.s_dve.cnt += 1
            var_done[ti] = k.s_dve.cnt
            k.s_act3.cnt += 1
            k.e_a0_ln[ti] = k.s_act3.cnt
            k.s_act.cnt += 1
            rstd_done[ti] = k.s_act.cnt
            k.s_dve.cnt += 1
            xln_done[ti] = k.s_dve.cnt
            for q in range(4):
                k.s_pe.cnt += 4
                tp_done[ti][q] = k.s_pe.cnt
                k.s_act2.cnt += 1
                evac_done[ti][q] = k.s_act2.cnt
                k.st_s[q].cnt += 16
                st_done[ti][q] = k.st_s[q].cnt
        cc_gather = [self.new_cc() for _ in range(4)]
        self.cc_gather = cc_gather
        st_tot = [(k.st_s[q].h, k.st_s[q].cnt) for q in range(4)]

        def sy_body(sy):
            for ti in range(NT):
                if ti >= 1:
                    sy.wait_ge(k.s_dve.h, xln_done[ti - 1])
                    sy.wait_ge(k.s_act.h, ssq_done[ti - 1])
                sy.dma_start(k.sbx[:, :],
                             k.x_home[128 * ti:128 * (ti + 1), :]
                             ).then_inc(k.s_lx.h, 16)
        self.on("sync", sy_body)

        def ve_body(v):
            for ti in range(NT):
                v.wait_ge(k.s_lx.h, ld_x[ti])
                if ti >= 1:
                    v.wait_ge(k.s_act.h, rstd_done[ti - 1])
                    v.wait_ge(k.s_pe.h, tp_done[ti - 1][3])
                v.reduce_sum(k.c_tmp[:, :], k.sbx[:, :],
                             axis=AX.X).then_inc(k.s_dve.h, 1)
                v.wait_ge(k.s_act.h, mean_done[ti])
                v.tensor_scalar_mul(k.c_var[:, :], k.c_ssq[:, :],
                                    1.0 / D).then_inc(k.s_dve.h, 1)
                v.tensor_mul(k.c_tmp2[:, :], k.c_mean[:, :],
                             k.c_mean[:, :]).then_inc(k.s_dve.h, 1)
                v.wait_ge(k.s_dve.h, meansq_done[ti])
                v.tensor_sub(k.c_var[:, :], k.c_var[:, :],
                             k.c_tmp2[:, :]).then_inc(k.s_dve.h, 1)
                v.wait_ge(k.s_act.h, rstd_done[ti])
                v.tensor_scalar(k.STAGE[:, 0:2048], k.sbx[:, :],
                                k.c_mean[:, :], k.c_rstd[:, :],
                                ALU.subtract, ALU.mult).then_inc(k.s_dve.h, 1)
        self.on("vector", ve_body)

        def sc_body(sc):
            for ti in range(NT):
                sc.wait_ge(k.s_lx.h, ld_x[ti])
                if ti >= 1:
                    sc.wait_ge(k.s_dve.h, xln_done[ti - 1])
                sc.activation(k.sbf1[:, :], k.sbx[:, :], AF.Square,
                              accum_out=k.c_ssq[:, :]).then_inc(k.s_act.h, 1)
                sc.wait_ge(k.s_dve.h, red_done[ti])
                sc.mul(k.c_mean[:, :], k.c_tmp[:, :],
                       1.0 / D).then_inc(k.s_act.h, 1)
                sc.wait_ge(k.s_dve.h, var_done[ti])
                sc.activation(k.c_rstd[:, :], k.c_var[:, :], AF.Ln,
                              bias=k.eps_ln[:, :]).then_inc(k.s_act3.h, 1)
                sc.wait_ge(k.s_act3.h, k.e_a0_ln[ti])
                sc.activation(k.c_rstd[:, :], k.c_rstd[:, :], AF.Exp,
                              scale=-0.5).then_inc(k.s_act.h, 1)
                for q in range(4):
                    sc.wait_ge(k.s_pe.h, tp_done[ti][q])
                    sl = 2048 + 512 * q
                    if ti >= 1:
                        sc.wait_ge(k.st_s[q].h, st_done[ti - 1][q])
                    sc.copy(k.STAGE[:, sl:sl + 512],
                            k.psT[:, :]).then_inc(k.s_act2.h, 1)
                    sc.wait_ge(k.s_act2.h, evac_done[ti][q])
                    kk = ti // 4
                    dst = k.d_xlnT_own[kk][512 * q:512 * (q + 1),
                                           128 * (ti % 4):128 * (ti % 4) + 128]
                    dst = dst.rearrange("(i p) t -> p i t", p=128)
                    srcv = k.STAGE[:, sl:sl + 512].rearrange(
                        "p (i t) -> p i t", i=4)
                    sc.dma_start(dst, srcv).then_inc(k.st_s[q].h, 16)
        self.on("scalar", sc_body)

        def pe_body(t):
            for ti in range(NT):
                t.wait_ge(k.s_dve.h, xln_done[ti])
                for q in range(4):
                    gi = ti * 4 + q
                    if gi >= 1:
                        pti, pq = divmod(gi - 1, 4)
                        t.wait_ge(k.s_act2.h, evac_done[pti][pq])
                    for i in range(4):
                        dc = 4 * q + i
                        t.transpose(k.psT[:, 128 * i:128 * (i + 1)],
                                    k.STAGE[:, 128 * dc:128 * (dc + 1)],
                                    k.ident[:, :]).then_inc(k.s_pe.h, 1)
        self.on("tensor", pe_body)

        self.a0_spill_tots = st_tot
        self.flush("A0")

    def phase_A2(self):
        NTG = int(_os.environ.get("A2_TG", "16"))
        NOUT = int(_os.environ.get("A2_OUT", "12"))
        NOV = _os.environ.get("A2_NOV", "0") == "1"
        NOVST = _os.environ.get("A2_NOVST", "0") == "1"
        k = self
        ld_sem = lambda tg: (k.s_la if tg % 2 == 0 else k.s_lb)
        ld_done = [0] * 16
        mm_done = [[0] * 12 for _ in range(16)]
        evac_done = [[0] * 12 for _ in range(16)]
        vtp_done = [[0] * 4 for _ in range(16)]
        vte_done = [[0] * 4 for _ in range(16)]
        st_qk = [[0] * 8 for _ in range(16)]
        st_v = [[0] * 4 for _ in range(16)]
        for tg in range(NTG):
            ld_sem(tg).cnt += 16 * 16
            ld_done[tg] = ld_sem(tg).cnt
            for out in range(NOUT):
                k.s_pe.cnt += 1
                mm_done[tg][out] = k.s_pe.cnt
                k.s_act2.cnt += 1
                evac_done[tg][out] = k.s_act2.cnt
                if out < 8:
                    k.st_s[out % 4].cnt += 16
                    st_qk[tg][out] = k.st_s[out % 4].cnt
            if not NOV:
                for hl in range(4):
                    k.s_pe2.cnt += 4
                    vtp_done[tg][hl] = k.s_pe2.cnt
                    k.s_act3.cnt += 1
                    vte_done[tg][hl] = k.s_act3.cnt
                    k.st_s[4 + hl % 2].cnt += 16
                    st_v[tg][hl] = k.st_s[4 + hl % 2].cnt
        self.a2_st_tots = [(s.h, s.cnt) for s in self.st_s[0:6]]

        def a2g_body(g):
            for hh, cc in self.a0_spill_tots:
                g.wait_ge(hh, cc)
            for kk in range(4):
                if kk >= 1:
                    g.wait_ge(self.cc_gather[kk - 1].h, 1)
                g.collective_compute(
                    "AllGather", ALU.bypass, replica_groups=GROUPS,
                    ins=[k.d_xlnT_own[kk].ap().opt()],
                    outs=[k.d_xlnT_g[kk].ap().opt()],
                ).then_inc(self.cc_gather[kk].h)
        self.on("gpsimd", a2g_body)

        def sy_body(sy):
            for tg in range(NTG):
                kk, r = tg // 4, tg % 4
                sy.wait_ge(self.cc_gather[kk].h, 1)
                half = (tg % 2) * 8192
                if tg >= 2:
                    sy.wait_ge(k.s_pe.h, mm_done[tg - 2][NOUT - 1])
                for dc in range(16):
                    sy.dma_start(
                        k.BIGA[:, half + 512 * dc:half + 512 * (dc + 1)],
                        k.d_xlnT_g[kk][2048 * r + 128 * dc:
                                       2048 * r + 128 * (dc + 1), :],
                    ).then_inc(ld_sem(tg).h, 16)
        self.on("sync", sy_body)

        def pe_body(t):
            for tg in range(NTG):
                half = (tg % 2) * 8192
                t.wait_ge(ld_sem(tg).h, ld_done[tg])
                for out in range(NOUT):
                    wi, hl = divmod(out, 4)
                    bank = out % 4
                    gi = tg * NOUT + out
                    if gi >= 4:
                        pt, po = divmod(gi - 4, NOUT)
                        t.wait_ge(k.s_act2.h, evac_done[pt][po])
                    for dc in range(16):
                        mm = t.matmul(
                            k.ps[bank][:, :],
                            k.BIGB[:, 8192 * wi + 512 * dc + 128 * hl:
                                   8192 * wi + 512 * dc + 128 * (hl + 1)],
                            k.BIGA[:, half + 512 * dc:half + 512 * (dc + 1)],
                            start=(dc == 0), stop=(dc == 15))
                        if dc == 15:
                            mm.then_inc(k.s_pe.h, 1)
                for hl in range(0 if NOV else 4):
                    t.wait_ge(k.s_act2.h, evac_done[tg][8 + hl])
                    gi = tg * 4 + hl
                    if gi >= 1:
                        pt, ph = divmod(gi - 1, 4)
                        t.wait_ge(k.s_act3.h, vte_done[pt][ph])
                    vsl = 4096 + 512 * (hl % 2)
                    for i in range(4):
                        t.transpose(k.psT[:, 128 * i:128 * (i + 1)],
                                    k.STAGE[:, vsl + 128 * i:vsl + 128 * (i + 1)],
                                    k.ident[:, :]).then_inc(k.s_pe2.h, 1)
        self.on("tensor", pe_body)

        def sc_body(sc):
            for tg in range(NTG):
                kk, r = tg // 4, tg % 4
                tok = r * 2048 + 512 * kk
                def evac_v(hl):
                    bank = hl
                    vsl = 4096 + 512 * (hl % 2)
                    sc.wait_ge(k.s_pe.h, mm_done[tg][8 + hl])
                    if tg * 4 + hl >= 2:
                        pidx = tg * 4 + hl - 2
                        pt, ph = divmod(pidx, 4)
                        sc.wait_ge(k.s_pe2.h, vtp_done[pt][ph])
                    sc.copy(k.STAGE[:, vsl:vsl + 512],
                            k.ps[bank][:, :]).then_inc(k.s_act2.h, 1)
                for out in range(min(NOUT, 8)):
                    wi, hl = divmod(out, 4)
                    bank = out % 4
                    sc.wait_ge(k.s_pe.h, mm_done[tg][out])
                    sl = 512 * (out % 4)
                    if tg * 8 + out >= 4:
                        pidx = tg * 8 + out - 4
                        pt, po = divmod(pidx, 8)
                        sc.wait_ge(k.st_s[out % 4].h, st_qk[pt][po])
                    sc.copy(k.STAGE[:, sl:sl + 512],
                            k.ps[bank][:, :]).then_inc(k.s_act2.h, 1)
                    sc.wait_ge(k.s_act2.h, evac_done[tg][out])
                    dstt = (k.d_q if wi == 0 else k.d_k)[hl]
                    sc.dma_start(dstt[:, tok:tok + 512],
                                 k.STAGE[:, sl:sl + 512]
                                 ).then_inc(k.st_s[out % 4].h, 16)
                if NOUT > 8 and not NOV:
                    evac_v(0)
                    evac_v(1)
                for hl in range(0 if (NOV or NOUT <= 8) else 4):
                    sc.wait_ge(k.s_pe2.h, vtp_done[tg][hl])
                    osl = 6144 + 512 * (hl % 2)
                    if tg * 4 + hl >= 2:
                        pidx = tg * 4 + hl - 2
                        pt, ph = divmod(pidx, 4)
                        sc.wait_ge(k.st_s[4 + hl % 2].h, st_v[pt][ph])
                    sc.copy(k.STAGE[:, osl:osl + 512],
                            k.psT[:, :]).then_inc(k.s_act3.h, 1)
                    if hl + 2 <= 3:
                        evac_v(hl + 2)
                    if NOVST:
                        sc.dma_start(k.d_v[hl][tok:tok + 512, :],
                                     k.STAGE[:, osl:osl + 512].rearrange(
                                         "p (i c) -> (i p) c", i=4)
                                     ).then_inc(k.st_s[4 + hl % 2].h, 16) if False else None
                        sc.nop().then_inc(k.st_s[4 + hl % 2].h, 16)
                    else:
                        sc.wait_ge(k.s_act3.h, vte_done[tg][hl])
                        dst = k.d_v[hl][tok:tok + 512, :].rearrange(
                            "(i p) c -> p i c", p=128)
                        srcv = k.STAGE[:, osl:osl + 512].rearrange(
                            "p (i c) -> p i c", i=4)
                        sc.dma_start(dst, srcv).then_inc(k.st_s[4 + hl % 2].h, 16)
        self.on("scalar", sc_body)
        self.flush("A2")

    def phase_A3(self):
        k = self
        steps_all = []
        for gg in range(16):
            for tt in range(4 * gg + 4):
                steps_all.append((gg, tt))
        sidx = {s: i for i, s in enumerate(steps_all)}
        NS = len(steps_all)
        CB = [4, 5]

        ld_sem = lambda hh: (k.s_la if hh % 2 == 0 else k.s_lb)
        qkv_ld = [0] * MYH
        smm, expd, maskd, ladd, ctxd = {}, {}, {}, {}, {}
        lsum_d, recip_d, bcast_d, bccp_d, norm_d, stout_d = {}, {}, {}, {}, {}, {}
        head_done = [0] * MYH
        NG = MYH * 16

        def walk():
            for hh in range(MYH):
                ld_sem(hh).cnt += 16 * 3
                qkv_ld[hh] = ld_sem(hh).cnt
                for g in range(16):
                    gi = hh * 16 + g
                    K = 4 * g + 4
                    for kt in range(K):
                        k.s_pe.cnt += 1
                        smm[(hh, g, kt)] = k.s_pe.cnt
                        if kt == 0 and gi >= 1:
                            k.s_pe2.cnt += 1
                            bcast_d[gi - 1] = k.s_pe2.cnt
                        if kt % 2 == 1:
                            k.s_act.cnt += 1
                            expd[(hh, g, kt)] = k.s_act.cnt
                            expd[(hh, g, kt - 1)] = k.s_act.cnt
                        if kt == 1 and gi >= 1:
                            k.s_act2.cnt += 1
                            bccp_d[gi - 1] = k.s_act2.cnt
                        if kt == min(5, K - 1) and gi >= 1:
                            k.st_s[6].cnt += 16
                            stout_d[gi - 1] = k.st_s[6].cnt
                        if kt >= 4 * g:
                            k.s_dve.cnt += 1
                            maskd[(hh, g, kt)] = k.s_dve.cnt
                        if kt % 2 == 1:
                            k.s_dve.cnt += 1
                            ladd[(hh, g, kt)] = k.s_dve.cnt
                            ladd[(hh, g, kt - 1)] = k.s_dve.cnt
                        k.s_pe2.cnt += 1
                        ctxd[(hh, g, kt)] = k.s_pe2.cnt
                    k.s_pe2.cnt += 1
                    lsum_d[gi] = k.s_pe2.cnt
                    k.s_dve.cnt += 1
                    recip_d[gi] = k.s_dve.cnt
                    k.s_dve.cnt += 1
                    norm_d[gi] = k.s_dve.cnt
                head_done[hh] = k.s_pe2.cnt
            k.s_pe2.cnt += 1
            bcast_d[NG - 1] = k.s_pe2.cnt
            k.s_act2.cnt += 1
            bccp_d[NG - 1] = k.s_act2.cnt
            k.st_s[6].cnt += 16
            stout_d[NG - 1] = k.st_s[6].cnt
        walk()
        a2_gate = self.a2_st_tots
        ctx_st_tot = k.st_s[6].cnt

        def prev_step(hh, g, kt, back):
            i = sidx[(g, kt)]
            if i >= back:
                pg, pk = steps_all[i - back]
                return (hh, pg, pk)
            if hh > 0:
                pg, pk = steps_all[NS - back + i]
                return (hh - 1, pg, pk)
            return None

        def pt_ready(hh, g, kt):
            if kt >= 4 * g:
                return (k.s_dve.h, maskd[(hh, g, kt)])
            return (k.s_act.h, expd[(hh, g, kt)])

        def sy_body(sy):
            for hh, cc in a2_gate:
                sy.wait_ge(hh, cc)
            for hh in range(MYH):
                BIG = k.BIGA if hh % 2 == 0 else k.BIGB
                if hh >= 2:
                    sy.wait_ge(k.s_pe2.h, head_done[hh - 2])
                sy.dma_start(BIG[:, 0:8192], k.d_q[hh][:, :]
                             ).then_inc(ld_sem(hh).h, 16)
                sy.dma_start(BIG[:, 8192:16384], k.d_k[hh][:, :]
                             ).then_inc(ld_sem(hh).h, 16)
                sy.dma_start(
                    BIG[:, 16384:24576].rearrange("p (kt c) -> p kt c", c=128),
                    k.d_v[hh].rearrange("(kt p) c -> p kt c", p=128),
                ).then_inc(ld_sem(hh).h, 16)
        self.on("sync", sy_body)

        def emit_bcast(t, gi):
            t.wait_ge(k.s_dve.h, recip_d[gi])
            t.matmul(k.psW[:, :], k.ones_fr[:, :], k.recip_row[:, :],
                     start=True, stop=True, skip_group_check=True
                     ).then_inc(k.s_pe2.h, 1)

        def pe_body(t):
            for hh in range(MYH):
                BIG = k.BIGA if hh % 2 == 0 else k.BIGB
                t.wait_ge(ld_sem(hh).h, qkv_ld[hh])
                for g in range(16):
                    gi = hh * 16 + g
                    K = 4 * g + 4
                    for step in range(K + 1):
                        if step < K:
                            kt = step
                            ps4s = prev_step(hh, g, kt, 4)
                            if ps4s is not None:
                                t.wait_ge(k.s_act.h, expd[ps4s])
                            PT = k.psw[(kt // 2) % 2]
                            t.matmul(PT[:, 512 * (kt % 2):512 * (kt % 2) + 512],
                                     BIG[:, 8192 + 128 * kt:8192 + 128 * (kt + 1)],
                                     BIG[:, 512 * g:512 * (g + 1)],
                                     start=True, stop=True,
                                     skip_group_check=True
                                     ).then_inc(k.s_pe.h, 1)
                            if kt == 0 and gi >= 1:
                                emit_bcast(t, gi - 1)
                        if step >= 1:
                            kt = step - 1
                            sh, cn = pt_ready(hh, g, kt)
                            t.wait_ge(sh, cn)
                            if kt == 0 and gi >= 2:
                                t.wait_ge(k.s_dve.h, norm_d[gi - 2])
                            t.matmul(k.ps[CB[gi % 2]][:, :],
                                     BIG[:, 16384 + 128 * kt:16384 + 128 * (kt + 1)],
                                     k.STAGE[:, 512 * (kt % 4):512 * (kt % 4) + 512],
                                     start=(kt == 0), stop=(kt == K - 1),
                                     skip_group_check=True
                                     ).then_inc(k.s_pe2.h, 1)
                    t.wait_ge(k.s_dve.h, ladd[(hh, g, K - 1)])
                    if gi >= 1:
                        t.wait_ge(k.s_act2.h, bccp_d[gi - 1])
                    t.matmul(k.psW[0:1, :], k.ones_fc[:, :],
                             k.evf[:, 0:512],
                             start=True, stop=False, skip_group_check=True)
                    t.matmul(k.psW[0:1, :], k.ones_fc[:, :],
                             k.evf[:, 512:1024],
                             start=False, stop=True, skip_group_check=True
                             ).then_inc(k.s_pe2.h, 1)
            emit_bcast(t, NG - 1)
        self.on("tensor", pe_body)

        def emit_bccp(sc, gi):
            sc.wait_ge(k.s_pe2.h, bcast_d[gi])
            if gi >= 1:
                sc.wait_ge(k.s_dve.h, norm_d[gi - 1])
            sc.copy(k.sbf1[:, 0:512], k.psW[:, :]).then_inc(k.s_act2.h, 1)

        def emit_store(sc, gi):
            hh, g = divmod(gi, 16)
            sc.wait_ge(k.s_dve.h, norm_d[gi])
            if gi >= 1:
                sc.wait_ge(k.st_s[6].h, stout_d[gi - 1])
            dst = k.d_ctx_own[hh][:, 512 * g:512 * (g + 1)]
            sc.dma_start(dst, k.STAGE[:, 2048 + 512 * (gi % 2):
                                      2560 + 512 * (gi % 2)]
                         ).then_inc(k.st_s[6].h, 16)

        def sc_body(sc):
            for hh, cc in a2_gate:
                sc.wait_ge(hh, cc)
            for hh in range(MYH):
                for g in range(16):
                    gi = hh * 16 + g
                    K = 4 * g + 4
                    for kt in range(K):
                        if kt % 2 == 1:
                            sc.wait_ge(k.s_pe.h, smm[(hh, g, kt)])
                            ps4b = prev_step(hh, g, kt, 4)
                            if ps4b is not None:
                                sc.wait_ge(k.s_pe2.h, ctxd[ps4b])
                                sc.wait_ge(k.s_dve.h, ladd[ps4b])
                            psl = 512 * ((kt - 1) % 4)
                            sc.activation(
                                k.STAGE[:, psl:psl + 1024],
                                k.psw[(kt // 2) % 2][:, :], AF.Exp,
                                scale=ISQ).then_inc(k.s_act.h, 1)
                        if kt == 1 and gi >= 1:
                            emit_bccp(sc, gi - 1)
                        if kt == min(5, K - 1) and gi >= 1:
                            emit_store(sc, gi - 1)
            emit_bccp(sc, NG - 1)
            emit_store(sc, NG - 1)
        self.on("scalar", sc_body)

        def ve_body(v):
            for hh, cc in a2_gate:
                v.wait_ge(hh, cc)
            for hh in range(MYH):
                for g in range(16):
                    gi = hh * 16 + g
                    K = 4 * g + 4
                    for kt in range(K):
                        v.wait_ge(k.s_act.h, expd[(hh, g, kt)])
                        sl = 512 * (kt % 4)
                        if kt >= 4 * g:
                            r = kt - 4 * g
                            v.tensor_mul(k.STAGE[:, sl:sl + 512],
                                         k.STAGE[:, sl:sl + 512],
                                         k.masks[:, 512 * r:512 * (r + 1)]
                                         ).then_inc(k.s_dve.h, 1)
                            v.wait_ge(k.s_dve.h, maskd[(hh, g, kt)])
                        if kt % 2 == 1:
                            psl = 512 * ((kt - 1) % 4)
                            if kt == 1:
                                if gi >= 1:
                                    v.wait_ge(k.s_pe2.h, lsum_d[gi - 1])
                                v.tensor_copy(k.evf[:, 0:1024],
                                              k.STAGE[:, psl:psl + 1024]
                                              ).then_inc(k.s_dve.h, 1)
                            else:
                                v.wait_ge(k.s_dve.h, ladd[(hh, g, kt - 2)])
                                v.scalar_tensor_tensor(
                                    k.evf[:, 0:1024],
                                    k.STAGE[:, psl:psl + 1024], 1.0,
                                    k.evf[:, 0:1024], ALU.mult, ALU.add
                                ).then_inc(k.s_dve.h, 1)
                    v.wait_ge(k.s_pe2.h, lsum_d[gi])
                    v.reciprocal(k.recip_row[:, :], k.psW[0:1, :]
                                 ).then_inc(k.s_dve.h, 1)
                    v.wait_ge(k.s_act2.h, bccp_d[gi])
                    v.wait_ge(k.s_dve.h, recip_d[gi])
                    if gi >= 2:
                        v.wait_ge(k.st_s[6].h, stout_d[gi - 2])
                    v.tensor_tensor(k.STAGE[:, 2048 + 512 * (gi % 2):
                                            2560 + 512 * (gi % 2)],
                                    k.ps[CB[gi % 2]][:, :],
                                    k.sbf1[:, 0:512], ALU.mult
                                    ).then_inc(k.s_dve.h, 1)
        self.on("vector", ve_body)

        cc_a2a = [self.new_cc() for _ in range(MYH)]
        self.cc_a2a = cc_a2a
        head_st = [stout_d[16 * hh + 15] for hh in range(MYH)]

        def g_body(g):
            for hh in range(MYH):
                g.wait_ge(k.st_s[6].h, head_st[hh])
                if hh >= 1:
                    g.wait_ge(cc_a2a[hh - 1].h, 1)
                g.collective_compute(
                    "AllGather", ALU.bypass, replica_groups=GROUPS,
                    ins=[k.d_ctx_own[hh].ap().opt()],
                    outs=[k.d_ctx_g[hh].ap().opt()],
                ).then_inc(cc_a2a[hh].h)
        self.on("gpsimd", g_body)
        self.flush("A3")

    def phase_A5(self):
        k = self
        gd_w = k.s_gd.cnt + 16 * 16
        k.s_gd.cnt = gd_w
        self.a5_gdw = gd_w

        def g_body(g):
            g.wait_ge(self.cc_a2a[0].h, 1)
            for dc in range(16):
                g.dma_start(k.BIGA[:, 2048 * dc:2048 * (dc + 1)],
                            k.woT[128 * dc:128 * (dc + 1), :]
                            ).then_inc(k.s_gd.h, 16)
        self.on("gpsimd", g_body)

        ld_sem = lambda ti: (k.s_la if ti % 2 == 0 else k.s_lb)
        e_ln = [[0] * 3 for _ in range(NT)]
        ld_ctx = [0] * NT
        ld_x = [0] * NT
        o_done = [[0] * 4 for _ in range(NT)]
        x1p_done = [[0] * 4 for _ in range(NT)]
        xbo_done = [0] * NT
        m = {}
        xln3_done = [0] * NT
        x1st_done = [0] * NT
        x3st_done = [0] * NT
        tp_done = [[0] * 4 for _ in range(NT)]
        tpe_done = [[0] * 4 for _ in range(NT)]
        st_sp = [[0] * 4 for _ in range(NT)]
        rt_done = [0] * NT
        act_last = [0] * NT

        for ti in range(NT):
            ld_sem(ti).cnt += 16 * 4
            ld_ctx[ti] = ld_sem(ti).cnt
            k.s_lx.cnt += 16
            ld_x[ti] = k.s_lx.cnt
            k.s_dve.cnt += 1
            xbo_done[ti] = k.s_dve.cnt
            for jg in range(4):
                k.s_pe.cnt += 1
                o_done[ti][jg] = k.s_pe.cnt
                k.s_dve.cnt += 1
                x1p_done[ti][jg] = k.s_dve.cnt
            k.st_s[7].cnt += 16
            x1st_done[ti] = k.st_s[7].cnt
            k.s_dve.cnt += 1
            m[("red2", ti)] = k.s_dve.cnt
            k.s_act.cnt += 2
            m[("ms2", ti)] = k.s_act.cnt
            k.s_dve.cnt += 3
            m[("var2", ti)] = k.s_dve.cnt
            k.s_act3.cnt += 1
            e_ln[ti][0] = k.s_act3.cnt
            k.s_act.cnt += 1
            m[("rstd2", ti)] = k.s_act.cnt
            k.s_dve.cnt += 1
            m[("xln2", ti)] = k.s_dve.cnt
            k.s_dve.cnt += 1
            m[("red3", ti)] = k.s_dve.cnt
            k.s_act.cnt += 2
            m[("ms3", ti)] = k.s_act.cnt
            k.s_dve.cnt += 3
            m[("var3", ti)] = k.s_dve.cnt
            k.s_act3.cnt += 1
            e_ln[ti][1] = k.s_act3.cnt
            k.s_act.cnt += 1
            m[("rstd3", ti)] = k.s_act.cnt
            k.s_dve.cnt += 3
            m[("x3f", ti)] = k.s_dve.cnt
            k.s_dve.cnt += 1
            xln3_done[ti] = k.s_dve.cnt
            k.st_s[5].cnt += 16
            x3st_done[ti] = k.st_s[5].cnt
            k.s_act.cnt += 1
            m[("rt_sq", ti)] = k.s_act.cnt
            k.s_act3.cnt += 1
            e_ln[ti][2] = k.s_act3.cnt
            k.s_act.cnt += 1
            m[("rt_act", ti)] = k.s_act.cnt
            act_last[ti] = k.s_act.cnt
            k.s_dve.cnt += 2
            rt_done[ti] = k.s_dve.cnt
            for q in range(4):
                k.s_pe2.cnt += 4
                tp_done[ti][q] = k.s_pe2.cnt
                k.s_act2.cnt += 1
                tpe_done[ti][q] = k.s_act2.cnt
                k.st_s[q].cnt += 16
                st_sp[ti][q] = k.st_s[q].cnt
        self.a5_sp_tots = [(k.st_s[q].h, k.st_s[q].cnt) for q in range(4)]
        self.a5_x3_tot = (k.st_s[5].h, k.st_s[5].cnt)
        self.a5_x1_tot = (k.st_s[7].h, k.st_s[7].cnt)

        def sy_body(sy):
            for cc in self.cc_a2a:
                sy.wait_ge(cc.h, 1)
            tok0 = (sy.partition_id() % 4) * 2048
            ctx3d = [k.d_ctx_g[hh].ap().rearrange("(r p) t -> p r t", p=128)
                     for hh in range(MYH)]
            for ti in range(NT):
                half = (ti % 2) * 2048
                if ti >= 2:
                    sy.wait_ge(k.s_pe.h, o_done[ti - 2][3])
                for hh in range(MYH):
                    srcv = ctx3d[hh][:, :, 128 * ti:][:, :, bass.ds(tok0, 128)]
                    dst = k.BIGB[:, half + 512 * hh:half + 512 * (hh + 1)
                                 ].rearrange("p (r t) -> p r t", t=128)
                    sy.dma_start(dst, srcv).then_inc(ld_sem(ti).h, 16)
                if ti >= 1:
                    sy.wait_ge(k.s_dve.h, xbo_done[ti - 1])
                    sy.wait_ge(k.s_act.h, m[("ms2", ti - 1)])
                sy.dma_start(k.sbx[:, :], k.x_home[128 * ti:128 * (ti + 1), :]
                             ).then_inc(k.s_lx.h, 16)
        self.on("sync", sy_body)

        def pe_body(t):
            t.wait_ge(k.s_gd.h, gd_w)
            for ti in range(NT):
                half = (ti % 2) * 2048
                t.wait_ge(ld_sem(ti).h, ld_ctx[ti])
                for jg in range(4):
                    if ti >= 1:
                        t.wait_ge(k.s_dve.h, x1p_done[ti - 1][jg])
                    for li in range(16):
                        hh4, r = divmod(li, 4)
                        mc = 4 * r + hh4
                        mm = t.matmul(
                            k.ps[jg][:, :],
                            k.BIGB[:, half + 128 * li:half + 128 * (li + 1)],
                            k.BIGA[:, 2048 * mc + 512 * jg:
                                   2048 * mc + 512 * (jg + 1)],
                            start=(li == 0), stop=(li == 15),
                            skip_group_check=True)
                        if li == 15:
                            mm.then_inc(k.s_pe.h, 1)
                t.wait_ge(k.s_dve.h, xln3_done[ti])
                for q in range(4):
                    gi = ti * 4 + q
                    if gi >= 1:
                        pti, pq = divmod(gi - 1, 4)
                        t.wait_ge(k.s_act2.h, tpe_done[pti][pq])
                    for i in range(4):
                        dc = 4 * q + i
                        t.transpose(
                            k.psT[:, 128 * i:128 * (i + 1)],
                            k.STAGE[:, 2048 + 128 * dc:2048 + 128 * (dc + 1)],
                            k.ident[:, :]).then_inc(k.s_pe2.h, 1)
        self.on("tensor", pe_body)

        def ve_body(v):
            for ti in range(NT):
                v.wait_ge(k.s_lx.h, ld_x[ti])
                if ti >= 1:
                    v.wait_ge(k.s_act.h, act_last[ti - 1])
                v.tensor_add(k.sbf3[:, :], k.sbx[:, :],
                             k.bc[:, 2 * D:3 * D]).then_inc(k.s_dve.h, 1)
                v.wait_ge(k.s_dve.h, xbo_done[ti])
                for jg in range(4):
                    v.wait_ge(k.s_pe.h, o_done[ti][jg])
                    if ti >= 1 and jg == 0:
                        v.wait_ge(k.st_s[7].h, x1st_done[ti - 1])
                    v.scalar_tensor_tensor(
                        k.sbf1[:, 512 * jg:512 * (jg + 1)],
                        k.ps[jg][:, :], 1.0,
                        k.sbf3[:, 512 * jg:512 * (jg + 1)],
                        ALU.mult, ALU.add).then_inc(k.s_dve.h, 1)
                v.wait_ge(k.s_dve.h, x1p_done[ti][3])
                v.reduce_sum(k.c_tmp[:, :], k.sbf1[:, :],
                             axis=AX.X).then_inc(k.s_dve.h, 1)
                v.wait_ge(k.s_act.h, m[("ms2", ti)])
                v.tensor_scalar_mul(k.c_var[:, :], k.c_ssq[:, :],
                                    1.0 / D).then_inc(k.s_dve.h, 1)
                v.tensor_mul(k.c_tmp2[:, :], k.c_mean[:, :], k.c_mean[:, :]
                             ).then_inc(k.s_dve.h, 1)
                v.wait_ge(k.s_dve.h, m[("var2", ti)] - 1)
                v.tensor_sub(k.c_var[:, :], k.c_var[:, :], k.c_tmp2[:, :]
                             ).then_inc(k.s_dve.h, 1)
                v.wait_ge(k.s_act.h, m[("rstd2", ti)])
                v.tensor_scalar(k.sbf2[:, :], k.sbf1[:, :], k.c_mean[:, :],
                                k.c_rstd[:, :], ALU.subtract, ALU.mult
                                ).then_inc(k.s_dve.h, 1)
                v.wait_ge(k.s_dve.h, m[("xln2", ti)])
                v.reduce_sum(k.c_tmp[:, :], k.sbf2[:, :],
                             axis=AX.X).then_inc(k.s_dve.h, 1)
                v.wait_ge(k.s_act.h, m[("ms3", ti)])
                v.tensor_scalar_mul(k.c_var[:, :], k.c_ssq[:, :],
                                    1.0 / D).then_inc(k.s_dve.h, 1)
                v.tensor_mul(k.c_tmp2[:, :], k.c_mean[:, :], k.c_mean[:, :]
                             ).then_inc(k.s_dve.h, 1)
                v.wait_ge(k.s_dve.h, m[("var3", ti)] - 1)
                v.tensor_sub(k.c_var[:, :], k.c_var[:, :], k.c_tmp2[:, :]
                             ).then_inc(k.s_dve.h, 1)
                v.wait_ge(k.s_act.h, m[("rstd3", ti)])
                v.tensor_scalar(k.sbf3[:, :], k.sbf2[:, :], k.c_mean[:, :],
                                k.c_rstd[:, :], ALU.subtract, ALU.mult
                                ).then_inc(k.s_dve.h, 1)
                v.wait_ge(k.s_dve.h, m[("x3f", ti)] - 2)
                v.tensor_mul(k.sbf2[:, :], k.sbf3[:, :], k.bc[:, 0:D]
                             ).then_inc(k.s_dve.h, 1)
                v.wait_ge(k.s_dve.h, m[("x3f", ti)] - 1)
                v.tensor_add(k.sbf3[:, :], k.sbf2[:, :], k.bc[:, D:2 * D]
                             ).then_inc(k.s_dve.h, 1)
                v.wait_ge(k.s_dve.h, m[("x3f", ti)])
                if ti >= 1:
                    v.wait_ge(k.s_pe2.h, tp_done[ti - 1][3])
                    v.wait_ge(k.st_s[5].h, x3st_done[ti - 1])
                v.tensor_copy(k.STAGE[:, 2048:4096], k.sbf3[:, :]
                              ).then_inc(k.s_dve.h, 1)
                v.wait_ge(k.s_act.h, m[("rt_act", ti)])
                v.tensor_scalar_min(k.c_rstd[:, :], k.c_rstd[:, :], 1e12
                                    ).then_inc(k.s_dve.h, 1)
                v.wait_ge(k.s_dve.h, rt_done[ti] - 1)
                v.tensor_mul(k.r_all[:, ti:ti + 1], k.c_rstd[:, :],
                             k.scale_col[:, :]).then_inc(k.s_dve.h, 1)
        self.on("vector", ve_body)

        def sc_body(sc):
            for ti in range(NT):
                if ti >= 1:
                    sc.wait_ge(k.s_dve.h, rt_done[ti - 1])
                sc.wait_ge(k.s_dve.h, x1p_done[ti][3])
                sc.dma_start(k.d_x1[128 * ti:128 * (ti + 1), :], k.sbf1[:, :]
                             ).then_inc(k.st_s[7].h, 16)
                sc.activation(k.STAGE[:, 0:2048], k.sbf1[:, :], AF.Square,
                              accum_out=k.c_ssq[:, :]).then_inc(k.s_act.h, 1)
                sc.wait_ge(k.s_dve.h, m[("red2", ti)])
                sc.mul(k.c_mean[:, :], k.c_tmp[:, :], 1.0 / D
                       ).then_inc(k.s_act.h, 1)
                sc.wait_ge(k.s_dve.h, m[("var2", ti)])
                sc.activation(k.c_rstd[:, :], k.c_var[:, :], AF.Ln,
                              bias=k.eps_ln[:, :]).then_inc(k.s_act3.h, 1)
                sc.wait_ge(k.s_act3.h, e_ln[ti][0])
                sc.activation(k.c_rstd[:, :], k.c_rstd[:, :], AF.Exp,
                              scale=-0.5).then_inc(k.s_act.h, 1)
                sc.wait_ge(k.s_dve.h, m[("xln2", ti)])
                sc.activation(k.STAGE[:, 0:2048], k.sbf2[:, :], AF.Square,
                              accum_out=k.c_ssq[:, :]).then_inc(k.s_act.h, 1)
                sc.wait_ge(k.s_dve.h, m[("red3", ti)])
                sc.mul(k.c_mean[:, :], k.c_tmp[:, :], 1.0 / D
                       ).then_inc(k.s_act.h, 1)
                sc.wait_ge(k.s_dve.h, m[("var3", ti)])
                sc.activation(k.c_rstd[:, :], k.c_var[:, :], AF.Ln,
                              bias=k.eps_ln[:, :]).then_inc(k.s_act3.h, 1)
                sc.wait_ge(k.s_act3.h, e_ln[ti][1])
                sc.activation(k.c_rstd[:, :], k.c_rstd[:, :], AF.Exp,
                              scale=-0.5).then_inc(k.s_act.h, 1)
                sc.wait_ge(k.s_dve.h, xln3_done[ti])
                sc.dma_start(k.d_xln3[128 * ti:128 * (ti + 1), :],
                             k.STAGE[:, 2048:4096]).then_inc(k.st_s[5].h, 16)
                sc.activation(k.STAGE[:, 0:2048], k.sbf3[:, :], AF.Square,
                              accum_out=k.c_ssq[:, :]).then_inc(k.s_act.h, 1)
                sc.wait_ge(k.s_act.h, m[("rt_sq", ti)])
                sc.activation(k.c_rstd[:, :], k.c_ssq[:, :], AF.Ln
                              ).then_inc(k.s_act3.h, 1)
                sc.wait_ge(k.s_act3.h, e_ln[ti][2])
                sc.activation(k.c_rstd[:, :], k.c_rstd[:, :], AF.Exp,
                              scale=-0.5).then_inc(k.s_act.h, 1)
                for q in range(4):
                    sc.wait_ge(k.s_pe2.h, tp_done[ti][q])
                    sl = 4096 + 512 * q
                    if ti >= 1:
                        sc.wait_ge(k.st_s[q].h, st_sp[ti - 1][q])
                    sc.copy(k.STAGE[:, sl:sl + 512], k.psT[:, :]
                            ).then_inc(k.s_act2.h, 1)
                    sc.wait_ge(k.s_act2.h, tpe_done[ti][q])
                    dst = k.d_xln3T[512 * q:512 * (q + 1),
                                    128 * ti:128 * (ti + 1)]
                    dst = dst.rearrange("(i p) t -> p i t", p=128)
                    srcv = k.STAGE[:, sl:sl + 512].rearrange(
                        "p (i t) -> p i t", i=4)
                    sc.dma_start(dst, srcv).then_inc(k.st_s[q].h, 16)
        self.on("scalar", sc_body)
        self.a5x = {
            "o15": o_done[15][3], "xbo15": xbo_done[15],
            "x1p15": x1p_done[15][3], "red2_15": m[("red2", 15)],
            "x1st15": x1st_done[15], "tp15": tp_done[15][3],
            "tpe15": tpe_done[15][3], "x3st15": x3st_done[15],
            "rt_done": list(rt_done)}

    def phase_F0(self):
        k = self
        ld = [0] * 16
        sq_done = [0] * 16
        cast_done = [0] * 16
        mm_done = [0] * 16
        for dc in range(16):
            k.s_lx.cnt += 16
            ld[dc] = k.s_lx.cnt
            k.s_act.cnt += 1
            sq_done[dc] = k.s_act.cnt
            k.s_dve.cnt += 1
            cast_done[dc] = k.s_dve.cnt
            k.s_pe.cnt += 1
            mm_done[dc] = k.s_pe.cnt
        k.s_act.cnt += 4
        ln4 = k.s_act.cnt
        k.s_act.cnt += 1
        rn_act = k.s_act.cnt
        k.s_dve.cnt += 1
        rn_min = k.s_dve.cnt
        pe_b, ac_b = [], []
        for j in range(4):
            k.s_pe.cnt += 1
            pe_b.append(k.s_pe.cnt)
            k.s_act.cnt += 1
            ac_b.append(k.s_act.cnt)

        def sy_body(sy):
            sy.wait_ge(k.s_dve.h, self.a5x["xbo15"])
            for dc in range(16):
                if dc >= 1:
                    sy.wait_ge(k.s_act.h, sq_done[dc - 1])
                    sy.wait_ge(k.s_dve.h, cast_done[dc - 1])
                sy.dma_start(k.sbx[:, :], k.mu2d[128 * dc:128 * (dc + 1), :]
                             ).then_inc(k.s_lx.h, 16)
        self.on("sync", sy_body)

        def sc_body(sc):
            sc.wait_ge(k.s_dve.h, self.a5x["red2_15"])
            sc.wait_ge(k.st_s[7].h, self.a5x["x1st15"])
            for dc in range(16):
                sc.wait_ge(k.s_lx.h, ld[dc])
                if dc >= 1:
                    sc.wait_ge(k.s_pe.h, mm_done[dc - 1])
                sc.activation(k.sbf1[:, :], k.sbx[:, :], AF.Square
                              ).then_inc(k.s_act.h, 1)
            sc.wait_ge(k.s_pe.h, mm_done[15])
            for j in range(4):
                sc.activation(k.g_r[0:1, 512 * j:512 * (j + 1)],
                              k.ps[2 + j][0:1, :], AF.Ln
                              ).then_inc(k.s_act.h, 1)
            sc.wait_ge(k.s_act.h, ln4)
            sc.activation(k.g_r[0:1, 0:D], k.g_r[0:1, 0:D], AF.Exp,
                          scale=-0.5).then_inc(k.s_act.h, 1)
            for j in range(4):
                sc.wait_ge(k.s_pe.h, pe_b[j])
                sc.copy(k.bc[:, 2 * D + 512 * j:2 * D + 512 * (j + 1)],
                        k.psW[:, :]).then_inc(k.s_act.h, 1)
        self.on("scalar", sc_body)

        def ve_body(v):
            v.wait_ge(k.s_pe.h, self.a5x["o15"])
            for dc in range(16):
                v.wait_ge(k.s_lx.h, ld[dc])
                v.tensor_copy(k.BIGA[:, 2048 * dc:2048 * (dc + 1)], k.sbx[:, :]
                              ).then_inc(k.s_dve.h, 1)
            v.wait_ge(k.s_act.h, rn_act)
            v.tensor_scalar_min(k.g_r[0:1, 0:D], k.g_r[0:1, 0:D], 1e12
                                ).then_inc(k.s_dve.h, 1)
        self.on("vector", ve_body)

        def pe_body(t):
            t.wait_ge(k.s_dve.h, self.a5x["x1p15"])
            for dc in range(16):
                t.wait_ge(k.s_act.h, sq_done[dc])
                for j in range(4):
                    mm = t.matmul(k.ps[2 + j][0:1, :], k.ones_fc[:, :],
                                  k.sbf1[:, 512 * j:512 * (j + 1)],
                                  start=(dc == 0), stop=(dc == 15),
                                  skip_group_check=True)
                    if j == 3:
                        mm.then_inc(k.s_pe.h, 1)
            t.wait_ge(k.s_dve.h, rn_min)
            for j in range(4):
                if j > 0:
                    t.wait_ge(k.s_act.h, ac_b[j - 1])
                t.matmul(k.psW[:, :], k.ones_fr[:, :],
                         k.g_r[0:1, 512 * j:512 * (j + 1)],
                         start=True, stop=True, skip_group_check=True
                         ).then_inc(k.s_pe.h, 1)
        self.on("tensor", pe_body)
        self.f0x = {"mm15": mm_done[15], "rnact": rn_act, "acb3": ac_b[3],
                    "cast15": cast_done[15]}

    def phase_F2(self):
        k = self
        ld_sem = lambda ti: (k.s_la if ti % 2 == 0 else k.s_lb)
        ld = [0] * NT
        mm_done = [[0] * 4 for _ in range(NT)]
        stt_done = [[0] * 4 for _ in range(NT)]
        exp_done = [[0] * 4 for _ in range(NT)]
        tp_done = [[0] * 4 for _ in range(NT)]
        tpe_done = [[0] * 4 for _ in range(NT)]
        st_p2 = [[0] * 4 for _ in range(NT)]
        st_p2t = [[0] * 4 for _ in range(NT)]
        for ti in range(NT):
            ld_sem(ti).cnt += 16 * 16
            ld[ti] = ld_sem(ti).cnt
            for eg in range(4):
                k.s_pe.cnt += 1
                mm_done[ti][eg] = k.s_pe.cnt
                k.s_dve.cnt += 1
                stt_done[ti][eg] = k.s_dve.cnt
                k.s_act.cnt += 1
                exp_done[ti][eg] = k.s_act.cnt
                k.st_s[eg].cnt += 16
                st_p2[ti][eg] = k.st_s[eg].cnt
                k.s_pe2.cnt += 4
                tp_done[ti][eg] = k.s_pe2.cnt
                k.s_act2.cnt += 1
                tpe_done[ti][eg] = k.s_act2.cnt
                k.st_s[4 + eg % 2].cnt += 16
                st_p2t[ti][eg] = k.st_s[4 + eg % 2].cnt
        gate = self.a5_sp_tots
        self.f2_p2_tots = [(k.st_s[i].h, k.st_s[i].cnt) for i in range(4)]
        self.f2_p2t_tots = [(k.st_s[4 + i].h, k.st_s[4 + i].cnt) for i in range(2)]

        def sy_body(sy):
            for hh, cc in gate:
                sy.wait_ge(hh, cc)
            sy.wait_ge(k.s_pe.h, self.a5x["o15"])
            for ti in range(NT):
                half = (ti % 2) * 2048
                if ti >= 2:
                    sy.wait_ge(k.s_pe.h, mm_done[ti - 2][3])
                for dc in range(16):
                    srcv = k.d_xln3T[128 * dc:128 * (dc + 1),
                                     128 * ti:128 * (ti + 1)]
                    sy.dma_start(k.BIGB[:, half + 128 * dc:half + 128 * (dc + 1)],
                                 srcv).then_inc(ld_sem(ti).h, 16)
        self.on("sync", sy_body)

        def pe_body(t):
            t.wait_ge(k.s_act.h, self.f0x["rnact"])
            t.wait_ge(k.s_act2.h, self.a5x["tpe15"])
            t.wait_ge(k.s_dve.h, self.f0x["cast15"])
            for ti in range(NT):
                half = (ti % 2) * 2048
                t.wait_ge(ld_sem(ti).h, ld[ti])
                if ti >= 1:
                    t.wait_ge(k.s_dve.h, stt_done[ti - 1][3])
                for dc in range(16):
                    for eg in range(4):
                        mm = t.matmul(k.ps[eg][:, :],
                                      k.BIGB[:, half + 128 * dc:
                                             half + 128 * (dc + 1)],
                                      k.BIGA[:, 2048 * dc + 512 * eg:
                                             2048 * dc + 512 * (eg + 1)],
                                      start=(dc == 0), stop=(dc == 15),
                                      skip_group_check=True)
                        if dc == 15:
                            mm.then_inc(k.s_pe.h, 1)
                for eg in range(4):
                    t.wait_ge(k.s_act.h, exp_done[ti][eg])
                    gi = ti * 4 + eg
                    if gi >= 1:
                        pti, pq = divmod(gi - 1, 4)
                        t.wait_ge(k.s_act2.h, tpe_done[pti][pq])
                    sl = 512 * eg
                    for i in range(4):
                        t.transpose(k.psT[:, 128 * i:128 * (i + 1)],
                                    k.STAGE[:, sl + 128 * i:sl + 128 * (i + 1)],
                                    k.ident[:, :]).then_inc(k.s_pe2.h, 1)
        self.on("tensor", pe_body)

        def ve_body(v):
            v.wait_ge(k.s_pe.h, self.f0x["mm15"])
            v.wait_ge(k.s_act.h, self.f0x["acb3"])
            for ti in range(NT):
                v.wait_ge(k.s_dve.h, self.a5x["rt_done"][ti])
                for eg in range(4):
                    v.wait_ge(k.s_pe.h, mm_done[ti][eg])
                    if ti >= 1:
                        v.wait_ge(k.s_act.h, exp_done[ti - 1][eg])
                    v.scalar_tensor_tensor(
                        k.sbf1[:, 512 * eg:512 * (eg + 1)], k.ps[eg][:, :],
                        k.r_all[:, ti:ti + 1],
                        k.bc[:, 2 * D + 512 * eg:2 * D + 512 * (eg + 1)],
                        ALU.mult, ALU.mult).then_inc(k.s_dve.h, 1)
        self.on("vector", ve_body)

        def sc_body(sc):
            sc.wait_ge(k.s_pe2.h, self.a5x["tp15"])
            sc.wait_ge(k.st_s[5].h, self.a5x["x3st15"])
            for ti in range(NT):
                for eg in range(4):
                    sc.wait_ge(k.s_dve.h, stt_done[ti][eg])
                    sl = 512 * eg
                    if ti >= 1:
                        sc.wait_ge(k.s_pe2.h, tp_done[ti - 1][eg])
                        sc.wait_ge(k.st_s[eg].h, st_p2[ti - 1][eg])
                    sc.activation(k.STAGE[:, sl:sl + 512],
                                  k.sbf1[:, 512 * eg:512 * (eg + 1)],
                                  AF.Exp).then_inc(k.s_act.h, 1)
                    sc.wait_ge(k.s_act.h, exp_done[ti][eg])
                    sc.dma_start(
                        k.d_P2[128 * ti:128 * (ti + 1),
                               512 * eg:512 * (eg + 1)],
                        k.STAGE[:, sl:sl + 512]).then_inc(k.st_s[eg].h, 16)
                for eg in range(4):
                    sc.wait_ge(k.s_pe2.h, tp_done[ti][eg])
                    sl = 2048 + 512 * (eg % 2)
                    gi = ti * 4 + eg
                    if gi >= 2:
                        pti, pq = divmod(gi - 2, 4)
                        sc.wait_ge(k.st_s[4 + eg % 2].h, st_p2t[pti][pq])
                    sc.copy(k.STAGE[:, sl:sl + 512], k.psT[:, :]
                            ).then_inc(k.s_act2.h, 1)
                    sc.wait_ge(k.s_act2.h, tpe_done[ti][eg])
                    dst = k.d_P2T[512 * eg:512 * (eg + 1),
                                  128 * ti:128 * (ti + 1)]
                    dst = dst.rearrange("(i p) t -> p i t", p=128)
                    srcv = k.STAGE[:, sl:sl + 512].rearrange(
                        "p (i t) -> p i t", i=4)
                    sc.dma_start(dst, srcv).then_inc(k.st_s[4 + eg % 2].h, 16)
        self.on("scalar", sc_body)
        self.f2x = {"mm15": mm_done[15][3],
                    "stt15": stt_done[15][3]}

    def phase_F3(self):
        k = self
        ld_sem = lambda ec: (k.s_la if ec % 2 == 0 else k.s_lb)
        k.s_lc.cnt += 16 * NT
        lda_done = k.s_lc.cnt
        ORD = [4 * j + el for el in range(4) for j in range(4)]
        ld = [0] * 16
        mm_done = [0] * 16
        ev_done = [0] * 16
        st_done = [0] * 16
        for i in range(16):
            ld_sem(i).cnt += 16 * 16
            ld[i] = ld_sem(i).cnt
            k.s_pe.cnt += 1
            mm_done[i] = k.s_pe.cnt
            k.s_act.cnt += 5
            ev_done[i] = k.s_act.cnt
            k.st_s[6].cnt += 16
            st_done[i] = k.st_s[6].cnt
        gate = self.f2_p2_tots + [self.a5_x3_tot]
        rs_st = [st_done[4 * el + 3] for el in range(4)]

        def sy_body(sy):
            for hh, cc in gate:
                sy.wait_ge(hh, cc)
            sy.wait_ge(k.s_pe.h, self.f2x["mm15"])
            for tc in range(NT):
                sy.dma_start(k.BIGA[:, 2048 * tc:2048 * (tc + 1)],
                             k.d_xln3[128 * tc:128 * (tc + 1), :]
                             ).then_inc(k.s_lc.h, 16)
            for i in range(16):
                ec = ORD[i]
                half = (i % 2) * 2048
                if i >= 2:
                    sy.wait_ge(k.s_pe.h, mm_done[i - 2])
                for tc in range(16):
                    srcv = k.d_P2[128 * tc:128 * (tc + 1),
                                  128 * ec:128 * (ec + 1)]
                    sy.dma_start(k.BIGB[:, half + 128 * tc:half + 128 * (tc + 1)],
                                 srcv).then_inc(ld_sem(i).h, 16)
        self.on("sync", sy_body)

        def pe_body(t):
            t.wait_ge(k.s_dve.h, self.f2x["stt15"])
            t.wait_ge(k.s_act.h, self.f0x["acb3"])
            t.wait_ge(k.s_lc.h, lda_done)
            for i in range(16):
                ec = ORD[i]
                half = (i % 2) * 2048
                t.wait_ge(ld_sem(i).h, ld[i])
                if i >= 1:
                    t.wait_ge(k.s_act.h, ev_done[i - 1])
                for tc in range(16):
                    for dg in range(4):
                        t.matmul(k.ps[dg][:, :],
                                 k.BIGB[:, half + 128 * tc:half + 128 * (tc + 1)],
                                 k.BIGA[:, 2048 * tc + 512 * dg:
                                        2048 * tc + 512 * (dg + 1)],
                                 start=(tc == 0), stop=(tc == 15),
                                 skip_group_check=True)
                    mm = t.matmul(k.psW[:, 0:4],
                                  k.BIGB[:, half + 128 * tc:half + 128 * (tc + 1)],
                                  k.ones4[:, :],
                                  start=(tc == 0), stop=(tc == 15),
                                  skip_group_check=True)
                    if tc == 15:
                        mm.then_inc(k.s_pe.h, 1)
        self.on("tensor", pe_body)

        def sc_body(sc):
            for i in range(16):
                ec = ORD[i]
                el, j = ec % 4, ec // 4
                sc.wait_ge(k.s_pe.h, mm_done[i])
                if i >= 1:
                    sc.wait_ge(k.st_s[6].h, st_done[i - 1])
                for dg in range(4):
                    sc.copy(k.evf[:, 512 * dg:512 * (dg + 1)],
                            k.ps[dg][:, :]).then_inc(k.s_act.h, 1)
                sc.copy(k.evf[:, 2048:2052], k.psW[:, 0:4]
                        ).then_inc(k.s_act.h, 1)
                sc.wait_ge(k.s_act.h, ev_done[i])
                sc.dma_start(k.d_U_p[el][128 * j:128 * (j + 1), :],
                             k.evf[:, :]).then_inc(k.st_s[6].h, 16)
        self.on("scalar", sc_body)

        cc_rs = [self.new_cc() for _ in range(4)]
        self.cc_rs = cc_rs

        def g_body(g):
            g.wait_ge(k.s_gd.h, self.a5_gdw)
            for dc in range(16):
                g.dma_start(k.BIGB[:, 8192 + 512 * dc:8192 + 512 * (dc + 1)],
                            k.w1_my[0, 128 * dc:128 * (dc + 1), :]
                            ).then_inc(k.s_gd.h, 16)
            for hc in range(4):
                g.dma_start(
                    k.BIGB[:, 16384 + 2048 * hc:16384 + 2048 * (hc + 1)],
                    k.w2_my[0, 128 * hc:128 * (hc + 1), :]
                ).then_inc(k.s_wd.h, 16)
            for el in range(4):
                g.wait_ge(k.st_s[6].h, rs_st[el])
                if el >= 1:
                    g.wait_ge(cc_rs[el - 1].h, 1)
                g.collective_compute(
                    "ReduceScatter", ALU.add, replica_groups=GROUPS,
                    ins=[k.d_U_p[el].ap().opt()],
                    outs=[k.d_U_m[el].ap().opt()],
                ).then_inc(cc_rs[el].h)
        self.on("gpsimd", g_body)
        self.flush("A5F3")

    def phase_F4(self):
        k = self
        uld = [0] * 4
        b1ld = [0] * 4
        b2ld = [0] * 4
        w1ld = [0] * 4
        w2ld = [0] * 4
        recz = [0] * 4
        castu = [0] * 4
        tpu = [[0] * 4 for _ in range(4)]
        tpue = [[0] * 4 for _ in range(4)]
        b1b = [0] * 4
        b1bc = [0] * 4
        b2b = [[0] * 4 for _ in range(4)]
        b2bc = [[0] * 4 for _ in range(4)]
        h1m = [0] * 4
        hpre = [0] * 4
        hglu = [0] * 4
        tph = [0] * 4
        tphe = [0] * 4
        som = [[0] * 4 for _ in range(4)]
        sod = [[0] * 4 for _ in range(4)]
        stso = [0] * 4
        for el in range(4):
            k.s_lx.cnt += 16
            uld[el] = k.s_lx.cnt
            k.s_lr.cnt += 16
            b1ld[el] = k.s_lr.cnt
            k.s_lr.cnt += 16
            b2ld[el] = k.s_lr.cnt
            k.s_gd.cnt += 16 * 16
            w1ld[el] = k.s_gd.cnt
            k.s_wd.cnt += 16 * 4
            w2ld[el] = k.s_wd.cnt
            k.s_dve.cnt += 1
            recz[el] = k.s_dve.cnt
            k.s_dve.cnt += 1
            castu[el] = k.s_dve.cnt
            for q in range(4):
                k.s_pe.cnt += 4
                tpu[el][q] = k.s_pe.cnt
                k.s_act.cnt += 1
                tpue[el][q] = k.s_act.cnt
            k.s_pe.cnt += 1
            b1b[el] = k.s_pe.cnt
            k.s_act.cnt += 1
            b1bc[el] = k.s_act.cnt
            for j in range(4):
                k.s_pe.cnt += 1
                b2b[el][j] = k.s_pe.cnt
                k.s_act.cnt += 1
                b2bc[el][j] = k.s_act.cnt
            k.s_pe.cnt += 1
            h1m[el] = k.s_pe.cnt
            k.s_dve.cnt += 1
            hpre[el] = k.s_dve.cnt
            k.s_act.cnt += 1
            hglu[el] = k.s_act.cnt
            k.s_pe.cnt += 4
            tph[el] = k.s_pe.cnt
            k.s_act.cnt += 1
            tphe[el] = k.s_act.cnt
            for dg in range(4):
                k.s_pe.cnt += 1
                som[el][dg] = k.s_pe.cnt
                k.s_dve.cnt += 1
                sod[el][dg] = k.s_dve.cnt
            k.st_s[7].cnt += 16
            stso[el] = k.st_s[7].cnt
        slot_st_tot = k.st_s[7].cnt

        def sy_body(sy):
            for el in range(4):
                sy.wait_ge(self.cc_rs[el].h, 1)
                if el >= 1:
                    sy.wait_ge(k.s_dve.h, castu[el - 1])
                sy.dma_start(k.evf[:, :],
                             k.d_U_m[el][:, :]
                             ).then_inc(k.s_lx.h, 16)
                if el >= 1:
                    sy.wait_ge(k.s_pe.h, b2b[el - 1][3])
                sy.dma_start(k.g_r[0:1, 0:HFF],
                             k.b1_my[el:el + 1, :]).then_inc(k.s_lr.h, 16)
                sy.wait_ge(k.s_pe.h, b1b[el])
                sy.dma_start(k.g_r[0:1, 0:D],
                             k.b2_my[el:el + 1, :]).then_inc(k.s_lr.h, 16)
        self.on("sync", sy_body)

        def wload(g, el):
            if el >= 1:
                g.wait_ge(k.s_pe.h, h1m[el - 1])
            for dc in range(16):
                g.dma_start(k.BIGB[:, 8192 + 512 * dc:8192 + 512 * (dc + 1)],
                            k.w1_my[el, 128 * dc:128 * (dc + 1), :]
                            ).then_inc(k.s_gd.h, 16)
            if el >= 1:
                g.wait_ge(k.s_pe.h, som[el - 1][3])
            for hc in range(4):
                g.dma_start(
                    k.BIGB[:, 16384 + 2048 * hc:16384 + 2048 * (hc + 1)],
                    k.w2_my[el, 128 * hc:128 * (hc + 1), :]
                ).then_inc(k.s_wd.h, 16)
        self.f4_wload = wload

        def ve_body(v):
            for hh, cc in self.f2_p2_tots:
                v.wait_ge(hh, cc)
            for el in range(4):
                v.wait_ge(k.s_lx.h, uld[el])
                v.reciprocal(k.c_tmp[:, :], k.evf[:, 2048:2049]
                             ).then_inc(k.s_dve.h, 1)
                if el >= 1:
                    v.wait_ge(k.s_pe.h, tpu[el - 1][3])
                v.tensor_copy(k.STAGE[:, 0:2048], k.evf[:, 0:2048]
                              ).then_inc(k.s_dve.h, 1)
                v.wait_ge(k.s_pe.h, h1m[el])
                v.wait_ge(k.s_act.h, b1bc[el])
                v.wait_ge(k.s_dve.h, recz[el])
                if el >= 1:
                    v.wait_ge(k.s_act.h, hglu[el - 1])
                v.scalar_tensor_tensor(k.sbf1[:, 0:512], k.ps[0][:, :],
                                       k.c_tmp[:, :], k.bc[:, 0:512],
                                       ALU.mult, ALU.add).then_inc(k.s_dve.h, 1)
                for dg in range(4):
                    v.wait_ge(k.s_pe.h, som[el][dg])
                    v.wait_ge(k.s_act.h, b2bc[el][dg])
                    if el >= 1 and dg == 0:
                        v.wait_ge(k.st_s[7].h, stso[el - 1])
                    v.tensor_add(
                        k.STAGE[:, 5120 + 512 * dg:5120 + 512 * (dg + 1)],
                        k.ps[1 + dg][:, :],
                        k.bc[:, D + 512 * dg:D + 512 * (dg + 1)]
                    ).then_inc(k.s_dve.h, 1)
        self.on("vector", ve_body)

        def pe_body(t):
            for el in range(4):
                t.wait_ge(k.s_dve.h, castu[el])
                for q in range(4):
                    gi = el * 4 + q
                    if gi >= 1:
                        pel, pq = divmod(gi - 1, 4)
                        t.wait_ge(k.s_act.h, tpue[pel][pq])
                    for i in range(4):
                        dc = 4 * q + i
                        t.transpose(
                            k.psT[:, 128 * i:128 * (i + 1)],
                            k.STAGE[:, 128 * dc:128 * (dc + 1)],
                            k.ident[:, :]).then_inc(k.s_pe.h, 1)
                t.wait_ge(k.s_lr.h, b1ld[el])
                if el >= 1:
                    t.wait_ge(k.s_act.h, b1bc[el - 1])
                    t.wait_ge(k.s_dve.h, hpre[el - 1])
                t.matmul(k.psW[:, 0:HFF], k.ones_fr[:, :], k.g_r[0:1, 0:HFF],
                         start=True, stop=True, skip_group_check=True
                         ).then_inc(k.s_pe.h, 1)
                t.wait_ge(k.s_lr.h, b2ld[el])
                for j in range(4):
                    if el + j >= 1:
                        prev = b2bc[el][j - 1] if j >= 1 else b2bc[el - 1][3]
                        t.wait_ge(k.s_act.h, prev)
                        if j == 0 and el >= 1:
                            t.wait_ge(k.s_dve.h, sod[el - 1][3])
                    t.matmul(k.ps[5][:, :], k.ones_fr[:, :],
                             k.g_r[0:1, 512 * j:512 * (j + 1)],
                             start=True, stop=True, skip_group_check=True
                             ).then_inc(k.s_pe.h, 1)
                t.wait_ge(k.s_act.h, tpue[el][3])
                t.wait_ge(k.s_gd.h, w1ld[el])
                if el >= 1:
                    t.wait_ge(k.s_dve.h, hpre[el - 1])
                for dc in range(16):
                    mm = t.matmul(
                        k.ps[0][:, :],
                        k.STAGE[:, 2048 + 128 * dc:2048 + 128 * (dc + 1)],
                        k.BIGB[:, 8192 + 512 * dc:8192 + 512 * (dc + 1)],
                        start=(dc == 0), stop=(dc == 15),
                        skip_group_check=True)
                    if dc == 15:
                        mm.then_inc(k.s_pe.h, 1)
                t.wait_ge(k.s_act.h, hglu[el])
                if el >= 1:
                    t.wait_ge(k.s_act.h, tphe[el - 1])
                for i in range(4):
                    t.transpose(k.psT[:, 128 * i:128 * (i + 1)],
                                k.STAGE[:, 4096 + 128 * i:4096 + 128 * (i + 1)],
                                k.ident[:, :]).then_inc(k.s_pe.h, 1)
                t.wait_ge(k.s_act.h, tphe[el])
                t.wait_ge(k.s_wd.h, w2ld[el])
                for dg in range(4):
                    if el >= 1:
                        t.wait_ge(k.s_dve.h, sod[el - 1][dg])
                    for hc in range(4):
                        mm = t.matmul(
                            k.ps[1 + dg][:, :],
                            k.STAGE[:, 4608 + 128 * hc:4608 + 128 * (hc + 1)],
                            k.BIGB[:, 16384 + 2048 * hc + 512 * dg:
                                   16384 + 2048 * hc + 512 * (dg + 1)],
                            start=(hc == 0), stop=(hc == 3),
                            skip_group_check=True)
                        if hc == 3:
                            mm.then_inc(k.s_pe.h, 1)
        self.on("tensor", pe_body)

        def sc_body(sc):
            for hh, cc in self.f2_p2t_tots:
                sc.wait_ge(hh, cc)
            for el in range(4):
                for q in range(4):
                    sc.wait_ge(k.s_pe.h, tpu[el][q])
                    sl = 2048 + 512 * q
                    sc.copy(k.STAGE[:, sl:sl + 512], k.psT[:, :]
                            ).then_inc(k.s_act.h, 1)
                sc.wait_ge(k.s_pe.h, b1b[el])
                if el >= 1:
                    sc.wait_ge(k.s_dve.h, hpre[el - 1])
                sc.copy(k.bc[:, 0:512], k.psW[:, 0:HFF]).then_inc(k.s_act.h, 1)
                for j in range(4):
                    sc.wait_ge(k.s_pe.h, b2b[el][j])
                    if el >= 1:
                        sc.wait_ge(k.s_dve.h, sod[el - 1][j])
                    sc.copy(k.bc[:, D + 512 * j:D + 512 * (j + 1)],
                            k.ps[5][:, :]).then_inc(k.s_act.h, 1)
                sc.wait_ge(k.s_dve.h, hpre[el])
                sc.activation(k.STAGE[:, 4096:4608], k.sbf1[:, 0:512],
                              AF.Gelu).then_inc(k.s_act.h, 1)
                sc.wait_ge(k.s_pe.h, tph[el])
                sc.copy(k.STAGE[:, 4608:5120], k.psT[:, :]
                        ).then_inc(k.s_act.h, 1)
                sc.wait_ge(k.s_dve.h, sod[el][3])
                sc.dma_start(k.d_sp[el][:, :],
                             k.STAGE[:, 5120:7168]).then_inc(k.st_s[7].h, 16)
        self.on("scalar", sc_body)

        cc_sg = [self.new_cc() for _ in range(4)]
        self.cc_sg = cc_sg

        def g_body2(g):
            for el in range(4):
                if el + 1 <= 3:
                    self.f4_wload(g, el + 1)
                if el == 0:
                    g.wait_ge(self.cc_rs[3].h, 1)
                g.wait_ge(k.st_s[7].h, stso[el])
                if el >= 1:
                    g.wait_ge(cc_sg[el - 1].h, 1)
                g.collective_compute(
                    "AllGather", ALU.bypass, replica_groups=GROUPS,
                    ins=[k.d_sp[el].ap().opt()],
                    outs=[k.d_sg[el].ap().opt()],
                ).then_inc(cc_sg[el].h)
        self.on("gpsimd", g_body2)
        self.flush("F4")

    def phase_F5(self):
        k = self
        ld_sem = lambda ti: (k.s_la if ti % 2 == 0 else k.s_lb)
        k.s_lc.cnt += 16 * 16
        lda = k.s_lc.cnt
        ld = [0] * NT
        ldx = [0] * NT
        mm_done = [0] * NT
        rec_done = [0] * NT
        out_done = [[0] * 4 for _ in range(NT)]
        st_y = [0] * NT
        for ti in range(NT):
            ld_sem(ti).cnt += 16 * 16
            ld[ti] = ld_sem(ti).cnt
            k.s_lx.cnt += 16
            ldx[ti] = k.s_lx.cnt
            k.s_pe.cnt += 1
            mm_done[ti] = k.s_pe.cnt
            k.s_dve.cnt += 1
            rec_done[ti] = k.s_dve.cnt
            for dg in range(4):
                k.s_dve.cnt += 1
                out_done[ti][dg] = k.s_dve.cnt
            k.st_s[6].cnt += 16
            st_y[ti] = k.st_s[6].cnt
        gate = self.f2_p2t_tots + [self.a5_x1_tot]

        def sy_body(sy):
            for hh, cc in gate:
                sy.wait_ge(hh, cc)
            for el in range(4):
                sy.wait_ge(self.cc_sg[el].h, 1)
                for j in range(4):
                    ec = 4 * j + el
                    sy.dma_start(k.BIGA[:, 2048 * ec:2048 * (ec + 1)],
                                 k.d_sg[el][128 * j:128 * (j + 1), :]
                                 ).then_inc(k.s_lc.h, 16)
            for ti in range(NT):
                half = (ti % 2) * 2048
                if ti >= 2:
                    sy.wait_ge(k.s_pe.h, mm_done[ti - 2])
                for ec in range(16):
                    srcv = k.d_P2T[128 * ec:128 * (ec + 1),
                                   128 * ti:128 * (ti + 1)]
                    sy.dma_start(k.BIGB[:, half + 128 * ec:half + 128 * (ec + 1)],
                                 srcv).then_inc(ld_sem(ti).h, 16)
                if ti >= 1:
                    sy.wait_ge(k.s_dve.h, out_done[ti - 1][3])
                sy.dma_start(k.sbx[:, :], k.d_x1[128 * ti:128 * (ti + 1), :]
                             ).then_inc(k.s_lx.h, 16)
        self.on("sync", sy_body)

        def pe_body(t):
            t.wait_ge(k.s_lc.h, lda)
            for ti in range(NT):
                half = (ti % 2) * 2048
                t.wait_ge(ld_sem(ti).h, ld[ti])
                if ti >= 1:
                    t.wait_ge(k.s_dve.h, out_done[ti - 1][3])
                    t.wait_ge(k.s_dve.h, rec_done[ti - 1])
                for ec in range(16):
                    for dg in range(4):
                        t.matmul(k.ps[dg][:, :],
                                 k.BIGB[:, half + 128 * ec:half + 128 * (ec + 1)],
                                 k.BIGA[:, 2048 * ec + 512 * dg:
                                        2048 * ec + 512 * (dg + 1)],
                                 start=(ec == 0), stop=(ec == 15),
                                 skip_group_check=True)
                    mm = t.matmul(k.psW[:, 0:1],
                                  k.BIGB[:, half + 128 * ec:half + 128 * (ec + 1)],
                                  k.ones_bc[:, :],
                                  start=(ec == 0), stop=(ec == 15),
                                  skip_group_check=True)
                    if ec == 15:
                        mm.then_inc(k.s_pe.h, 1)
        self.on("tensor", pe_body)

        def ve_body(v):
            for ti in range(NT):
                v.wait_ge(k.s_pe.h, mm_done[ti])
                v.wait_ge(k.s_lx.h, ldx[ti])
                v.reciprocal(k.c_tmp[:, :], k.psW[:, 0:1]
                             ).then_inc(k.s_dve.h, 1)
                if ti >= 1:
                    v.wait_ge(k.st_s[6].h, st_y[ti - 1])
                v.wait_ge(k.s_dve.h, rec_done[ti])
                for dg in range(4):
                    v.scalar_tensor_tensor(
                        k.sbf1[:, 512 * dg:512 * (dg + 1)], k.ps[dg][:, :],
                        k.c_tmp[:, :], k.sbx[:, 512 * dg:512 * (dg + 1)],
                        ALU.mult, ALU.add).then_inc(k.s_dve.h, 1)
        self.on("vector", ve_body)

        def sc_body(sc):
            for ti in range(NT):
                sc.wait_ge(k.s_dve.h, out_done[ti][3])
                sc.dma_start(k.y[128 * ti:128 * (ti + 1), :], k.sbf1[:, :]
                             ).then_inc(k.st_s[6].h, 16)
        self.on("scalar", sc_body)
        self.flush("F5")


def build_nc(stage=99, debug=False):
    return KB(stage=stage, debug=debug).build()


_NC_CACHE = {}


def _get_nc():
    if "nc" not in _NC_CACHE:
        _NC_CACHE["nc"] = build_nc()
    return _NC_CACHE["nc"]


def make_in_maps(inputs):
    ins = {kk: np.asarray(v) for kk, v in inputs.items()}
    x = ins["L0_attn_x"].astype(np.float32, copy=False)
    wq, wk, wv = ins["L0_attn_wq"], ins["L0_attn_wk"], ins["L0_attn_wv"]
    wo, bo = ins["L0_attn_wo"], ins["L0_attn_bo"]
    gamma, beta = ins["L0_ffn_gamma"], ins["L0_ffn_beta"]
    mu, scale = ins["L0_ffn_mu"], ins["L0_ffn_scale"]
    W1, b1 = ins["L0_ffn_W1"], ins["L0_ffn_b1"]
    W2, b2 = ins["L0_ffn_W2"], ins["L0_ffn_b2"]
    wqT = np.ascontiguousarray(wq.T)
    wkT = np.ascontiguousarray(wk.T)
    wvT = np.ascontiguousarray(wv.T)
    woT = np.ascontiguousarray(wo.T)
    mu2d = np.ascontiguousarray(mu.reshape(D, ES))
    in_maps = []
    for c in range(8):
        bg, j = divmod(c, 4)
        in_maps.append({
            "x_home": np.ascontiguousarray(x[bg, TH * j:TH * (j + 1), :]),
            "wq_my": np.ascontiguousarray(wqT[:, 512 * j:512 * (j + 1)]),
            "wk_my": np.ascontiguousarray(wkT[:, 512 * j:512 * (j + 1)]),
            "wv_my": np.ascontiguousarray(wvT[:, 512 * j:512 * (j + 1)]),
            "woT": woT,
            "bo": np.ascontiguousarray(bo.reshape(1, D)).astype(np.float32),
            "gamma": np.ascontiguousarray(gamma.reshape(1, D)).astype(np.float32),
            "beta": np.ascontiguousarray(beta.reshape(1, D)).astype(np.float32),
            "scale_in": np.ascontiguousarray(scale.reshape(1, 1)).astype(np.float32),
            "mu2d": mu2d,
            "w1_my": np.ascontiguousarray(W1[4 * j:4 * (j + 1)]),
            "b1_my": np.ascontiguousarray(b1[4 * j:4 * (j + 1)]),
            "w2_my": np.ascontiguousarray(W2[4 * j:4 * (j + 1)]),
            "b2_my": np.ascontiguousarray(b2[4 * j:4 * (j + 1)]),
        })
    return in_maps


def kernel(**inputs):
    nc = _get_nc()
    in_maps = make_in_maps(inputs)
    res = run_bass_kernel_spmd(nc, in_maps, core_ids=list(range(8)))
    out = np.empty((B, S, D), np.float32)
    for c in range(8):
        bg, j = divmod(c, 4)
        out[bg, TH * j:TH * (j + 1), :] = res.results[c]["y"]
    return out


# version 10
# speedup vs baseline: 1.2344x; 1.2344x over previous
import math
import os as _os
from contextlib import ExitStack

import numpy as np

import concourse.bass as bass
import concourse.mybir as mybir
from concourse.bass_utils import run_bass_kernel_spmd

F32 = mybir.dt.float32
BF16 = mybir.dt.bfloat16
AF = mybir.ActivationFunctionType
ALU = mybir.AluOpType
AX = mybir.AxisListType

B, S, D, H, HD = 2, 8192, 2048, 16, 128
GROUPS = [[0, 1, 2, 3], [4, 5, 6, 7]]
TH = 2048
NT = TH // 128
E, SLOT, HFF = 16, 128, 512
ES = E * SLOT
MYH, MYE = 4, 4
ISQ = 1.0 / math.sqrt(HD)
LN_EPS = 1e-5
UW = D + 4

ENGS = ("sync", "tensor", "scalar", "vector", "gpsimd")


class Sem:
    def __init__(self, h):
        self.h = h
        self.cnt = 0


class KB:
    def __init__(self, stage=99, debug=False):
        self.stage = stage
        self.debug = debug
        self.ops = {e: [] for e in ENGS}

    def on(self, eng_name, fn):
        self.ops[eng_name].append(fn)

    def flush(self, name=None):
        ops, self.ops = self.ops, {e: [] for e in ENGS}
        with self.nc.Block() as blk:
            for e in ENGS:
                if ops[e]:
                    def body(eng, _ops=tuple(ops[e])):
                        for f in _ops:
                            f(eng)
                    getattr(blk, e)(body)

    def new_cc(self):
        s = self.cc_sems[self.cc_used]
        self.cc_used += 1
        return s

    def build(self):
        nc = bass.Bass("TRN2", target_bir_lowering=False)
        self.nc = nc

        self.x_home = nc.dram_tensor("x_home", [TH, D], F32, kind="ExternalInput")
        self.wq_my = nc.dram_tensor("wq_my", [D, MYH * HD], F32, kind="ExternalInput")
        self.wk_my = nc.dram_tensor("wk_my", [D, MYH * HD], F32, kind="ExternalInput")
        self.wv_my = nc.dram_tensor("wv_my", [D, MYH * HD], F32, kind="ExternalInput")
        self.woT = nc.dram_tensor("woT", [D, D], F32, kind="ExternalInput")
        self.bo = nc.dram_tensor("bo", [1, D], F32, kind="ExternalInput")
        self.gamma = nc.dram_tensor("gamma", [1, D], F32, kind="ExternalInput")
        self.beta = nc.dram_tensor("beta", [1, D], F32, kind="ExternalInput")
        self.scale_in = nc.dram_tensor("scale_in", [1, 1], F32, kind="ExternalInput")
        self.mu2d = nc.dram_tensor("mu2d", [D, ES], F32, kind="ExternalInput")
        self.w1_my = nc.dram_tensor("w1_my", [MYE, D, HFF], F32, kind="ExternalInput")
        self.b1_my = nc.dram_tensor("b1_my", [MYE, HFF], F32, kind="ExternalInput")
        self.w2_my = nc.dram_tensor("w2_my", [MYE, HFF, D], F32, kind="ExternalInput")
        self.b2_my = nc.dram_tensor("b2_my", [MYE, D], F32, kind="ExternalInput")
        self.y = nc.dram_tensor("y", [TH, D], F32, kind="ExternalOutput")

        self.d_xlnT_own = [nc.dram_tensor(f"d_xlnT_own{k}", [D, 512], BF16)
                           for k in range(4)]
        self.d_xlnT_g = [nc.dram_tensor(f"d_xlnT_g{k}", [4 * D, 512], BF16)
                         for k in range(4)]
        self.d_q = [nc.dram_tensor(f"d_q{h}", [HD, S], BF16) for h in range(MYH)]
        self.d_k = [nc.dram_tensor(f"d_k{h}", [HD, S], BF16) for h in range(MYH)]
        self.d_v = [nc.dram_tensor(f"d_v{h}", [S, HD], BF16) for h in range(MYH)]
        self.d_ctx_own = [nc.dram_tensor(f"d_ctx_own{h}", [HD, S], BF16)
                          for h in range(MYH)]
        self.d_ctx_g = [nc.dram_tensor(f"d_ctx_g{h}", [4 * HD, S], BF16)
                        for h in range(MYH)]
        self.d_x1 = nc.dram_tensor("d_x1", [TH, D], F32)
        self.d_xln3 = nc.dram_tensor("d_xln3", [TH, D], BF16)
        self.d_xln3T = nc.dram_tensor("d_xln3T", [D, TH], BF16)
        self.d_P2 = nc.dram_tensor("d_P2", [TH, ES], BF16)
        self.d_P2T = nc.dram_tensor("d_P2T", [ES, TH], BF16)
        self.d_U_p = [nc.dram_tensor(f"d_U_p{e}", [4 * SLOT, UW], F32)
                      for e in range(MYE)]
        self.d_U_m = [nc.dram_tensor(f"d_U_m{e}", [SLOT, UW], F32)
                      for e in range(MYE)]
        self.d_sp = [nc.dram_tensor(f"d_sp{e}", [SLOT, D], BF16)
                     for e in range(MYE)]
        self.d_sg = [nc.dram_tensor(f"d_sg{e}", [4 * SLOT, D], BF16)
                     for e in range(MYE)]

        self.dbg = {}
        if self.debug:
            def mkdbg(name, src, shape, dt):
                t = nc.dram_tensor("dbg_" + name, shape, dt, kind="ExternalOutput")
                self.dbg[name] = (t, src)
            for kk in range(4):
                mkdbg(f"xlnT_g{kk}", self.d_xlnT_g[kk], [4 * D, 512], BF16)
                mkdbg(f"xlnT_o{kk}", self.d_xlnT_own[kk], [D, 512], BF16)
            for h in range(MYH):
                mkdbg(f"q{h}", self.d_q[h], [HD, S], BF16)
                mkdbg(f"k{h}", self.d_k[h], [HD, S], BF16)
                mkdbg(f"v{h}", self.d_v[h], [S, HD], BF16)
            mkdbg("x1", self.d_x1, [TH, D], F32)
            mkdbg("xln3", self.d_xln3, [TH, D], BF16)
            mkdbg("xln3T", self.d_xln3T, [D, TH], BF16)
            mkdbg("P2", self.d_P2, [TH, ES], BF16)
            mkdbg("P2T", self.d_P2T, [ES, TH], BF16)

        with ExitStack() as ctx:
            en = ctx.enter_context
            self.BIGA = en(nc.sbuf_tensor("BIGA", [128, 32768], BF16))
            self.BIGB = en(nc.sbuf_tensor("BIGB", [128, 24576], BF16))
            self.STAGE = en(nc.sbuf_tensor("STAGE", [128, 7168], BF16))
            self.sbx = en(nc.sbuf_tensor("sbx", [128, D], F32))
            self.sbf1 = en(nc.sbuf_tensor("sbf1", [128, D], F32))
            self.sbf2 = en(nc.sbuf_tensor("sbf2", [128, D], F32))
            self.sbf3 = en(nc.sbuf_tensor("sbf3", [128, D], F32))
            self.evf = en(nc.sbuf_tensor("evf", [128, UW], F32))
            self.bc = en(nc.sbuf_tensor("bc_pool", [128, 3 * D], F32))
            self.Lt = en(nc.sbuf_tensor("Lt", [128, 512], F32))
            self.masks = en(nc.sbuf_tensor("masks", [128, 2048], BF16))
            self.ident = en(nc.sbuf_tensor("ident", [128, 128], BF16))
            self.ones_fc = en(nc.sbuf_tensor("ones_fc", [128, 1], F32))
            self.ones_fr = en(nc.sbuf_tensor("ones_fr", [1, 128], F32))
            self.ones_bc = en(nc.sbuf_tensor("ones_bc", [128, 1], BF16))
            self.ones4 = en(nc.sbuf_tensor("ones4", [128, 4], BF16))
            self.eps_ln = en(nc.sbuf_tensor("eps_ln", [128, 1], F32))
            self.scale_col = en(nc.sbuf_tensor("scale_col", [128, 1], F32))
            self.r_all = en(nc.sbuf_tensor("r_all", [128, NT], F32))
            self.c_mean = en(nc.sbuf_tensor("c_mean", [128, 1], F32))
            self.c_ssq = en(nc.sbuf_tensor("c_ssq", [128, 1], F32))
            self.c_var = en(nc.sbuf_tensor("c_var", [128, 1], F32))
            self.c_rstd = en(nc.sbuf_tensor("c_rstd", [128, 1], F32))
            self.c_tmp = en(nc.sbuf_tensor("c_tmp", [128, 1], F32))
            self.c_tmp2 = en(nc.sbuf_tensor("c_tmp2", [128, 1], F32))
            self.recip_row = en(nc.sbuf_tensor("recip_row", [1, 512], F32))
            self.g_r = en(nc.sbuf_tensor("g_row", [1, D], F32))
            self.ps = [en(nc.psum_tensor(f"ps{i}", [128, 512], F32))
                       for i in range(6)]
            self.psT = en(nc.psum_tensor("psT", [128, 512], BF16))
            self.psW = en(nc.psum_tensor("psW", [128, 512], F32))
            for nm in ("gd", "gd2", "wd", "pe", "act", "dve", "pe2", "act2", "act3",
                       "la", "lb", "lc", "lx", "lr"):
                setattr(self, "s_" + nm, Sem(en(nc.semaphore("s_" + nm))))
            self.st_s = [Sem(en(nc.semaphore(f"s_sts{i}"))) for i in range(8)]
            self.cc_sems = [Sem(en(nc.semaphore(f"s_cc{i}"))) for i in range(16)]
            self.cc_used = 0

            self.phase0()
            if self.stage >= 1:
                self.phase_A0()
            if self.stage >= 2:
                self.phase_A2()
            if self.stage == 2:
                self.flush("A2only")
            if self.stage >= 3:
                self.phase_A3()
            if self.stage >= 4:
                self.phase_A5()
            if self.stage >= 5:
                self.phase_F0()
                self.phase_F2()
            if self.stage >= 6:
                self.phase_F3()
            if 4 <= self.stage < 6:
                self.flush("A5F3partial")
            if self.stage >= 7:
                self.phase_F4()
            if self.stage == 7:
                self.flush("F4only")
            if self.stage >= 8:
                self.phase_F5()

            if self.debug:
                k = self
                gd_now = k.s_gd.cnt + 16 * len(self.dbg)
                k.s_gd.cnt = gd_now
                st_tots_d = [(s.h, s.cnt) for s in self.st_s]
                cc_n_d = self.cc_used

                def g_dbg(g):
                    for i in range(cc_n_d):
                        g.wait_ge(self.cc_sems[i].h, 1)
                    for hh, cc in st_tots_d:
                        g.wait_ge(hh, cc)
                    for name, (t, src) in k.dbg.items():
                        g.dma_start(t.ap(), src.ap()).then_inc(k.s_gd.h, 16)
                    g.wait_ge(k.s_gd.h, gd_now)
                self.on("gpsimd", g_dbg)
                self.flush("dbg")

            st_tots = [(s.h, s.cnt) for s in self.st_s]
            gd_f, gd2_f, wd_f = self.s_gd.cnt, self.s_gd2.cnt, self.s_wd.cnt
            cc_n = self.cc_used

            def fin(sy):
                for hh, cc in st_tots:
                    sy.wait_ge(hh, cc)
                sy.wait_ge(self.s_gd.h, gd_f)
                sy.wait_ge(self.s_gd2.h, gd2_f)
                sy.wait_ge(self.s_wd.h, wd_f)
                for i in range(cc_n):
                    sy.wait_ge(self.cc_sems[i].h, 1)
            self.on("sync", fin)
            self.flush("fin")
        return nc

    def phase0(self):
        k = self
        k.s_gd2.cnt += 7
        gd_target = k.s_gd.cnt + 16 * 48

        def g_body(g):
            g.memset(k.masks[:, :], 1.0).then_inc(k.s_gd2.h, 1)
            g.wait_ge(k.s_gd2.h, 1)
            for r in range(4):
                g.affine_select(out=k.masks[:, 512 * r:512 * (r + 1)],
                                in_=k.masks[:, 512 * r:512 * (r + 1)],
                                compare_op=ALU.is_ge, fill=0.0, base=-128 * r,
                                pattern=[[1, 512]], channel_multiplier=-1
                                ).then_inc(k.s_gd2.h, 1)
            g.memset(k.ident[:, :], 0.0).then_inc(k.s_gd2.h, 1)
            g.wait_ge(k.s_gd2.h, 6)
            g.affine_select(out=k.ident[:, :], in_=k.ident[:, :],
                            compare_op=ALU.not_equal, fill=1.0, base=0,
                            pattern=[[-1, 128]], channel_multiplier=1)
            g.memset(k.ones_fc[:, :], 1.0)
            g.memset(k.ones_fr[:, :], 1.0)
            g.memset(k.ones_bc[:, :], 1.0)
            g.memset(k.ones4[:, :], 0.0).then_inc(k.s_gd2.h, 1)
            g.wait_ge(k.s_gd2.h, 7)
            g.memset(k.ones4[:, 0:1], 1.0)
            g.memset(k.eps_ln[:, :], LN_EPS)
            if _os.environ.get("SIMINIT", "0") == "1":
                g.memset(k.BIGA[:, :], 0.0)
                g.memset(k.STAGE[:, :], 0.0)
            for wi, w in enumerate((k.wq_my, k.wk_my, k.wv_my)):
                for dc in range(16):
                    g.dma_start(
                        k.BIGB[:, 8192 * wi + 512 * dc:8192 * wi + 512 * (dc + 1)],
                        w[128 * dc:128 * (dc + 1), :],
                    ).then_inc(k.s_gd.h, 16)
            g.wait_ge(k.s_gd.h, gd_target)
        k.s_gd.cnt += 16 * 48
        self.on("gpsimd", g_body)
        self.flush("p0")

        for src_ap, width, dst_fn in (
            (self.gamma[:, :], D, lambda j: self.bc[:, 512 * j:512 * (j + 1)]),
            (self.beta[:, :], D, lambda j: self.bc[:, D + 512 * j:D + 512 * (j + 1)]),
            (self.bo[:, :], D, lambda j: self.bc[:, 2 * D + 512 * j:2 * D + 512 * (j + 1)]),
            (self.scale_in[:, :], 1, lambda j: self.scale_col[:, :]),
        ):
            nsl = (width + 511) // 512
            k.s_lr.cnt += 16
            ld_at = k.s_lr.cnt
            pe_at, ac_at = [], []
            for j in range(nsl):
                k.s_pe.cnt += 1
                pe_at.append(k.s_pe.cnt)
                k.s_act.cnt += 1
                ac_at.append(k.s_act.cnt)

            def sy_body(sy, src_ap=src_ap, width=width):
                sy.dma_start(k.g_r[:, 0:width], src_ap).then_inc(k.s_lr.h, 16)
            self.on("sync", sy_body)

            def pe_body(t, nsl=nsl, width=width, ld_at=ld_at, ac_at=tuple(ac_at)):
                t.wait_ge(k.s_lr.h, ld_at)
                for j in range(nsl):
                    w = min(512, width - 512 * j)
                    if j > 0:
                        t.wait_ge(k.s_act.h, ac_at[j - 1])
                    t.matmul(k.psW[:, 0:w], k.ones_fr[:, :],
                             k.g_r[:, 512 * j:512 * j + w],
                             start=True, stop=True).then_inc(k.s_pe.h, 1)
            self.on("tensor", pe_body)

            def sc_body(sc, nsl=nsl, width=width, dst_fn=dst_fn,
                        pe_at=tuple(pe_at)):
                for j in range(nsl):
                    w = min(512, width - 512 * j)
                    sc.wait_ge(k.s_pe.h, pe_at[j])
                    sc.copy(dst_fn(j), k.psW[:, 0:w]).then_inc(k.s_act.h, 1)
            self.on("scalar", sc_body)
            self.flush("bcast")

    def phase_A0(self):
        k = self
        k.e_a0_ln = [0] * NT
        ld_x = [0] * NT
        red_done = [0] * NT
        meansq_done = [0] * NT
        var_done = [0] * NT
        xln_done = [0] * NT
        ssq_done = [0] * NT
        mean_done = [0] * NT
        rstd_done = [0] * NT
        tp_done = [[0] * 4 for _ in range(NT)]
        evac_done = [[0] * 4 for _ in range(NT)]
        st_done = [[0] * 4 for _ in range(NT)]

        for ti in range(NT):
            k.s_lx.cnt += 16
            ld_x[ti] = k.s_lx.cnt
            k.s_dve.cnt += 1
            red_done[ti] = k.s_dve.cnt
            k.s_act.cnt += 1
            ssq_done[ti] = k.s_act.cnt
            k.s_act.cnt += 1
            mean_done[ti] = k.s_act.cnt
            k.s_dve.cnt += 2
            meansq_done[ti] = k.s_dve.cnt
            k.s_dve.cnt += 1
            var_done[ti] = k.s_dve.cnt
            k.s_act3.cnt += 1
            k.e_a0_ln[ti] = k.s_act3.cnt
            k.s_act.cnt += 1
            rstd_done[ti] = k.s_act.cnt
            k.s_dve.cnt += 1
            xln_done[ti] = k.s_dve.cnt
            for q in range(4):
                k.s_pe.cnt += 4
                tp_done[ti][q] = k.s_pe.cnt
                k.s_act2.cnt += 1
                evac_done[ti][q] = k.s_act2.cnt
                k.st_s[q].cnt += 16
                st_done[ti][q] = k.st_s[q].cnt
        cc_gather = [self.new_cc() for _ in range(4)]
        self.cc_gather = cc_gather
        st_tot = [(k.st_s[q].h, k.st_s[q].cnt) for q in range(4)]

        def sy_body(sy):
            for ti in range(NT):
                if ti >= 1:
                    sy.wait_ge(k.s_dve.h, xln_done[ti - 1])
                    sy.wait_ge(k.s_act.h, ssq_done[ti - 1])
                sy.dma_start(k.sbx[:, :],
                             k.x_home[128 * ti:128 * (ti + 1), :]
                             ).then_inc(k.s_lx.h, 16)
        self.on("sync", sy_body)

        def ve_body(v):
            for ti in range(NT):
                v.wait_ge(k.s_lx.h, ld_x[ti])
                if ti >= 1:
                    v.wait_ge(k.s_act.h, rstd_done[ti - 1])
                    v.wait_ge(k.s_pe.h, tp_done[ti - 1][3])
                v.reduce_sum(k.c_tmp[:, :], k.sbx[:, :],
                             axis=AX.X).then_inc(k.s_dve.h, 1)
                v.wait_ge(k.s_act.h, mean_done[ti])
                v.tensor_scalar_mul(k.c_var[:, :], k.c_ssq[:, :],
                                    1.0 / D).then_inc(k.s_dve.h, 1)
                v.tensor_mul(k.c_tmp2[:, :], k.c_mean[:, :],
                             k.c_mean[:, :]).then_inc(k.s_dve.h, 1)
                v.wait_ge(k.s_dve.h, meansq_done[ti])
                v.tensor_sub(k.c_var[:, :], k.c_var[:, :],
                             k.c_tmp2[:, :]).then_inc(k.s_dve.h, 1)
                v.wait_ge(k.s_act.h, rstd_done[ti])
                v.tensor_scalar(k.STAGE[:, 0:2048], k.sbx[:, :],
                                k.c_mean[:, :], k.c_rstd[:, :],
                                ALU.subtract, ALU.mult).then_inc(k.s_dve.h, 1)
        self.on("vector", ve_body)

        def sc_body(sc):
            for ti in range(NT):
                sc.wait_ge(k.s_lx.h, ld_x[ti])
                if ti >= 1:
                    sc.wait_ge(k.s_dve.h, xln_done[ti - 1])
                sc.activation(k.sbf1[:, :], k.sbx[:, :], AF.Square,
                              accum_out=k.c_ssq[:, :]).then_inc(k.s_act.h, 1)
                sc.wait_ge(k.s_dve.h, red_done[ti])
                sc.mul(k.c_mean[:, :], k.c_tmp[:, :],
                       1.0 / D).then_inc(k.s_act.h, 1)
                sc.wait_ge(k.s_dve.h, var_done[ti])
                sc.activation(k.c_rstd[:, :], k.c_var[:, :], AF.Ln,
                              bias=k.eps_ln[:, :]).then_inc(k.s_act3.h, 1)
                sc.wait_ge(k.s_act3.h, k.e_a0_ln[ti])
                sc.activation(k.c_rstd[:, :], k.c_rstd[:, :], AF.Exp,
                              scale=-0.5).then_inc(k.s_act.h, 1)
                for q in range(4):
                    sc.wait_ge(k.s_pe.h, tp_done[ti][q])
                    sl = 2048 + 512 * q
                    if ti >= 1:
                        sc.wait_ge(k.st_s[q].h, st_done[ti - 1][q])
                    sc.copy(k.STAGE[:, sl:sl + 512],
                            k.psT[:, :]).then_inc(k.s_act2.h, 1)
                    sc.wait_ge(k.s_act2.h, evac_done[ti][q])
                    kk = ti // 4
                    dst = k.d_xlnT_own[kk][512 * q:512 * (q + 1),
                                           128 * (ti % 4):128 * (ti % 4) + 128]
                    dst = dst.rearrange("(i p) t -> p i t", p=128)
                    srcv = k.STAGE[:, sl:sl + 512].rearrange(
                        "p (i t) -> p i t", i=4)
                    sc.dma_start(dst, srcv).then_inc(k.st_s[q].h, 16)
        self.on("scalar", sc_body)

        def pe_body(t):
            for ti in range(NT):
                t.wait_ge(k.s_dve.h, xln_done[ti])
                for q in range(4):
                    gi = ti * 4 + q
                    if gi >= 1:
                        pti, pq = divmod(gi - 1, 4)
                        t.wait_ge(k.s_act2.h, evac_done[pti][pq])
                    for i in range(4):
                        dc = 4 * q + i
                        t.transpose(k.psT[:, 128 * i:128 * (i + 1)],
                                    k.STAGE[:, 128 * dc:128 * (dc + 1)],
                                    k.ident[:, :]).then_inc(k.s_pe.h, 1)
        self.on("tensor", pe_body)

        self.a0_spill_tots = st_tot
        self.flush("A0")

    def phase_A2(self):
        NTG = int(_os.environ.get("A2_TG", "16"))
        NOUT = int(_os.environ.get("A2_OUT", "12"))
        NOV = _os.environ.get("A2_NOV", "0") == "1"
        NOVST = _os.environ.get("A2_NOVST", "0") == "1"
        k = self
        ld_sem = lambda tg: (k.s_la if tg % 2 == 0 else k.s_lb)
        ld_done = [0] * 16
        mm_done = [[0] * 12 for _ in range(16)]
        evac_done = [[0] * 12 for _ in range(16)]
        vtp_done = [[0] * 4 for _ in range(16)]
        vte_done = [[0] * 4 for _ in range(16)]
        st_qk = [[0] * 8 for _ in range(16)]
        st_v = [[0] * 4 for _ in range(16)]
        for tg in range(NTG):
            ld_sem(tg).cnt += 16 * 16
            ld_done[tg] = ld_sem(tg).cnt
            for out in range(NOUT):
                k.s_pe.cnt += 1
                mm_done[tg][out] = k.s_pe.cnt
                k.s_act2.cnt += 1
                evac_done[tg][out] = k.s_act2.cnt
                if out < 8:
                    k.st_s[out % 4].cnt += 16
                    st_qk[tg][out] = k.st_s[out % 4].cnt
            if not NOV:
                for hl in range(4):
                    k.s_pe2.cnt += 4
                    vtp_done[tg][hl] = k.s_pe2.cnt
                    k.s_act3.cnt += 1
                    vte_done[tg][hl] = k.s_act3.cnt
                    k.st_s[4 + hl % 2].cnt += 16
                    st_v[tg][hl] = k.st_s[4 + hl % 2].cnt
        self.a2_st_tots = [(s.h, s.cnt) for s in self.st_s[0:6]]

        def a2g_body(g):
            for hh, cc in self.a0_spill_tots:
                g.wait_ge(hh, cc)
            for kk in range(4):
                if kk >= 1:
                    g.wait_ge(self.cc_gather[kk - 1].h, 1)
                g.collective_compute(
                    "AllGather", ALU.bypass, replica_groups=GROUPS,
                    ins=[k.d_xlnT_own[kk].ap().opt()],
                    outs=[k.d_xlnT_g[kk].ap().opt()],
                ).then_inc(self.cc_gather[kk].h)
        self.on("gpsimd", a2g_body)

        def sy_body(sy):
            for tg in range(NTG):
                kk, r = tg // 4, tg % 4
                sy.wait_ge(self.cc_gather[kk].h, 1)
                half = (tg % 2) * 8192
                if tg >= 2:
                    sy.wait_ge(k.s_pe.h, mm_done[tg - 2][NOUT - 1])
                for dc in range(16):
                    sy.dma_start(
                        k.BIGA[:, half + 512 * dc:half + 512 * (dc + 1)],
                        k.d_xlnT_g[kk][2048 * r + 128 * dc:
                                       2048 * r + 128 * (dc + 1), :],
                    ).then_inc(ld_sem(tg).h, 16)
        self.on("sync", sy_body)

        def pe_body(t):
            for tg in range(NTG):
                half = (tg % 2) * 8192
                t.wait_ge(ld_sem(tg).h, ld_done[tg])
                for out in range(NOUT):
                    wi, hl = divmod(out, 4)
                    bank = out % 4
                    gi = tg * NOUT + out
                    if gi >= 4:
                        pt, po = divmod(gi - 4, NOUT)
                        t.wait_ge(k.s_act2.h, evac_done[pt][po])
                    for dc in range(16):
                        mm = t.matmul(
                            k.ps[bank][:, :],
                            k.BIGB[:, 8192 * wi + 512 * dc + 128 * hl:
                                   8192 * wi + 512 * dc + 128 * (hl + 1)],
                            k.BIGA[:, half + 512 * dc:half + 512 * (dc + 1)],
                            start=(dc == 0), stop=(dc == 15))
                        if dc == 15:
                            mm.then_inc(k.s_pe.h, 1)
                for hl in range(0 if NOV else 4):
                    t.wait_ge(k.s_act2.h, evac_done[tg][8 + hl])
                    gi = tg * 4 + hl
                    if gi >= 1:
                        pt, ph = divmod(gi - 1, 4)
                        t.wait_ge(k.s_act3.h, vte_done[pt][ph])
                    vsl = 4096 + 512 * (hl % 2)
                    for i in range(4):
                        t.transpose(k.psT[:, 128 * i:128 * (i + 1)],
                                    k.STAGE[:, vsl + 128 * i:vsl + 128 * (i + 1)],
                                    k.ident[:, :]).then_inc(k.s_pe2.h, 1)
        self.on("tensor", pe_body)

        def sc_body(sc):
            for tg in range(NTG):
                kk, r = tg // 4, tg % 4
                tok = r * 2048 + 512 * kk
                def evac_v(hl):
                    bank = hl
                    vsl = 4096 + 512 * (hl % 2)
                    sc.wait_ge(k.s_pe.h, mm_done[tg][8 + hl])
                    if tg * 4 + hl >= 2:
                        pidx = tg * 4 + hl - 2
                        pt, ph = divmod(pidx, 4)
                        sc.wait_ge(k.s_pe2.h, vtp_done[pt][ph])
                    sc.copy(k.STAGE[:, vsl:vsl + 512],
                            k.ps[bank][:, :]).then_inc(k.s_act2.h, 1)
                for out in range(min(NOUT, 8)):
                    wi, hl = divmod(out, 4)
                    bank = out % 4
                    sc.wait_ge(k.s_pe.h, mm_done[tg][out])
                    sl = 512 * (out % 4)
                    if tg * 8 + out >= 4:
                        pidx = tg * 8 + out - 4
                        pt, po = divmod(pidx, 8)
                        sc.wait_ge(k.st_s[out % 4].h, st_qk[pt][po])
                    sc.copy(k.STAGE[:, sl:sl + 512],
                            k.ps[bank][:, :]).then_inc(k.s_act2.h, 1)
                    sc.wait_ge(k.s_act2.h, evac_done[tg][out])
                    dstt = (k.d_q if wi == 0 else k.d_k)[hl]
                    sc.dma_start(dstt[:, tok:tok + 512],
                                 k.STAGE[:, sl:sl + 512]
                                 ).then_inc(k.st_s[out % 4].h, 16)
                if NOUT > 8 and not NOV:
                    evac_v(0)
                    evac_v(1)
                for hl in range(0 if (NOV or NOUT <= 8) else 4):
                    sc.wait_ge(k.s_pe2.h, vtp_done[tg][hl])
                    osl = 6144 + 512 * (hl % 2)
                    if tg * 4 + hl >= 2:
                        pidx = tg * 4 + hl - 2
                        pt, ph = divmod(pidx, 4)
                        sc.wait_ge(k.st_s[4 + hl % 2].h, st_v[pt][ph])
                    sc.copy(k.STAGE[:, osl:osl + 512],
                            k.psT[:, :]).then_inc(k.s_act3.h, 1)
                    if hl + 2 <= 3:
                        evac_v(hl + 2)
                    if NOVST:
                        sc.dma_start(k.d_v[hl][tok:tok + 512, :],
                                     k.STAGE[:, osl:osl + 512].rearrange(
                                         "p (i c) -> (i p) c", i=4)
                                     ).then_inc(k.st_s[4 + hl % 2].h, 16) if False else None
                        sc.nop().then_inc(k.st_s[4 + hl % 2].h, 16)
                    else:
                        sc.wait_ge(k.s_act3.h, vte_done[tg][hl])
                        dst = k.d_v[hl][tok:tok + 512, :].rearrange(
                            "(i p) c -> p i c", p=128)
                        srcv = k.STAGE[:, osl:osl + 512].rearrange(
                            "p (i c) -> p i c", i=4)
                        sc.dma_start(dst, srcv).then_inc(k.st_s[4 + hl % 2].h, 16)
        self.on("scalar", sc_body)
        self.a2_mm_tot = (k.s_pe.h, k.s_pe.cnt)

    def phase_A3(self):
        k = self
        steps_all = []
        for gg in range(16):
            for tt in range(4 * gg + 4):
                steps_all.append((gg, tt))
        sidx = {s: i for i, s in enumerate(steps_all)}
        NS = len(steps_all)
        SB = [0, 1, 5]
        CB = [4, 2]

        ld_sem = lambda hh: (k.s_la if hh % 2 == 0 else k.s_lb)
        qkv_ld = [0] * MYH
        smm, expd, maskd, ladd, ctxd = {}, {}, {}, {}, {}
        lsum_d, recip_d, bcast_d, bccp_d, norm_d, stout_d = {}, {}, {}, {}, {}, {}
        head_done = [0] * MYH
        NG = MYH * 16

        def walk():
            for hh in range(MYH):
                ld_sem(hh).cnt += 16 * 3
                qkv_ld[hh] = ld_sem(hh).cnt
                for g in range(16):
                    gi = hh * 16 + g
                    K = 4 * g + 4
                    for kt in range(K):
                        k.s_pe.cnt += 1
                        smm[(hh, g, kt)] = k.s_pe.cnt
                        if kt == 0 and gi >= 1:
                            k.s_pe2.cnt += 1
                            bcast_d[gi - 1] = k.s_pe2.cnt
                        k.s_act.cnt += 1
                        expd[(hh, g, kt)] = k.s_act.cnt
                        if kt == 1 and gi >= 1:
                            k.s_act2.cnt += 1
                            bccp_d[gi - 1] = k.s_act2.cnt
                        if kt == min(5, K - 1) and gi >= 1:
                            k.st_s[6].cnt += 16
                            stout_d[gi - 1] = k.st_s[6].cnt
                        if kt >= 4 * g:
                            k.s_dve.cnt += 1
                            maskd[(hh, g, kt)] = k.s_dve.cnt
                        if kt % 2 == 1:
                            k.s_dve.cnt += 1
                            ladd[(hh, g, kt)] = k.s_dve.cnt
                            ladd[(hh, g, kt - 1)] = k.s_dve.cnt
                        k.s_pe2.cnt += 1
                        ctxd[(hh, g, kt)] = k.s_pe2.cnt
                    k.s_pe2.cnt += 1
                    lsum_d[gi] = k.s_pe2.cnt
                    k.s_dve.cnt += 1
                    recip_d[gi] = k.s_dve.cnt
                    k.s_dve.cnt += 1
                    norm_d[gi] = k.s_dve.cnt
                head_done[hh] = k.s_pe2.cnt
            k.s_pe2.cnt += 1
            bcast_d[NG - 1] = k.s_pe2.cnt
            k.s_act2.cnt += 1
            bccp_d[NG - 1] = k.s_act2.cnt
            k.st_s[6].cnt += 16
            stout_d[NG - 1] = k.st_s[6].cnt
        walk()
        a2_gate = self.a2_st_tots
        ctx_st_tot = k.st_s[6].cnt

        def prev_step(hh, g, kt, back):
            i = sidx[(g, kt)]
            if i >= back:
                pg, pk = steps_all[i - back]
                return (hh, pg, pk)
            if hh > 0:
                pg, pk = steps_all[NS - back + i]
                return (hh - 1, pg, pk)
            return None

        def pt_ready(hh, g, kt):
            if kt >= 4 * g:
                return (k.s_dve.h, maskd[(hh, g, kt)])
            return (k.s_act.h, expd[(hh, g, kt)])

        def sy_body(sy):
            for hh, cc in a2_gate:
                sy.wait_ge(hh, cc)
            sy.wait_ge(self.a2_mm_tot[0], self.a2_mm_tot[1])
            for hh in range(MYH):
                BIG = k.BIGA if hh % 2 == 0 else k.BIGB
                if hh >= 2:
                    sy.wait_ge(k.s_pe2.h, head_done[hh - 2])
                sy.dma_start(BIG[:, 0:8192], k.d_q[hh][:, :]
                             ).then_inc(ld_sem(hh).h, 16)
                sy.dma_start(BIG[:, 8192:16384], k.d_k[hh][:, :]
                             ).then_inc(ld_sem(hh).h, 16)
                sy.dma_start(
                    BIG[:, 16384:24576].rearrange("p (kt c) -> p kt c", c=128),
                    k.d_v[hh].rearrange("(kt p) c -> p kt c", p=128),
                ).then_inc(ld_sem(hh).h, 16)
        self.on("sync", sy_body)

        def emit_bcast(t, gi):
            t.wait_ge(k.s_dve.h, recip_d[gi])
            t.matmul(k.ps[3][:, :], k.ones_fr[:, :], k.recip_row[:, :],
                     start=True, stop=True, skip_group_check=True
                     ).then_inc(k.s_pe2.h, 1)

        def pe_body(t):
            for hh in range(MYH):
                BIG = k.BIGA if hh % 2 == 0 else k.BIGB
                t.wait_ge(ld_sem(hh).h, qkv_ld[hh])
                for g in range(16):
                    gi = hh * 16 + g
                    K = 4 * g + 4
                    for step in range(K + 1):
                        if step < K:
                            kt = step
                            ps3b = prev_step(hh, g, kt, 3)
                            if ps3b is not None:
                                t.wait_ge(k.s_act.h, expd[ps3b])
                            t.matmul(k.ps[SB[kt % 3]][:, :],
                                     BIG[:, 8192 + 128 * kt:8192 + 128 * (kt + 1)],
                                     BIG[:, 512 * g:512 * (g + 1)],
                                     start=True, stop=True,
                                     skip_group_check=True
                                     ).then_inc(k.s_pe.h, 1)
                            if kt == 0 and gi >= 1:
                                emit_bcast(t, gi - 1)
                        if step >= 1:
                            kt = step - 1
                            sh, cn = pt_ready(hh, g, kt)
                            t.wait_ge(sh, cn)
                            if kt == 0 and gi >= 2:
                                t.wait_ge(k.s_dve.h, norm_d[gi - 2])
                            t.matmul(k.ps[CB[gi % 2]][:, :],
                                     BIG[:, 16384 + 128 * kt:16384 + 128 * (kt + 1)],
                                     k.STAGE[:, 512 * (kt % 4):512 * (kt % 4) + 512],
                                     start=(kt == 0), stop=(kt == K - 1),
                                     skip_group_check=True
                                     ).then_inc(k.s_pe2.h, 1)
                    t.wait_ge(k.s_dve.h, ladd[(hh, g, K - 1)])
                    if gi >= 1:
                        t.wait_ge(k.s_act2.h, bccp_d[gi - 1])
                    t.matmul(k.ps[3][0:1, :], k.ones_fc[:, :],
                             k.evf[:, 0:512],
                             start=True, stop=False, skip_group_check=True)
                    t.matmul(k.ps[3][0:1, :], k.ones_fc[:, :],
                             k.evf[:, 512:1024],
                             start=False, stop=True, skip_group_check=True
                             ).then_inc(k.s_pe2.h, 1)
            emit_bcast(t, NG - 1)
        self.on("tensor", pe_body)

        def emit_bccp(sc, gi):
            sc.wait_ge(k.s_pe2.h, bcast_d[gi])
            if gi >= 1:
                sc.wait_ge(k.s_dve.h, norm_d[gi - 1])
            sc.copy(k.sbf1[:, 0:512], k.ps[3][:, :]).then_inc(k.s_act2.h, 1)

        def emit_store(sc, gi):
            hh, g = divmod(gi, 16)
            sc.wait_ge(k.s_dve.h, norm_d[gi])
            if gi >= 1:
                sc.wait_ge(k.st_s[6].h, stout_d[gi - 1])
            dst = k.d_ctx_own[hh][:, 512 * g:512 * (g + 1)]
            sc.dma_start(dst, k.STAGE[:, 2048 + 512 * (gi % 2):
                                      2560 + 512 * (gi % 2)]
                         ).then_inc(k.st_s[6].h, 16)

        def sc_body(sc):
            for hh, cc in a2_gate:
                sc.wait_ge(hh, cc)
            for hh in range(MYH):
                for g in range(16):
                    gi = hh * 16 + g
                    K = 4 * g + 4
                    for kt in range(K):
                        sc.wait_ge(k.s_pe.h, smm[(hh, g, kt)])
                        ps4b = prev_step(hh, g, kt, 4)
                        if ps4b is not None:
                            sc.wait_ge(k.s_pe2.h, ctxd[ps4b])
                            sc.wait_ge(k.s_dve.h, ladd[ps4b])
                        sc.activation(
                            k.STAGE[:, 512 * (kt % 4):512 * (kt % 4) + 512],
                            k.ps[SB[kt % 3]][:, :], AF.Exp,
                            scale=ISQ).then_inc(k.s_act.h, 1)
                        if kt == 1 and gi >= 1:
                            emit_bccp(sc, gi - 1)
                        if kt == min(5, K - 1) and gi >= 1:
                            emit_store(sc, gi - 1)
            emit_bccp(sc, NG - 1)
            emit_store(sc, NG - 1)
        self.on("scalar", sc_body)

        def ve_body(v):
            for hh, cc in a2_gate:
                v.wait_ge(hh, cc)
            for hh in range(MYH):
                for g in range(16):
                    gi = hh * 16 + g
                    K = 4 * g + 4
                    for kt in range(K):
                        v.wait_ge(k.s_act.h, expd[(hh, g, kt)])
                        sl = 512 * (kt % 4)
                        if kt >= 4 * g:
                            r = kt - 4 * g
                            v.tensor_mul(k.STAGE[:, sl:sl + 512],
                                         k.STAGE[:, sl:sl + 512],
                                         k.masks[:, 512 * r:512 * (r + 1)]
                                         ).then_inc(k.s_dve.h, 1)
                            v.wait_ge(k.s_dve.h, maskd[(hh, g, kt)])
                        if kt % 2 == 1:
                            psl = 512 * ((kt - 1) % 4)
                            if kt == 1:
                                if gi >= 1:
                                    v.wait_ge(k.s_pe2.h, lsum_d[gi - 1])
                                v.tensor_copy(k.evf[:, 0:1024],
                                              k.STAGE[:, psl:psl + 1024]
                                              ).then_inc(k.s_dve.h, 1)
                            else:
                                v.wait_ge(k.s_dve.h, ladd[(hh, g, kt - 2)])
                                v.scalar_tensor_tensor(
                                    k.evf[:, 0:1024],
                                    k.STAGE[:, psl:psl + 1024], 1.0,
                                    k.evf[:, 0:1024], ALU.mult, ALU.add
                                ).then_inc(k.s_dve.h, 1)
                    v.wait_ge(k.s_pe2.h, lsum_d[gi])
                    v.reciprocal(k.recip_row[:, :], k.ps[3][0:1, :]
                                 ).then_inc(k.s_dve.h, 1)
                    v.wait_ge(k.s_act2.h, bccp_d[gi])
                    v.wait_ge(k.s_dve.h, recip_d[gi])
                    if gi >= 2:
                        v.wait_ge(k.st_s[6].h, stout_d[gi - 2])
                    v.tensor_tensor(k.STAGE[:, 2048 + 512 * (gi % 2):
                                            2560 + 512 * (gi % 2)],
                                    k.ps[CB[gi % 2]][:, :],
                                    k.sbf1[:, 0:512], ALU.mult
                                    ).then_inc(k.s_dve.h, 1)
        self.on("vector", ve_body)

        cc_a2a = [self.new_cc() for _ in range(MYH)]
        self.cc_a2a = cc_a2a
        head_st = [stout_d[16 * hh + 15] for hh in range(MYH)]

        def g_body(g):
            for hh in range(MYH):
                g.wait_ge(k.st_s[6].h, head_st[hh])
                if hh >= 1:
                    g.wait_ge(cc_a2a[hh - 1].h, 1)
                g.collective_compute(
                    "AllGather", ALU.bypass, replica_groups=GROUPS,
                    ins=[k.d_ctx_own[hh].ap().opt()],
                    outs=[k.d_ctx_g[hh].ap().opt()],
                ).then_inc(cc_a2a[hh].h)
        self.on("gpsimd", g_body)
        self.flush("A2A3")

    def phase_A5(self):
        k = self
        gd_w = k.s_gd.cnt + 16 * 16
        k.s_gd.cnt = gd_w
        self.a5_gdw = gd_w

        def g_body(g):
            g.wait_ge(self.cc_a2a[0].h, 1)
            for dc in range(16):
                g.dma_start(k.BIGA[:, 2048 * dc:2048 * (dc + 1)],
                            k.woT[128 * dc:128 * (dc + 1), :]
                            ).then_inc(k.s_gd.h, 16)
        self.on("gpsimd", g_body)

        ld_sem = lambda ti: (k.s_la if ti % 2 == 0 else k.s_lb)
        e_ln = [[0] * 3 for _ in range(NT)]
        ld_ctx = [0] * NT
        ld_x = [0] * NT
        o_done = [[0] * 4 for _ in range(NT)]
        x1p_done = [[0] * 4 for _ in range(NT)]
        xbo_done = [0] * NT
        m = {}
        xln3_done = [0] * NT
        x1st_done = [0] * NT
        x3st_done = [0] * NT
        tp_done = [[0] * 4 for _ in range(NT)]
        tpe_done = [[0] * 4 for _ in range(NT)]
        st_sp = [[0] * 4 for _ in range(NT)]
        rt_done = [0] * NT
        act_last = [0] * NT

        for ti in range(NT):
            ld_sem(ti).cnt += 16 * 4
            ld_ctx[ti] = ld_sem(ti).cnt
            k.s_lx.cnt += 16
            ld_x[ti] = k.s_lx.cnt
            k.s_dve.cnt += 1
            xbo_done[ti] = k.s_dve.cnt
            for jg in range(4):
                k.s_pe.cnt += 1
                o_done[ti][jg] = k.s_pe.cnt
                k.s_dve.cnt += 1
                x1p_done[ti][jg] = k.s_dve.cnt
            k.st_s[7].cnt += 16
            x1st_done[ti] = k.st_s[7].cnt
            k.s_dve.cnt += 1
            m[("red2", ti)] = k.s_dve.cnt
            k.s_act.cnt += 2
            m[("ms2", ti)] = k.s_act.cnt
            k.s_dve.cnt += 3
            m[("var2", ti)] = k.s_dve.cnt
            k.s_act3.cnt += 1
            e_ln[ti][0] = k.s_act3.cnt
            k.s_act.cnt += 1
            m[("rstd2", ti)] = k.s_act.cnt
            k.s_dve.cnt += 1
            m[("xln2", ti)] = k.s_dve.cnt
            k.s_dve.cnt += 1
            m[("red3", ti)] = k.s_dve.cnt
            k.s_act.cnt += 2
            m[("ms3", ti)] = k.s_act.cnt
            k.s_dve.cnt += 3
            m[("var3", ti)] = k.s_dve.cnt
            k.s_act3.cnt += 1
            e_ln[ti][1] = k.s_act3.cnt
            k.s_act.cnt += 1
            m[("rstd3", ti)] = k.s_act.cnt
            k.s_dve.cnt += 3
            m[("x3f", ti)] = k.s_dve.cnt
            k.s_dve.cnt += 1
            xln3_done[ti] = k.s_dve.cnt
            k.st_s[5].cnt += 16
            x3st_done[ti] = k.st_s[5].cnt
            k.s_act.cnt += 1
            m[("rt_sq", ti)] = k.s_act.cnt
            k.s_act3.cnt += 1
            e_ln[ti][2] = k.s_act3.cnt
            k.s_act.cnt += 1
            m[("rt_act", ti)] = k.s_act.cnt
            act_last[ti] = k.s_act.cnt
            k.s_dve.cnt += 2
            rt_done[ti] = k.s_dve.cnt
            for q in range(4):
                k.s_pe2.cnt += 4
                tp_done[ti][q] = k.s_pe2.cnt
                k.s_act2.cnt += 1
                tpe_done[ti][q] = k.s_act2.cnt
                k.st_s[q].cnt += 16
                st_sp[ti][q] = k.st_s[q].cnt
        self.a5_sp_tots = [(k.st_s[q].h, k.st_s[q].cnt) for q in range(4)]
        self.a5_x3_tot = (k.st_s[5].h, k.st_s[5].cnt)
        self.a5_x1_tot = (k.st_s[7].h, k.st_s[7].cnt)

        def sy_body(sy):
            for cc in self.cc_a2a:
                sy.wait_ge(cc.h, 1)
            tok0 = (sy.partition_id() % 4) * 2048
            ctx3d = [k.d_ctx_g[hh].ap().rearrange("(r p) t -> p r t", p=128)
                     for hh in range(MYH)]
            for ti in range(NT):
                half = (ti % 2) * 2048
                if ti >= 2:
                    sy.wait_ge(k.s_pe.h, o_done[ti - 2][3])
                for hh in range(MYH):
                    srcv = ctx3d[hh][:, :, 128 * ti:][:, :, bass.ds(tok0, 128)]
                    dst = k.BIGB[:, half + 512 * hh:half + 512 * (hh + 1)
                                 ].rearrange("p (r t) -> p r t", t=128)
                    sy.dma_start(dst, srcv).then_inc(ld_sem(ti).h, 16)
                if ti >= 1:
                    sy.wait_ge(k.s_dve.h, xbo_done[ti - 1])
                    sy.wait_ge(k.s_act.h, m[("ms2", ti - 1)])
                sy.dma_start(k.sbx[:, :], k.x_home[128 * ti:128 * (ti + 1), :]
                             ).then_inc(k.s_lx.h, 16)
        self.on("sync", sy_body)

        def pe_body(t):
            t.wait_ge(k.s_gd.h, gd_w)
            for ti in range(NT):
                half = (ti % 2) * 2048
                t.wait_ge(ld_sem(ti).h, ld_ctx[ti])
                for jg in range(4):
                    if ti >= 1:
                        t.wait_ge(k.s_dve.h, x1p_done[ti - 1][jg])
                    for li in range(16):
                        hh4, r = divmod(li, 4)
                        mc = 4 * r + hh4
                        mm = t.matmul(
                            k.ps[jg][:, :],
                            k.BIGB[:, half + 128 * li:half + 128 * (li + 1)],
                            k.BIGA[:, 2048 * mc + 512 * jg:
                                   2048 * mc + 512 * (jg + 1)],
                            start=(li == 0), stop=(li == 15),
                            skip_group_check=True)
                        if li == 15:
                            mm.then_inc(k.s_pe.h, 1)
                t.wait_ge(k.s_dve.h, xln3_done[ti])
                for q in range(4):
                    gi = ti * 4 + q
                    if gi >= 1:
                        pti, pq = divmod(gi - 1, 4)
                        t.wait_ge(k.s_act2.h, tpe_done[pti][pq])
                    for i in range(4):
                        dc = 4 * q + i
                        t.transpose(
                            k.psT[:, 128 * i:128 * (i + 1)],
                            k.STAGE[:, 2048 + 128 * dc:2048 + 128 * (dc + 1)],
                            k.ident[:, :]).then_inc(k.s_pe2.h, 1)
        self.on("tensor", pe_body)

        def ve_body(v):
            for ti in range(NT):
                v.wait_ge(k.s_lx.h, ld_x[ti])
                if ti >= 1:
                    v.wait_ge(k.s_act.h, act_last[ti - 1])
                v.tensor_add(k.sbf3[:, :], k.sbx[:, :],
                             k.bc[:, 2 * D:3 * D]).then_inc(k.s_dve.h, 1)
                v.wait_ge(k.s_dve.h, xbo_done[ti])
                for jg in range(4):
                    v.wait_ge(k.s_pe.h, o_done[ti][jg])
                    if ti >= 1 and jg == 0:
                        v.wait_ge(k.st_s[7].h, x1st_done[ti - 1])
                    v.scalar_tensor_tensor(
                        k.sbf1[:, 512 * jg:512 * (jg + 1)],
                        k.ps[jg][:, :], 1.0,
                        k.sbf3[:, 512 * jg:512 * (jg + 1)],
                        ALU.mult, ALU.add).then_inc(k.s_dve.h, 1)
                v.wait_ge(k.s_dve.h, x1p_done[ti][3])
                v.reduce_sum(k.c_tmp[:, :], k.sbf1[:, :],
                             axis=AX.X).then_inc(k.s_dve.h, 1)
                v.wait_ge(k.s_act.h, m[("ms2", ti)])
                v.tensor_scalar_mul(k.c_var[:, :], k.c_ssq[:, :],
                                    1.0 / D).then_inc(k.s_dve.h, 1)
                v.tensor_mul(k.c_tmp2[:, :], k.c_mean[:, :], k.c_mean[:, :]
                             ).then_inc(k.s_dve.h, 1)
                v.wait_ge(k.s_dve.h, m[("var2", ti)] - 1)
                v.tensor_sub(k.c_var[:, :], k.c_var[:, :], k.c_tmp2[:, :]
                             ).then_inc(k.s_dve.h, 1)
                v.wait_ge(k.s_act.h, m[("rstd2", ti)])
                v.tensor_scalar(k.sbf2[:, :], k.sbf1[:, :], k.c_mean[:, :],
                                k.c_rstd[:, :], ALU.subtract, ALU.mult
                                ).then_inc(k.s_dve.h, 1)
                v.wait_ge(k.s_dve.h, m[("xln2", ti)])
                v.reduce_sum(k.c_tmp[:, :], k.sbf2[:, :],
                             axis=AX.X).then_inc(k.s_dve.h, 1)
                v.wait_ge(k.s_act.h, m[("ms3", ti)])
                v.tensor_scalar_mul(k.c_var[:, :], k.c_ssq[:, :],
                                    1.0 / D).then_inc(k.s_dve.h, 1)
                v.tensor_mul(k.c_tmp2[:, :], k.c_mean[:, :], k.c_mean[:, :]
                             ).then_inc(k.s_dve.h, 1)
                v.wait_ge(k.s_dve.h, m[("var3", ti)] - 1)
                v.tensor_sub(k.c_var[:, :], k.c_var[:, :], k.c_tmp2[:, :]
                             ).then_inc(k.s_dve.h, 1)
                v.wait_ge(k.s_act.h, m[("rstd3", ti)])
                v.tensor_scalar(k.sbf3[:, :], k.sbf2[:, :], k.c_mean[:, :],
                                k.c_rstd[:, :], ALU.subtract, ALU.mult
                                ).then_inc(k.s_dve.h, 1)
                v.wait_ge(k.s_dve.h, m[("x3f", ti)] - 2)
                v.tensor_mul(k.sbf2[:, :], k.sbf3[:, :], k.bc[:, 0:D]
                             ).then_inc(k.s_dve.h, 1)
                v.wait_ge(k.s_dve.h, m[("x3f", ti)] - 1)
                v.tensor_add(k.sbf3[:, :], k.sbf2[:, :], k.bc[:, D:2 * D]
                             ).then_inc(k.s_dve.h, 1)
                v.wait_ge(k.s_dve.h, m[("x3f", ti)])
                if ti >= 1:
                    v.wait_ge(k.s_pe2.h, tp_done[ti - 1][3])
                    v.wait_ge(k.st_s[5].h, x3st_done[ti - 1])
                v.tensor_copy(k.STAGE[:, 2048:4096], k.sbf3[:, :]
                              ).then_inc(k.s_dve.h, 1)
                v.wait_ge(k.s_act.h, m[("rt_act", ti)])
                v.tensor_scalar_min(k.c_rstd[:, :], k.c_rstd[:, :], 1e12
                                    ).then_inc(k.s_dve.h, 1)
                v.wait_ge(k.s_dve.h, rt_done[ti] - 1)
                v.tensor_mul(k.r_all[:, ti:ti + 1], k.c_rstd[:, :],
                             k.scale_col[:, :]).then_inc(k.s_dve.h, 1)
        self.on("vector", ve_body)

        def sc_body(sc):
            for ti in range(NT):
                if ti >= 1:
                    sc.wait_ge(k.s_dve.h, rt_done[ti - 1])
                sc.wait_ge(k.s_dve.h, x1p_done[ti][3])
                sc.dma_start(k.d_x1[128 * ti:128 * (ti + 1), :], k.sbf1[:, :]
                             ).then_inc(k.st_s[7].h, 16)
                sc.activation(k.STAGE[:, 0:2048], k.sbf1[:, :], AF.Square,
                              accum_out=k.c_ssq[:, :]).then_inc(k.s_act.h, 1)
                sc.wait_ge(k.s_dve.h, m[("red2", ti)])
                sc.mul(k.c_mean[:, :], k.c_tmp[:, :], 1.0 / D
                       ).then_inc(k.s_act.h, 1)
                sc.wait_ge(k.s_dve.h, m[("var2", ti)])
                sc.activation(k.c_rstd[:, :], k.c_var[:, :], AF.Ln,
                              bias=k.eps_ln[:, :]).then_inc(k.s_act3.h, 1)
                sc.wait_ge(k.s_act3.h, e_ln[ti][0])
                sc.activation(k.c_rstd[:, :], k.c_rstd[:, :], AF.Exp,
                              scale=-0.5).then_inc(k.s_act.h, 1)
                sc.wait_ge(k.s_dve.h, m[("xln2", ti)])
                sc.activation(k.STAGE[:, 0:2048], k.sbf2[:, :], AF.Square,
                              accum_out=k.c_ssq[:, :]).then_inc(k.s_act.h, 1)
                sc.wait_ge(k.s_dve.h, m[("red3", ti)])
                sc.mul(k.c_mean[:, :], k.c_tmp[:, :], 1.0 / D
                       ).then_inc(k.s_act.h, 1)
                sc.wait_ge(k.s_dve.h, m[("var3", ti)])
                sc.activation(k.c_rstd[:, :], k.c_var[:, :], AF.Ln,
                              bias=k.eps_ln[:, :]).then_inc(k.s_act3.h, 1)
                sc.wait_ge(k.s_act3.h, e_ln[ti][1])
                sc.activation(k.c_rstd[:, :], k.c_rstd[:, :], AF.Exp,
                              scale=-0.5).then_inc(k.s_act.h, 1)
                sc.wait_ge(k.s_dve.h, xln3_done[ti])
                sc.dma_start(k.d_xln3[128 * ti:128 * (ti + 1), :],
                             k.STAGE[:, 2048:4096]).then_inc(k.st_s[5].h, 16)
                sc.activation(k.STAGE[:, 0:2048], k.sbf3[:, :], AF.Square,
                              accum_out=k.c_ssq[:, :]).then_inc(k.s_act.h, 1)
                sc.wait_ge(k.s_act.h, m[("rt_sq", ti)])
                sc.activation(k.c_rstd[:, :], k.c_ssq[:, :], AF.Ln
                              ).then_inc(k.s_act3.h, 1)
                sc.wait_ge(k.s_act3.h, e_ln[ti][2])
                sc.activation(k.c_rstd[:, :], k.c_rstd[:, :], AF.Exp,
                              scale=-0.5).then_inc(k.s_act.h, 1)
                for q in range(4):
                    sc.wait_ge(k.s_pe2.h, tp_done[ti][q])
                    sl = 4096 + 512 * q
                    if ti >= 1:
                        sc.wait_ge(k.st_s[q].h, st_sp[ti - 1][q])
                    sc.copy(k.STAGE[:, sl:sl + 512], k.psT[:, :]
                            ).then_inc(k.s_act2.h, 1)
                    sc.wait_ge(k.s_act2.h, tpe_done[ti][q])
                    dst = k.d_xln3T[512 * q:512 * (q + 1),
                                    128 * ti:128 * (ti + 1)]
                    dst = dst.rearrange("(i p) t -> p i t", p=128)
                    srcv = k.STAGE[:, sl:sl + 512].rearrange(
                        "p (i t) -> p i t", i=4)
                    sc.dma_start(dst, srcv).then_inc(k.st_s[q].h, 16)
        self.on("scalar", sc_body)
        self.a5x = {
            "o15": o_done[15][3], "xbo15": xbo_done[15],
            "x1p15": x1p_done[15][3], "red2_15": m[("red2", 15)],
            "x1st15": x1st_done[15], "tp15": tp_done[15][3],
            "tpe15": tpe_done[15][3], "x3st15": x3st_done[15],
            "rt_done": list(rt_done)}

    def phase_F0(self):
        k = self
        ld = [0] * 16
        sq_done = [0] * 16
        cast_done = [0] * 16
        mm_done = [0] * 16
        for dc in range(16):
            k.s_lx.cnt += 16
            ld[dc] = k.s_lx.cnt
            k.s_act.cnt += 1
            sq_done[dc] = k.s_act.cnt
            k.s_dve.cnt += 1
            cast_done[dc] = k.s_dve.cnt
            k.s_pe.cnt += 1
            mm_done[dc] = k.s_pe.cnt
        k.s_act.cnt += 4
        ln4 = k.s_act.cnt
        k.s_act.cnt += 1
        rn_act = k.s_act.cnt
        k.s_dve.cnt += 1
        rn_min = k.s_dve.cnt
        pe_b, ac_b = [], []
        for j in range(4):
            k.s_pe.cnt += 1
            pe_b.append(k.s_pe.cnt)
            k.s_act.cnt += 1
            ac_b.append(k.s_act.cnt)

        def sy_body(sy):
            sy.wait_ge(k.s_dve.h, self.a5x["xbo15"])
            for dc in range(16):
                if dc >= 1:
                    sy.wait_ge(k.s_act.h, sq_done[dc - 1])
                    sy.wait_ge(k.s_dve.h, cast_done[dc - 1])
                sy.dma_start(k.sbx[:, :], k.mu2d[128 * dc:128 * (dc + 1), :]
                             ).then_inc(k.s_lx.h, 16)
        self.on("sync", sy_body)

        def sc_body(sc):
            sc.wait_ge(k.s_dve.h, self.a5x["red2_15"])
            sc.wait_ge(k.st_s[7].h, self.a5x["x1st15"])
            for dc in range(16):
                sc.wait_ge(k.s_lx.h, ld[dc])
                if dc >= 1:
                    sc.wait_ge(k.s_pe.h, mm_done[dc - 1])
                sc.activation(k.sbf1[:, :], k.sbx[:, :], AF.Square
                              ).then_inc(k.s_act.h, 1)
            sc.wait_ge(k.s_pe.h, mm_done[15])
            for j in range(4):
                sc.activation(k.g_r[0:1, 512 * j:512 * (j + 1)],
                              k.ps[2 + j][0:1, :], AF.Ln
                              ).then_inc(k.s_act.h, 1)
            sc.wait_ge(k.s_act.h, ln4)
            sc.activation(k.g_r[0:1, 0:D], k.g_r[0:1, 0:D], AF.Exp,
                          scale=-0.5).then_inc(k.s_act.h, 1)
            for j in range(4):
                sc.wait_ge(k.s_pe.h, pe_b[j])
                sc.copy(k.bc[:, 2 * D + 512 * j:2 * D + 512 * (j + 1)],
                        k.psW[:, :]).then_inc(k.s_act.h, 1)
        self.on("scalar", sc_body)

        def ve_body(v):
            v.wait_ge(k.s_pe.h, self.a5x["o15"])
            for dc in range(16):
                v.wait_ge(k.s_lx.h, ld[dc])
                v.tensor_copy(k.BIGA[:, 2048 * dc:2048 * (dc + 1)], k.sbx[:, :]
                              ).then_inc(k.s_dve.h, 1)
            v.wait_ge(k.s_act.h, rn_act)
            v.tensor_scalar_min(k.g_r[0:1, 0:D], k.g_r[0:1, 0:D], 1e12
                                ).then_inc(k.s_dve.h, 1)
        self.on("vector", ve_body)

        def pe_body(t):
            t.wait_ge(k.s_dve.h, self.a5x["x1p15"])
            for dc in range(16):
                t.wait_ge(k.s_act.h, sq_done[dc])
                for j in range(4):
                    mm = t.matmul(k.ps[2 + j][0:1, :], k.ones_fc[:, :],
                                  k.sbf1[:, 512 * j:512 * (j + 1)],
                                  start=(dc == 0), stop=(dc == 15),
                                  skip_group_check=True)
                    if j == 3:
                        mm.then_inc(k.s_pe.h, 1)
            t.wait_ge(k.s_dve.h, rn_min)
            for j in range(4):
                if j > 0:
                    t.wait_ge(k.s_act.h, ac_b[j - 1])
                t.matmul(k.psW[:, :], k.ones_fr[:, :],
                         k.g_r[0:1, 512 * j:512 * (j + 1)],
                         start=True, stop=True, skip_group_check=True
                         ).then_inc(k.s_pe.h, 1)
        self.on("tensor", pe_body)
        self.f0x = {"mm15": mm_done[15], "rnact": rn_act, "acb3": ac_b[3],
                    "cast15": cast_done[15]}

    def phase_F2(self):
        k = self
        ld_sem = lambda ti: (k.s_la if ti % 2 == 0 else k.s_lb)
        ld = [0] * NT
        mm_done = [[0] * 4 for _ in range(NT)]
        stt_done = [[0] * 4 for _ in range(NT)]
        exp_done = [[0] * 4 for _ in range(NT)]
        tp_done = [[0] * 4 for _ in range(NT)]
        tpe_done = [[0] * 4 for _ in range(NT)]
        st_p2 = [[0] * 4 for _ in range(NT)]
        st_p2t = [[0] * 4 for _ in range(NT)]
        for ti in range(NT):
            ld_sem(ti).cnt += 16 * 16
            ld[ti] = ld_sem(ti).cnt
            for eg in range(4):
                k.s_pe.cnt += 1
                mm_done[ti][eg] = k.s_pe.cnt
                k.s_dve.cnt += 1
                stt_done[ti][eg] = k.s_dve.cnt
                k.s_act.cnt += 1
                exp_done[ti][eg] = k.s_act.cnt
                k.st_s[eg].cnt += 16
                st_p2[ti][eg] = k.st_s[eg].cnt
                k.s_pe2.cnt += 4
                tp_done[ti][eg] = k.s_pe2.cnt
                k.s_act2.cnt += 1
                tpe_done[ti][eg] = k.s_act2.cnt
                k.st_s[4 + eg % 2].cnt += 16
                st_p2t[ti][eg] = k.st_s[4 + eg % 2].cnt
        gate = self.a5_sp_tots
        self.f2_p2_tots = [(k.st_s[i].h, k.st_s[i].cnt) for i in range(4)]
        self.f2_p2t_tots = [(k.st_s[4 + i].h, k.st_s[4 + i].cnt) for i in range(2)]

        def sy_body(sy):
            for hh, cc in gate:
                sy.wait_ge(hh, cc)
            sy.wait_ge(k.s_pe.h, self.a5x["o15"])
            for ti in range(NT):
                half = (ti % 2) * 2048
                if ti >= 2:
                    sy.wait_ge(k.s_pe.h, mm_done[ti - 2][3])
                for dc in range(16):
                    srcv = k.d_xln3T[128 * dc:128 * (dc + 1),
                                     128 * ti:128 * (ti + 1)]
                    sy.dma_start(k.BIGB[:, half + 128 * dc:half + 128 * (dc + 1)],
                                 srcv).then_inc(ld_sem(ti).h, 16)
        self.on("sync", sy_body)

        def pe_body(t):
            t.wait_ge(k.s_act.h, self.f0x["rnact"])
            t.wait_ge(k.s_act2.h, self.a5x["tpe15"])
            t.wait_ge(k.s_dve.h, self.f0x["cast15"])
            for ti in range(NT):
                half = (ti % 2) * 2048
                t.wait_ge(ld_sem(ti).h, ld[ti])
                if ti >= 1:
                    t.wait_ge(k.s_dve.h, stt_done[ti - 1][3])
                for dc in range(16):
                    for eg in range(4):
                        mm = t.matmul(k.ps[eg][:, :],
                                      k.BIGB[:, half + 128 * dc:
                                             half + 128 * (dc + 1)],
                                      k.BIGA[:, 2048 * dc + 512 * eg:
                                             2048 * dc + 512 * (eg + 1)],
                                      start=(dc == 0), stop=(dc == 15),
                                      skip_group_check=True)
                        if dc == 15:
                            mm.then_inc(k.s_pe.h, 1)
                for eg in range(4):
                    t.wait_ge(k.s_act.h, exp_done[ti][eg])
                    gi = ti * 4 + eg
                    if gi >= 1:
                        pti, pq = divmod(gi - 1, 4)
                        t.wait_ge(k.s_act2.h, tpe_done[pti][pq])
                    sl = 512 * eg
                    for i in range(4):
                        t.transpose(k.psT[:, 128 * i:128 * (i + 1)],
                                    k.STAGE[:, sl + 128 * i:sl + 128 * (i + 1)],
                                    k.ident[:, :]).then_inc(k.s_pe2.h, 1)
        self.on("tensor", pe_body)

        def ve_body(v):
            v.wait_ge(k.s_pe.h, self.f0x["mm15"])
            v.wait_ge(k.s_act.h, self.f0x["acb3"])
            for ti in range(NT):
                v.wait_ge(k.s_dve.h, self.a5x["rt_done"][ti])
                for eg in range(4):
                    v.wait_ge(k.s_pe.h, mm_done[ti][eg])
                    if ti >= 1:
                        v.wait_ge(k.s_act.h, exp_done[ti - 1][eg])
                    v.scalar_tensor_tensor(
                        k.sbf1[:, 512 * eg:512 * (eg + 1)], k.ps[eg][:, :],
                        k.r_all[:, ti:ti + 1],
                        k.bc[:, 2 * D + 512 * eg:2 * D + 512 * (eg + 1)],
                        ALU.mult, ALU.mult).then_inc(k.s_dve.h, 1)
        self.on("vector", ve_body)

        def sc_body(sc):
            sc.wait_ge(k.s_pe2.h, self.a5x["tp15"])
            sc.wait_ge(k.st_s[5].h, self.a5x["x3st15"])
            for ti in range(NT):
                for eg in range(4):
                    sc.wait_ge(k.s_dve.h, stt_done[ti][eg])
                    sl = 512 * eg
                    if ti >= 1:
                        sc.wait_ge(k.s_pe2.h, tp_done[ti - 1][eg])
                        sc.wait_ge(k.st_s[eg].h, st_p2[ti - 1][eg])
                    sc.activation(k.STAGE[:, sl:sl + 512],
                                  k.sbf1[:, 512 * eg:512 * (eg + 1)],
                                  AF.Exp).then_inc(k.s_act.h, 1)
                    sc.wait_ge(k.s_act.h, exp_done[ti][eg])
                    sc.dma_start(
                        k.d_P2[128 * ti:128 * (ti + 1),
                               512 * eg:512 * (eg + 1)],
                        k.STAGE[:, sl:sl + 512]).then_inc(k.st_s[eg].h, 16)
                for eg in range(4):
                    sc.wait_ge(k.s_pe2.h, tp_done[ti][eg])
                    sl = 2048 + 512 * (eg % 2)
                    gi = ti * 4 + eg
                    if gi >= 2:
                        pti, pq = divmod(gi - 2, 4)
                        sc.wait_ge(k.st_s[4 + eg % 2].h, st_p2t[pti][pq])
                    sc.copy(k.STAGE[:, sl:sl + 512], k.psT[:, :]
                            ).then_inc(k.s_act2.h, 1)
                    sc.wait_ge(k.s_act2.h, tpe_done[ti][eg])
                    dst = k.d_P2T[512 * eg:512 * (eg + 1),
                                  128 * ti:128 * (ti + 1)]
                    dst = dst.rearrange("(i p) t -> p i t", p=128)
                    srcv = k.STAGE[:, sl:sl + 512].rearrange(
                        "p (i t) -> p i t", i=4)
                    sc.dma_start(dst, srcv).then_inc(k.st_s[4 + eg % 2].h, 16)
        self.on("scalar", sc_body)
        self.f2x = {"mm15": mm_done[15][3],
                    "stt15": stt_done[15][3]}

    def phase_F3(self):
        k = self
        ld_sem = lambda ec: (k.s_la if ec % 2 == 0 else k.s_lb)
        k.s_lc.cnt += 16 * NT
        lda_done = k.s_lc.cnt
        ORD = [4 * j + el for el in range(4) for j in range(4)]
        ld = [0] * 16
        mm_done = [0] * 16
        ev_done = [0] * 16
        st_done = [0] * 16
        for i in range(16):
            ld_sem(i).cnt += 16 * 16
            ld[i] = ld_sem(i).cnt
            k.s_pe.cnt += 1
            mm_done[i] = k.s_pe.cnt
            k.s_act.cnt += 5
            ev_done[i] = k.s_act.cnt
            k.st_s[6].cnt += 16
            st_done[i] = k.st_s[6].cnt
        gate = self.f2_p2_tots + [self.a5_x3_tot]
        rs_st = [st_done[4 * el + 3] for el in range(4)]

        def sy_body(sy):
            for hh, cc in gate:
                sy.wait_ge(hh, cc)
            sy.wait_ge(k.s_pe.h, self.f2x["mm15"])
            for tc in range(NT):
                sy.dma_start(k.BIGA[:, 2048 * tc:2048 * (tc + 1)],
                             k.d_xln3[128 * tc:128 * (tc + 1), :]
                             ).then_inc(k.s_lc.h, 16)
            for i in range(16):
                ec = ORD[i]
                half = (i % 2) * 2048
                if i >= 2:
                    sy.wait_ge(k.s_pe.h, mm_done[i - 2])
                for tc in range(16):
                    srcv = k.d_P2[128 * tc:128 * (tc + 1),
                                  128 * ec:128 * (ec + 1)]
                    sy.dma_start(k.BIGB[:, half + 128 * tc:half + 128 * (tc + 1)],
                                 srcv).then_inc(ld_sem(i).h, 16)
        self.on("sync", sy_body)

        def pe_body(t):
            t.wait_ge(k.s_dve.h, self.f2x["stt15"])
            t.wait_ge(k.s_act.h, self.f0x["acb3"])
            t.wait_ge(k.s_lc.h, lda_done)
            for i in range(16):
                ec = ORD[i]
                half = (i % 2) * 2048
                t.wait_ge(ld_sem(i).h, ld[i])
                if i >= 1:
                    t.wait_ge(k.s_act.h, ev_done[i - 1])
                for tc in range(16):
                    for dg in range(4):
                        t.matmul(k.ps[dg][:, :],
                                 k.BIGB[:, half + 128 * tc:half + 128 * (tc + 1)],
                                 k.BIGA[:, 2048 * tc + 512 * dg:
                                        2048 * tc + 512 * (dg + 1)],
                                 start=(tc == 0), stop=(tc == 15),
                                 skip_group_check=True)
                    mm = t.matmul(k.psW[:, 0:4],
                                  k.BIGB[:, half + 128 * tc:half + 128 * (tc + 1)],
                                  k.ones4[:, :],
                                  start=(tc == 0), stop=(tc == 15),
                                  skip_group_check=True)
                    if tc == 15:
                        mm.then_inc(k.s_pe.h, 1)
        self.on("tensor", pe_body)

        def sc_body(sc):
            for i in range(16):
                ec = ORD[i]
                el, j = ec % 4, ec // 4
                sc.wait_ge(k.s_pe.h, mm_done[i])
                if i >= 1:
                    sc.wait_ge(k.st_s[6].h, st_done[i - 1])
                for dg in range(4):
                    sc.copy(k.evf[:, 512 * dg:512 * (dg + 1)],
                            k.ps[dg][:, :]).then_inc(k.s_act.h, 1)
                sc.copy(k.evf[:, 2048:2052], k.psW[:, 0:4]
                        ).then_inc(k.s_act.h, 1)
                sc.wait_ge(k.s_act.h, ev_done[i])
                sc.dma_start(k.d_U_p[el][128 * j:128 * (j + 1), :],
                             k.evf[:, :]).then_inc(k.st_s[6].h, 16)
        self.on("scalar", sc_body)

        cc_rs = [self.new_cc() for _ in range(4)]
        self.cc_rs = cc_rs

        def g_body(g):
            g.wait_ge(k.s_gd.h, self.a5_gdw)
            for dc in range(16):
                g.dma_start(k.BIGB[:, 8192 + 512 * dc:8192 + 512 * (dc + 1)],
                            k.w1_my[0, 128 * dc:128 * (dc + 1), :]
                            ).then_inc(k.s_gd.h, 16)
            for hc in range(4):
                g.dma_start(
                    k.BIGB[:, 16384 + 2048 * hc:16384 + 2048 * (hc + 1)],
                    k.w2_my[0, 128 * hc:128 * (hc + 1), :]
                ).then_inc(k.s_wd.h, 16)
            for el in range(4):
                g.wait_ge(k.st_s[6].h, rs_st[el])
                if el >= 1:
                    g.wait_ge(cc_rs[el - 1].h, 1)
                g.collective_compute(
                    "ReduceScatter", ALU.add, replica_groups=GROUPS,
                    ins=[k.d_U_p[el].ap().opt()],
                    outs=[k.d_U_m[el].ap().opt()],
                ).then_inc(cc_rs[el].h)
        self.on("gpsimd", g_body)
        self.flush("A5F3")

    def phase_F4(self):
        k = self
        uld = [0] * 4
        b1ld = [0] * 4
        b2ld = [0] * 4
        w1ld = [0] * 4
        w2ld = [0] * 4
        recz = [0] * 4
        castu = [0] * 4
        tpu = [[0] * 4 for _ in range(4)]
        tpue = [[0] * 4 for _ in range(4)]
        b1b = [0] * 4
        b1bc = [0] * 4
        b2b = [[0] * 4 for _ in range(4)]
        b2bc = [[0] * 4 for _ in range(4)]
        h1m = [0] * 4
        hpre = [0] * 4
        hglu = [0] * 4
        tph = [0] * 4
        tphe = [0] * 4
        som = [[0] * 4 for _ in range(4)]
        sod = [[0] * 4 for _ in range(4)]
        stso = [0] * 4
        for el in range(4):
            k.s_lx.cnt += 16
            uld[el] = k.s_lx.cnt
            k.s_lr.cnt += 16
            b1ld[el] = k.s_lr.cnt
            k.s_lr.cnt += 16
            b2ld[el] = k.s_lr.cnt
            k.s_gd.cnt += 16 * 16
            w1ld[el] = k.s_gd.cnt
            k.s_wd.cnt += 16 * 4
            w2ld[el] = k.s_wd.cnt
            k.s_dve.cnt += 1
            recz[el] = k.s_dve.cnt
            k.s_dve.cnt += 1
            castu[el] = k.s_dve.cnt
            for q in range(4):
                k.s_pe.cnt += 4
                tpu[el][q] = k.s_pe.cnt
                k.s_act.cnt += 1
                tpue[el][q] = k.s_act.cnt
            k.s_pe.cnt += 1
            b1b[el] = k.s_pe.cnt
            k.s_act.cnt += 1
            b1bc[el] = k.s_act.cnt
            for j in range(4):
                k.s_pe.cnt += 1
                b2b[el][j] = k.s_pe.cnt
                k.s_act.cnt += 1
                b2bc[el][j] = k.s_act.cnt
            k.s_pe.cnt += 1
            h1m[el] = k.s_pe.cnt
            k.s_dve.cnt += 1
            hpre[el] = k.s_dve.cnt
            k.s_act.cnt += 1
            hglu[el] = k.s_act.cnt
            k.s_pe.cnt += 4
            tph[el] = k.s_pe.cnt
            k.s_act.cnt += 1
            tphe[el] = k.s_act.cnt
            for dg in range(4):
                k.s_pe.cnt += 1
                som[el][dg] = k.s_pe.cnt
                k.s_dve.cnt += 1
                sod[el][dg] = k.s_dve.cnt
            k.st_s[7].cnt += 16
            stso[el] = k.st_s[7].cnt
        slot_st_tot = k.st_s[7].cnt

        def sy_body(sy):
            for el in range(4):
                sy.wait_ge(self.cc_rs[el].h, 1)
                if el >= 1:
                    sy.wait_ge(k.s_dve.h, castu[el - 1])
                sy.dma_start(k.evf[:, :],
                             k.d_U_m[el][:, :]
                             ).then_inc(k.s_lx.h, 16)
                if el >= 1:
                    sy.wait_ge(k.s_pe.h, b2b[el - 1][3])
                sy.dma_start(k.g_r[0:1, 0:HFF],
                             k.b1_my[el:el + 1, :]).then_inc(k.s_lr.h, 16)
                sy.wait_ge(k.s_pe.h, b1b[el])
                sy.dma_start(k.g_r[0:1, 0:D],
                             k.b2_my[el:el + 1, :]).then_inc(k.s_lr.h, 16)
        self.on("sync", sy_body)

        def wload(g, el):
            if el >= 1:
                g.wait_ge(k.s_pe.h, h1m[el - 1])
            for dc in range(16):
                g.dma_start(k.BIGB[:, 8192 + 512 * dc:8192 + 512 * (dc + 1)],
                            k.w1_my[el, 128 * dc:128 * (dc + 1), :]
                            ).then_inc(k.s_gd.h, 16)
            if el >= 1:
                g.wait_ge(k.s_pe.h, som[el - 1][3])
            for hc in range(4):
                g.dma_start(
                    k.BIGB[:, 16384 + 2048 * hc:16384 + 2048 * (hc + 1)],
                    k.w2_my[el, 128 * hc:128 * (hc + 1), :]
                ).then_inc(k.s_wd.h, 16)
        self.f4_wload = wload

        def ve_body(v):
            for hh, cc in self.f2_p2_tots:
                v.wait_ge(hh, cc)
            for el in range(4):
                v.wait_ge(k.s_lx.h, uld[el])
                v.reciprocal(k.c_tmp[:, :], k.evf[:, 2048:2049]
                             ).then_inc(k.s_dve.h, 1)
                if el >= 1:
                    v.wait_ge(k.s_pe.h, tpu[el - 1][3])
                v.tensor_copy(k.STAGE[:, 0:2048], k.evf[:, 0:2048]
                              ).then_inc(k.s_dve.h, 1)
                v.wait_ge(k.s_pe.h, h1m[el])
                v.wait_ge(k.s_act.h, b1bc[el])
                v.wait_ge(k.s_dve.h, recz[el])
                if el >= 1:
                    v.wait_ge(k.s_act.h, hglu[el - 1])
                v.scalar_tensor_tensor(k.sbf1[:, 0:512], k.ps[0][:, :],
                                       k.c_tmp[:, :], k.bc[:, 0:512],
                                       ALU.mult, ALU.add).then_inc(k.s_dve.h, 1)
                for dg in range(4):
                    v.wait_ge(k.s_pe.h, som[el][dg])
                    v.wait_ge(k.s_act.h, b2bc[el][dg])
                    if el >= 1 and dg == 0:
                        v.wait_ge(k.st_s[7].h, stso[el - 1])
                    v.tensor_add(
                        k.STAGE[:, 5120 + 512 * dg:5120 + 512 * (dg + 1)],
                        k.ps[1 + dg][:, :],
                        k.bc[:, D + 512 * dg:D + 512 * (dg + 1)]
                    ).then_inc(k.s_dve.h, 1)
        self.on("vector", ve_body)

        def pe_body(t):
            for el in range(4):
                t.wait_ge(k.s_dve.h, castu[el])
                for q in range(4):
                    gi = el * 4 + q
                    if gi >= 1:
                        pel, pq = divmod(gi - 1, 4)
                        t.wait_ge(k.s_act.h, tpue[pel][pq])
                    for i in range(4):
                        dc = 4 * q + i
                        t.transpose(
                            k.psT[:, 128 * i:128 * (i + 1)],
                            k.STAGE[:, 128 * dc:128 * (dc + 1)],
                            k.ident[:, :]).then_inc(k.s_pe.h, 1)
                t.wait_ge(k.s_lr.h, b1ld[el])
                if el >= 1:
                    t.wait_ge(k.s_act.h, b1bc[el - 1])
                    t.wait_ge(k.s_dve.h, hpre[el - 1])
                t.matmul(k.psW[:, 0:HFF], k.ones_fr[:, :], k.g_r[0:1, 0:HFF],
                         start=True, stop=True, skip_group_check=True
                         ).then_inc(k.s_pe.h, 1)
                t.wait_ge(k.s_lr.h, b2ld[el])
                for j in range(4):
                    if el + j >= 1:
                        prev = b2bc[el][j - 1] if j >= 1 else b2bc[el - 1][3]
                        t.wait_ge(k.s_act.h, prev)
                        if j == 0 and el >= 1:
                            t.wait_ge(k.s_dve.h, sod[el - 1][3])
                    t.matmul(k.ps[5][:, :], k.ones_fr[:, :],
                             k.g_r[0:1, 512 * j:512 * (j + 1)],
                             start=True, stop=True, skip_group_check=True
                             ).then_inc(k.s_pe.h, 1)
                t.wait_ge(k.s_act.h, tpue[el][3])
                t.wait_ge(k.s_gd.h, w1ld[el])
                if el >= 1:
                    t.wait_ge(k.s_dve.h, hpre[el - 1])
                for dc in range(16):
                    mm = t.matmul(
                        k.ps[0][:, :],
                        k.STAGE[:, 2048 + 128 * dc:2048 + 128 * (dc + 1)],
                        k.BIGB[:, 8192 + 512 * dc:8192 + 512 * (dc + 1)],
                        start=(dc == 0), stop=(dc == 15),
                        skip_group_check=True)
                    if dc == 15:
                        mm.then_inc(k.s_pe.h, 1)
                t.wait_ge(k.s_act.h, hglu[el])
                if el >= 1:
                    t.wait_ge(k.s_act.h, tphe[el - 1])
                for i in range(4):
                    t.transpose(k.psT[:, 128 * i:128 * (i + 1)],
                                k.STAGE[:, 4096 + 128 * i:4096 + 128 * (i + 1)],
                                k.ident[:, :]).then_inc(k.s_pe.h, 1)
                t.wait_ge(k.s_act.h, tphe[el])
                t.wait_ge(k.s_wd.h, w2ld[el])
                for dg in range(4):
                    if el >= 1:
                        t.wait_ge(k.s_dve.h, sod[el - 1][dg])
                    for hc in range(4):
                        mm = t.matmul(
                            k.ps[1 + dg][:, :],
                            k.STAGE[:, 4608 + 128 * hc:4608 + 128 * (hc + 1)],
                            k.BIGB[:, 16384 + 2048 * hc + 512 * dg:
                                   16384 + 2048 * hc + 512 * (dg + 1)],
                            start=(hc == 0), stop=(hc == 3),
                            skip_group_check=True)
                        if hc == 3:
                            mm.then_inc(k.s_pe.h, 1)
        self.on("tensor", pe_body)

        def sc_body(sc):
            for hh, cc in self.f2_p2t_tots:
                sc.wait_ge(hh, cc)
            for el in range(4):
                for q in range(4):
                    sc.wait_ge(k.s_pe.h, tpu[el][q])
                    sl = 2048 + 512 * q
                    sc.copy(k.STAGE[:, sl:sl + 512], k.psT[:, :]
                            ).then_inc(k.s_act.h, 1)
                sc.wait_ge(k.s_pe.h, b1b[el])
                if el >= 1:
                    sc.wait_ge(k.s_dve.h, hpre[el - 1])
                sc.copy(k.bc[:, 0:512], k.psW[:, 0:HFF]).then_inc(k.s_act.h, 1)
                for j in range(4):
                    sc.wait_ge(k.s_pe.h, b2b[el][j])
                    if el >= 1:
                        sc.wait_ge(k.s_dve.h, sod[el - 1][j])
                    sc.copy(k.bc[:, D + 512 * j:D + 512 * (j + 1)],
                            k.ps[5][:, :]).then_inc(k.s_act.h, 1)
                sc.wait_ge(k.s_dve.h, hpre[el])
                sc.activation(k.STAGE[:, 4096:4608], k.sbf1[:, 0:512],
                              AF.Gelu).then_inc(k.s_act.h, 1)
                sc.wait_ge(k.s_pe.h, tph[el])
                sc.copy(k.STAGE[:, 4608:5120], k.psT[:, :]
                        ).then_inc(k.s_act.h, 1)
                sc.wait_ge(k.s_dve.h, sod[el][3])
                sc.dma_start(k.d_sp[el][:, :],
                             k.STAGE[:, 5120:7168]).then_inc(k.st_s[7].h, 16)
        self.on("scalar", sc_body)

        cc_sg = [self.new_cc() for _ in range(4)]
        self.cc_sg = cc_sg

        def g_body2(g):
            for el in range(4):
                if el + 1 <= 3:
                    self.f4_wload(g, el + 1)
                if el == 0:
                    g.wait_ge(self.cc_rs[3].h, 1)
                g.wait_ge(k.st_s[7].h, stso[el])
                if el >= 1:
                    g.wait_ge(cc_sg[el - 1].h, 1)
                g.collective_compute(
                    "AllGather", ALU.bypass, replica_groups=GROUPS,
                    ins=[k.d_sp[el].ap().opt()],
                    outs=[k.d_sg[el].ap().opt()],
                ).then_inc(cc_sg[el].h)
        self.on("gpsimd", g_body2)
        self.f4x = {"tphe3": tphe[3], "sod33": sod[3][3],
                    "stso3": stso[3]}

    def phase_F5(self):
        k = self
        ld_sem = lambda ti: (k.s_la if ti % 2 == 0 else k.s_lb)
        k.s_lc.cnt += 16 * 16
        lda = k.s_lc.cnt
        ld = [0] * NT
        ldx = [0] * NT
        mm_done = [0] * NT
        rec_done = [0] * NT
        out_done = [[0] * 4 for _ in range(NT)]
        st_y = [0] * NT
        for ti in range(NT):
            ld_sem(ti).cnt += 16 * 16
            ld[ti] = ld_sem(ti).cnt
            k.s_lx.cnt += 16
            ldx[ti] = k.s_lx.cnt
            k.s_pe.cnt += 1
            mm_done[ti] = k.s_pe.cnt
            k.s_dve.cnt += 1
            rec_done[ti] = k.s_dve.cnt
            for dg in range(4):
                k.s_dve.cnt += 1
                out_done[ti][dg] = k.s_dve.cnt
            k.st_s[6].cnt += 16
            st_y[ti] = k.st_s[6].cnt
        gate = self.f2_p2t_tots + [self.a5_x1_tot]

        def sy_body(sy):
            for hh, cc in gate:
                sy.wait_ge(hh, cc)
            for el in range(4):
                sy.wait_ge(self.cc_sg[el].h, 1)
                for j in range(4):
                    ec = 4 * j + el
                    sy.dma_start(k.BIGA[:, 2048 * ec:2048 * (ec + 1)],
                                 k.d_sg[el][128 * j:128 * (j + 1), :]
                                 ).then_inc(k.s_lc.h, 16)
            for ti in range(NT):
                half = (ti % 2) * 2048
                if ti >= 2:
                    sy.wait_ge(k.s_pe.h, mm_done[ti - 2])
                for ec in range(16):
                    srcv = k.d_P2T[128 * ec:128 * (ec + 1),
                                   128 * ti:128 * (ti + 1)]
                    sy.dma_start(k.BIGB[:, half + 128 * ec:half + 128 * (ec + 1)],
                                 srcv).then_inc(ld_sem(ti).h, 16)
                if ti >= 1:
                    sy.wait_ge(k.s_dve.h, out_done[ti - 1][3])
                sy.dma_start(k.sbx[:, :], k.d_x1[128 * ti:128 * (ti + 1), :]
                             ).then_inc(k.s_lx.h, 16)
        self.on("sync", sy_body)

        def pe_body(t):
            t.wait_ge(k.s_act.h, self.f4x["tphe3"])
            t.wait_ge(k.s_dve.h, self.f4x["sod33"])
            t.wait_ge(k.s_lc.h, lda)
            for ti in range(NT):
                half = (ti % 2) * 2048
                t.wait_ge(ld_sem(ti).h, ld[ti])
                if ti >= 1:
                    t.wait_ge(k.s_dve.h, out_done[ti - 1][3])
                    t.wait_ge(k.s_dve.h, rec_done[ti - 1])
                for ec in range(16):
                    for dg in range(4):
                        t.matmul(k.ps[dg][:, :],
                                 k.BIGB[:, half + 128 * ec:half + 128 * (ec + 1)],
                                 k.BIGA[:, 2048 * ec + 512 * dg:
                                        2048 * ec + 512 * (dg + 1)],
                                 start=(ec == 0), stop=(ec == 15),
                                 skip_group_check=True)
                    mm = t.matmul(k.psW[:, 0:1],
                                  k.BIGB[:, half + 128 * ec:half + 128 * (ec + 1)],
                                  k.ones_bc[:, :],
                                  start=(ec == 0), stop=(ec == 15),
                                  skip_group_check=True)
                    if ec == 15:
                        mm.then_inc(k.s_pe.h, 1)
        self.on("tensor", pe_body)

        def ve_body(v):
            v.wait_ge(k.s_act.h, self.f4x["tphe3"])
            v.wait_ge(k.st_s[7].h, self.f4x["stso3"])
            for ti in range(NT):
                v.wait_ge(k.s_pe.h, mm_done[ti])
                v.wait_ge(k.s_lx.h, ldx[ti])
                v.reciprocal(k.c_tmp[:, :], k.psW[:, 0:1]
                             ).then_inc(k.s_dve.h, 1)
                if ti >= 1:
                    v.wait_ge(k.st_s[6].h, st_y[ti - 1])
                v.wait_ge(k.s_dve.h, rec_done[ti])
                for dg in range(4):
                    v.scalar_tensor_tensor(
                        k.sbf1[:, 512 * dg:512 * (dg + 1)], k.ps[dg][:, :],
                        k.c_tmp[:, :], k.sbx[:, 512 * dg:512 * (dg + 1)],
                        ALU.mult, ALU.add).then_inc(k.s_dve.h, 1)
        self.on("vector", ve_body)

        def sc_body(sc):
            for ti in range(NT):
                sc.wait_ge(k.s_dve.h, out_done[ti][3])
                sc.dma_start(k.y[128 * ti:128 * (ti + 1), :], k.sbf1[:, :]
                             ).then_inc(k.st_s[6].h, 16)
        self.on("scalar", sc_body)
        self.flush("F4F5")


def build_nc(stage=99, debug=False):
    return KB(stage=stage, debug=debug).build()


_NC_CACHE = {}


def _get_nc():
    if "nc" not in _NC_CACHE:
        _NC_CACHE["nc"] = build_nc()
    return _NC_CACHE["nc"]


def make_in_maps(inputs):
    ins = {kk: np.asarray(v) for kk, v in inputs.items()}
    x = ins["L0_attn_x"].astype(np.float32, copy=False)
    wq, wk, wv = ins["L0_attn_wq"], ins["L0_attn_wk"], ins["L0_attn_wv"]
    wo, bo = ins["L0_attn_wo"], ins["L0_attn_bo"]
    gamma, beta = ins["L0_ffn_gamma"], ins["L0_ffn_beta"]
    mu, scale = ins["L0_ffn_mu"], ins["L0_ffn_scale"]
    W1, b1 = ins["L0_ffn_W1"], ins["L0_ffn_b1"]
    W2, b2 = ins["L0_ffn_W2"], ins["L0_ffn_b2"]
    wqT = np.ascontiguousarray(wq.T)
    wkT = np.ascontiguousarray(wk.T)
    wvT = np.ascontiguousarray(wv.T)
    woT = np.ascontiguousarray(wo.T)
    mu2d = np.ascontiguousarray(mu.reshape(D, ES))
    in_maps = []
    for c in range(8):
        bg, j = divmod(c, 4)
        in_maps.append({
            "x_home": np.ascontiguousarray(x[bg, TH * j:TH * (j + 1), :]),
            "wq_my": np.ascontiguousarray(wqT[:, 512 * j:512 * (j + 1)]),
            "wk_my": np.ascontiguousarray(wkT[:, 512 * j:512 * (j + 1)]),
            "wv_my": np.ascontiguousarray(wvT[:, 512 * j:512 * (j + 1)]),
            "woT": woT,
            "bo": np.ascontiguousarray(bo.reshape(1, D)).astype(np.float32),
            "gamma": np.ascontiguousarray(gamma.reshape(1, D)).astype(np.float32),
            "beta": np.ascontiguousarray(beta.reshape(1, D)).astype(np.float32),
            "scale_in": np.ascontiguousarray(scale.reshape(1, 1)).astype(np.float32),
            "mu2d": mu2d,
            "w1_my": np.ascontiguousarray(W1[4 * j:4 * (j + 1)]),
            "b1_my": np.ascontiguousarray(b1[4 * j:4 * (j + 1)]),
            "w2_my": np.ascontiguousarray(W2[4 * j:4 * (j + 1)]),
            "b2_my": np.ascontiguousarray(b2[4 * j:4 * (j + 1)]),
        })
    return in_maps


def kernel(**inputs):
    nc = _get_nc()
    in_maps = make_in_maps(inputs)
    res = run_bass_kernel_spmd(nc, in_maps, core_ids=list(range(8)))
    out = np.empty((B, S, D), np.float32)
    for c in range(8):
        bg, j = divmod(c, 4)
        out[bg, TH * j:TH * (j + 1), :] = res.results[c]["y"]
    return out


# version 11
# speedup vs baseline: 1.2524x; 1.0146x over previous
import math
import os as _os
from contextlib import ExitStack

import numpy as np

import concourse.bass as bass
import concourse.mybir as mybir
from concourse.bass_utils import run_bass_kernel_spmd

F32 = mybir.dt.float32
BF16 = mybir.dt.bfloat16
AF = mybir.ActivationFunctionType
ALU = mybir.AluOpType
AX = mybir.AxisListType

B, S, D, H, HD = 2, 8192, 2048, 16, 128
GROUPS = [[0, 1, 2, 3], [4, 5, 6, 7]]
TH = 2048
NT = TH // 128
E, SLOT, HFF = 16, 128, 512
ES = E * SLOT
MYH, MYE = 4, 4
ISQ = 1.0 / math.sqrt(HD)
LN_EPS = 1e-5
UW = D + 4

ENGS = ("sync", "tensor", "scalar", "vector", "gpsimd")


class Sem:
    def __init__(self, h):
        self.h = h
        self.cnt = 0


class KB:
    def __init__(self, stage=99, debug=False):
        self.stage = stage
        self.debug = debug
        self.ops = {e: [] for e in ENGS}

    def on(self, eng_name, fn):
        self.ops[eng_name].append(fn)

    def flush(self, name=None):
        ops, self.ops = self.ops, {e: [] for e in ENGS}
        with self.nc.Block() as blk:
            for e in ENGS:
                if ops[e]:
                    def body(eng, _ops=tuple(ops[e])):
                        for f in _ops:
                            f(eng)
                    getattr(blk, e)(body)

    def new_cc(self):
        s = self.cc_sems[self.cc_used]
        self.cc_used += 1
        return s

    def build(self):
        nc = bass.Bass("TRN2", target_bir_lowering=False)
        self.nc = nc

        self.x_home = nc.dram_tensor("x_home", [TH, D], F32, kind="ExternalInput")
        self.wq_my = nc.dram_tensor("wq_my", [D, MYH * HD], F32, kind="ExternalInput")
        self.wk_my = nc.dram_tensor("wk_my", [D, MYH * HD], F32, kind="ExternalInput")
        self.wv_my = nc.dram_tensor("wv_my", [D, MYH * HD], F32, kind="ExternalInput")
        self.woT = nc.dram_tensor("woT", [D, D], F32, kind="ExternalInput")
        self.bo = nc.dram_tensor("bo", [1, D], F32, kind="ExternalInput")
        self.gamma = nc.dram_tensor("gamma", [1, D], F32, kind="ExternalInput")
        self.beta = nc.dram_tensor("beta", [1, D], F32, kind="ExternalInput")
        self.scale_in = nc.dram_tensor("scale_in", [1, 1], F32, kind="ExternalInput")
        self.mu2d = nc.dram_tensor("mu2d", [D, ES], F32, kind="ExternalInput")
        self.w1_my = nc.dram_tensor("w1_my", [MYE, D, HFF], F32, kind="ExternalInput")
        self.b1_my = nc.dram_tensor("b1_my", [MYE, HFF], F32, kind="ExternalInput")
        self.w2_my = nc.dram_tensor("w2_my", [MYE, HFF, D], F32, kind="ExternalInput")
        self.b2_my = nc.dram_tensor("b2_my", [MYE, D], F32, kind="ExternalInput")
        self.y = nc.dram_tensor("y", [TH, D], F32, kind="ExternalOutput")

        self.d_xlnT_own = [nc.dram_tensor(f"d_xlnT_own{k}", [D, 512], BF16)
                           for k in range(4)]
        self.d_xlnT_g = [nc.dram_tensor(f"d_xlnT_g{k}", [4 * D, 512], BF16)
                         for k in range(4)]
        self.d_q = [nc.dram_tensor(f"d_q{h}", [HD, S], BF16) for h in range(MYH)]
        self.d_k = [nc.dram_tensor(f"d_k{h}", [HD, S], BF16) for h in range(MYH)]
        self.d_v = [nc.dram_tensor(f"d_v{h}", [S, HD], BF16) for h in range(MYH)]
        self.d_ctx_own = [nc.dram_tensor(f"d_ctx_own{h}", [HD, S], BF16)
                          for h in range(MYH)]
        self.d_ctx_g = [nc.dram_tensor(f"d_ctx_g{h}", [4 * HD, S], BF16)
                        for h in range(MYH)]
        self.d_x1 = nc.dram_tensor("d_x1", [TH, D], F32)
        self.d_xln3 = nc.dram_tensor("d_xln3", [TH, D], BF16)
        self.d_xln3T = nc.dram_tensor("d_xln3T", [D, TH], BF16)
        self.d_P2 = nc.dram_tensor("d_P2", [TH, ES], BF16)
        self.d_P2T = nc.dram_tensor("d_P2T", [ES, TH], BF16)
        self.d_U_p = [nc.dram_tensor(f"d_U_p{e}", [4 * SLOT, UW], F32)
                      for e in range(MYE)]
        self.d_U_m = [nc.dram_tensor(f"d_U_m{e}", [SLOT, UW], F32)
                      for e in range(MYE)]
        self.d_sp = [nc.dram_tensor(f"d_sp{e}", [SLOT, D], BF16)
                     for e in range(MYE)]
        self.d_sg = [nc.dram_tensor(f"d_sg{e}", [4 * SLOT, D], BF16)
                     for e in range(MYE)]

        self.dbg = {}
        if self.debug:
            def mkdbg(name, src, shape, dt):
                t = nc.dram_tensor("dbg_" + name, shape, dt, kind="ExternalOutput")
                self.dbg[name] = (t, src)
            for kk in range(4):
                mkdbg(f"xlnT_g{kk}", self.d_xlnT_g[kk], [4 * D, 512], BF16)
                mkdbg(f"xlnT_o{kk}", self.d_xlnT_own[kk], [D, 512], BF16)
            for h in range(MYH):
                mkdbg(f"q{h}", self.d_q[h], [HD, S], BF16)
                mkdbg(f"k{h}", self.d_k[h], [HD, S], BF16)
                mkdbg(f"v{h}", self.d_v[h], [S, HD], BF16)
            mkdbg("x1", self.d_x1, [TH, D], F32)
            mkdbg("xln3", self.d_xln3, [TH, D], BF16)
            mkdbg("xln3T", self.d_xln3T, [D, TH], BF16)
            mkdbg("P2", self.d_P2, [TH, ES], BF16)
            mkdbg("P2T", self.d_P2T, [ES, TH], BF16)

        with ExitStack() as ctx:
            en = ctx.enter_context
            self.BIGA = en(nc.sbuf_tensor("BIGA", [128, 32768], BF16))
            self.BIGB = en(nc.sbuf_tensor("BIGB", [128, 24576], BF16))
            self.STAGE = en(nc.sbuf_tensor("STAGE", [128, 7168], BF16))
            self.sbx = en(nc.sbuf_tensor("sbx", [128, D], F32))
            self.sbf1 = en(nc.sbuf_tensor("sbf1", [128, D], F32))
            self.sbf2 = en(nc.sbuf_tensor("sbf2", [128, D], F32))
            self.sbf3 = en(nc.sbuf_tensor("sbf3", [128, D], F32))
            self.evf = en(nc.sbuf_tensor("evf", [128, UW], F32))
            self.bc = en(nc.sbuf_tensor("bc_pool", [128, 3 * D], F32))
            self.Lt = en(nc.sbuf_tensor("Lt", [128, 512], F32))
            self.masks = en(nc.sbuf_tensor("masks", [128, 2048], BF16))
            self.ident = en(nc.sbuf_tensor("ident", [128, 128], BF16))
            self.ones_fc = en(nc.sbuf_tensor("ones_fc", [128, 1], F32))
            self.ones_fr = en(nc.sbuf_tensor("ones_fr", [1, 128], F32))
            self.ones_bc = en(nc.sbuf_tensor("ones_bc", [128, 1], BF16))
            self.ones4 = en(nc.sbuf_tensor("ones4", [128, 4], BF16))
            self.eps_ln = en(nc.sbuf_tensor("eps_ln", [128, 1], F32))
            self.scale_col = en(nc.sbuf_tensor("scale_col", [128, 1], F32))
            self.r_all = en(nc.sbuf_tensor("r_all", [128, NT], F32))
            self.c_mean = en(nc.sbuf_tensor("c_mean", [128, 1], F32))
            self.c_ssq = en(nc.sbuf_tensor("c_ssq", [128, 1], F32))
            self.c_var = en(nc.sbuf_tensor("c_var", [128, 1], F32))
            self.c_rstd = en(nc.sbuf_tensor("c_rstd", [128, 1], F32))
            self.c_tmp = en(nc.sbuf_tensor("c_tmp", [128, 1], F32))
            self.c_tmp2 = en(nc.sbuf_tensor("c_tmp2", [128, 1], F32))
            self.recip_row = en(nc.sbuf_tensor("recip_row", [1, 512], F32))
            self.g_r = en(nc.sbuf_tensor("g_row", [1, D], F32))
            self.psw = [en(nc.psum_tensor(f"pw{i}", [128, 1024], F32))
                        for i in range(3)]
            self.ps = [self.psw[i // 2][:, 512 * (i % 2):512 * (i % 2) + 512]
                       for i in range(6)]
            self.psT = en(nc.psum_tensor("psT", [128, 512], BF16))
            self.psW = en(nc.psum_tensor("psW", [128, 512], F32))
            for nm in ("gd", "gd2", "wd", "pe", "act", "dve", "pe2", "act2", "act3",
                       "la", "lb", "lc", "lx", "lr"):
                setattr(self, "s_" + nm, Sem(en(nc.semaphore("s_" + nm))))
            self.st_s = [Sem(en(nc.semaphore(f"s_sts{i}"))) for i in range(8)]
            self.cc_sems = [Sem(en(nc.semaphore(f"s_cc{i}"))) for i in range(16)]
            self.cc_used = 0

            self.phase0()
            if self.stage >= 1:
                self.phase_A0()
            if self.stage >= 2:
                self.phase_A2()
            if self.stage == 2:
                self.flush("A2only")
            if self.stage >= 3:
                self.phase_A3()
            if self.stage >= 4:
                self.phase_A5()
            if self.stage >= 5:
                self.phase_F0()
                self.phase_F2()
            if self.stage >= 6:
                self.phase_F3()
            if 4 <= self.stage < 6:
                self.flush("A5F3partial")
            if self.stage >= 7:
                self.phase_F4()
            if self.stage == 7:
                self.flush("F4only")
            if self.stage >= 8:
                self.phase_F5()

            if self.debug:
                k = self
                gd_now = k.s_gd.cnt + 16 * len(self.dbg)
                k.s_gd.cnt = gd_now
                st_tots_d = [(s.h, s.cnt) for s in self.st_s]
                cc_n_d = self.cc_used

                def g_dbg(g):
                    for i in range(cc_n_d):
                        g.wait_ge(self.cc_sems[i].h, 1)
                    for hh, cc in st_tots_d:
                        g.wait_ge(hh, cc)
                    for name, (t, src) in k.dbg.items():
                        g.dma_start(t.ap(), src.ap()).then_inc(k.s_gd.h, 16)
                    g.wait_ge(k.s_gd.h, gd_now)
                self.on("gpsimd", g_dbg)
                self.flush("dbg")

            st_tots = [(s.h, s.cnt) for s in self.st_s]
            gd_f, gd2_f, wd_f = self.s_gd.cnt, self.s_gd2.cnt, self.s_wd.cnt
            cc_n = self.cc_used

            def fin(sy):
                for hh, cc in st_tots:
                    sy.wait_ge(hh, cc)
                sy.wait_ge(self.s_gd.h, gd_f)
                sy.wait_ge(self.s_gd2.h, gd2_f)
                sy.wait_ge(self.s_wd.h, wd_f)
                for i in range(cc_n):
                    sy.wait_ge(self.cc_sems[i].h, 1)
            self.on("sync", fin)
            self.flush("fin")
        return nc

    def phase0(self):
        k = self
        k.s_gd2.cnt += 7
        gd_target = k.s_gd.cnt + 16 * 48

        def g_body(g):
            g.memset(k.masks[:, :], 1.0).then_inc(k.s_gd2.h, 1)
            g.wait_ge(k.s_gd2.h, 1)
            for r in range(4):
                g.affine_select(out=k.masks[:, 512 * r:512 * (r + 1)],
                                in_=k.masks[:, 512 * r:512 * (r + 1)],
                                compare_op=ALU.is_ge, fill=0.0, base=-128 * r,
                                pattern=[[1, 512]], channel_multiplier=-1
                                ).then_inc(k.s_gd2.h, 1)
            g.memset(k.ident[:, :], 0.0).then_inc(k.s_gd2.h, 1)
            g.wait_ge(k.s_gd2.h, 6)
            g.affine_select(out=k.ident[:, :], in_=k.ident[:, :],
                            compare_op=ALU.not_equal, fill=1.0, base=0,
                            pattern=[[-1, 128]], channel_multiplier=1)
            g.memset(k.ones_fc[:, :], 1.0)
            g.memset(k.ones_fr[:, :], 1.0)
            g.memset(k.ones_bc[:, :], 1.0)
            g.memset(k.ones4[:, :], 0.0).then_inc(k.s_gd2.h, 1)
            g.wait_ge(k.s_gd2.h, 7)
            g.memset(k.ones4[:, 0:1], 1.0)
            g.memset(k.eps_ln[:, :], LN_EPS)
            if _os.environ.get("SIMINIT", "0") == "1":
                g.memset(k.BIGA[:, :], 0.0)
                g.memset(k.STAGE[:, :], 0.0)
            for wi, w in enumerate((k.wq_my, k.wk_my, k.wv_my)):
                for dc in range(16):
                    g.dma_start(
                        k.BIGB[:, 8192 * wi + 512 * dc:8192 * wi + 512 * (dc + 1)],
                        w[128 * dc:128 * (dc + 1), :],
                    ).then_inc(k.s_gd.h, 16)
            g.wait_ge(k.s_gd.h, gd_target)
        k.s_gd.cnt += 16 * 48
        self.on("gpsimd", g_body)
        self.flush("p0")

        for src_ap, width, dst_fn in (
            (self.gamma[:, :], D, lambda j: self.bc[:, 512 * j:512 * (j + 1)]),
            (self.beta[:, :], D, lambda j: self.bc[:, D + 512 * j:D + 512 * (j + 1)]),
            (self.bo[:, :], D, lambda j: self.bc[:, 2 * D + 512 * j:2 * D + 512 * (j + 1)]),
            (self.scale_in[:, :], 1, lambda j: self.scale_col[:, :]),
        ):
            nsl = (width + 511) // 512
            k.s_lr.cnt += 16
            ld_at = k.s_lr.cnt
            pe_at, ac_at = [], []
            for j in range(nsl):
                k.s_pe.cnt += 1
                pe_at.append(k.s_pe.cnt)
                k.s_act.cnt += 1
                ac_at.append(k.s_act.cnt)

            def sy_body(sy, src_ap=src_ap, width=width):
                sy.dma_start(k.g_r[:, 0:width], src_ap).then_inc(k.s_lr.h, 16)
            self.on("sync", sy_body)

            def pe_body(t, nsl=nsl, width=width, ld_at=ld_at, ac_at=tuple(ac_at)):
                t.wait_ge(k.s_lr.h, ld_at)
                for j in range(nsl):
                    w = min(512, width - 512 * j)
                    if j > 0:
                        t.wait_ge(k.s_act.h, ac_at[j - 1])
                    t.matmul(k.psW[:, 0:w], k.ones_fr[:, :],
                             k.g_r[:, 512 * j:512 * j + w],
                             start=True, stop=True).then_inc(k.s_pe.h, 1)
            self.on("tensor", pe_body)

            def sc_body(sc, nsl=nsl, width=width, dst_fn=dst_fn,
                        pe_at=tuple(pe_at)):
                for j in range(nsl):
                    w = min(512, width - 512 * j)
                    sc.wait_ge(k.s_pe.h, pe_at[j])
                    sc.copy(dst_fn(j), k.psW[:, 0:w]).then_inc(k.s_act.h, 1)
            self.on("scalar", sc_body)
            self.flush("bcast")

    def phase_A0(self):
        k = self
        k.e_a0_ln = [0] * NT
        ld_x = [0] * NT
        red_done = [0] * NT
        meansq_done = [0] * NT
        var_done = [0] * NT
        xln_done = [0] * NT
        ssq_done = [0] * NT
        mean_done = [0] * NT
        rstd_done = [0] * NT
        tp_done = [[0] * 4 for _ in range(NT)]
        evac_done = [[0] * 4 for _ in range(NT)]
        st_done = [[0] * 4 for _ in range(NT)]

        for ti in range(NT):
            k.s_lx.cnt += 16
            ld_x[ti] = k.s_lx.cnt
            k.s_dve.cnt += 1
            red_done[ti] = k.s_dve.cnt
            k.s_act.cnt += 1
            ssq_done[ti] = k.s_act.cnt
            k.s_act.cnt += 1
            mean_done[ti] = k.s_act.cnt
            k.s_dve.cnt += 2
            meansq_done[ti] = k.s_dve.cnt
            k.s_dve.cnt += 1
            var_done[ti] = k.s_dve.cnt
            k.s_act3.cnt += 1
            k.e_a0_ln[ti] = k.s_act3.cnt
            k.s_act.cnt += 1
            rstd_done[ti] = k.s_act.cnt
            k.s_dve.cnt += 1
            xln_done[ti] = k.s_dve.cnt
            for q in range(4):
                k.s_pe.cnt += 4
                tp_done[ti][q] = k.s_pe.cnt
                k.s_act2.cnt += 1
                evac_done[ti][q] = k.s_act2.cnt
                k.st_s[q].cnt += 16
                st_done[ti][q] = k.st_s[q].cnt
        cc_gather = [self.new_cc() for _ in range(4)]
        self.cc_gather = cc_gather
        st_tot = [(k.st_s[q].h, k.st_s[q].cnt) for q in range(4)]

        def sy_body(sy):
            for ti in range(NT):
                if ti >= 1:
                    sy.wait_ge(k.s_dve.h, xln_done[ti - 1])
                    sy.wait_ge(k.s_act.h, ssq_done[ti - 1])
                sy.dma_start(k.sbx[:, :],
                             k.x_home[128 * ti:128 * (ti + 1), :]
                             ).then_inc(k.s_lx.h, 16)
        self.on("sync", sy_body)

        def ve_body(v):
            for ti in range(NT):
                v.wait_ge(k.s_lx.h, ld_x[ti])
                if ti >= 1:
                    v.wait_ge(k.s_act.h, rstd_done[ti - 1])
                    v.wait_ge(k.s_pe.h, tp_done[ti - 1][3])
                v.reduce_sum(k.c_tmp[:, :], k.sbx[:, :],
                             axis=AX.X).then_inc(k.s_dve.h, 1)
                v.wait_ge(k.s_act.h, mean_done[ti])
                v.tensor_scalar_mul(k.c_var[:, :], k.c_ssq[:, :],
                                    1.0 / D).then_inc(k.s_dve.h, 1)
                v.tensor_mul(k.c_tmp2[:, :], k.c_mean[:, :],
                             k.c_mean[:, :]).then_inc(k.s_dve.h, 1)
                v.wait_ge(k.s_dve.h, meansq_done[ti])
                v.tensor_sub(k.c_var[:, :], k.c_var[:, :],
                             k.c_tmp2[:, :]).then_inc(k.s_dve.h, 1)
                v.wait_ge(k.s_act.h, rstd_done[ti])
                v.tensor_scalar(k.STAGE[:, 0:2048], k.sbx[:, :],
                                k.c_mean[:, :], k.c_rstd[:, :],
                                ALU.subtract, ALU.mult).then_inc(k.s_dve.h, 1)
        self.on("vector", ve_body)

        def sc_body(sc):
            for ti in range(NT):
                sc.wait_ge(k.s_lx.h, ld_x[ti])
                if ti >= 1:
                    sc.wait_ge(k.s_dve.h, xln_done[ti - 1])
                sc.activation(k.sbf1[:, :], k.sbx[:, :], AF.Square,
                              accum_out=k.c_ssq[:, :]).then_inc(k.s_act.h, 1)
                sc.wait_ge(k.s_dve.h, red_done[ti])
                sc.mul(k.c_mean[:, :], k.c_tmp[:, :],
                       1.0 / D).then_inc(k.s_act.h, 1)
                sc.wait_ge(k.s_dve.h, var_done[ti])
                sc.activation(k.c_rstd[:, :], k.c_var[:, :], AF.Ln,
                              bias=k.eps_ln[:, :]).then_inc(k.s_act3.h, 1)
                sc.wait_ge(k.s_act3.h, k.e_a0_ln[ti])
                sc.activation(k.c_rstd[:, :], k.c_rstd[:, :], AF.Exp,
                              scale=-0.5).then_inc(k.s_act.h, 1)
                for q in range(4):
                    sc.wait_ge(k.s_pe.h, tp_done[ti][q])
                    sl = 2048 + 512 * q
                    if ti >= 1:
                        sc.wait_ge(k.st_s[q].h, st_done[ti - 1][q])
                    sc.copy(k.STAGE[:, sl:sl + 512],
                            k.psT[:, :]).then_inc(k.s_act2.h, 1)
                    sc.wait_ge(k.s_act2.h, evac_done[ti][q])
                    kk = ti // 4
                    dst = k.d_xlnT_own[kk][512 * q:512 * (q + 1),
                                           128 * (ti % 4):128 * (ti % 4) + 128]
                    dst = dst.rearrange("(i p) t -> p i t", p=128)
                    srcv = k.STAGE[:, sl:sl + 512].rearrange(
                        "p (i t) -> p i t", i=4)
                    sc.dma_start(dst, srcv).then_inc(k.st_s[q].h, 16)
        self.on("scalar", sc_body)

        def pe_body(t):
            for ti in range(NT):
                t.wait_ge(k.s_dve.h, xln_done[ti])
                for q in range(4):
                    gi = ti * 4 + q
                    if gi >= 1:
                        pti, pq = divmod(gi - 1, 4)
                        t.wait_ge(k.s_act2.h, evac_done[pti][pq])
                    for i in range(4):
                        dc = 4 * q + i
                        t.transpose(k.psT[:, 128 * i:128 * (i + 1)],
                                    k.STAGE[:, 128 * dc:128 * (dc + 1)],
                                    k.ident[:, :]).then_inc(k.s_pe.h, 1)
        self.on("tensor", pe_body)

        self.a0_spill_tots = st_tot
        self.flush("A0")

    def phase_A2(self):
        NTG = int(_os.environ.get("A2_TG", "16"))
        NOUT = int(_os.environ.get("A2_OUT", "12"))
        NOV = _os.environ.get("A2_NOV", "0") == "1"
        NOVST = _os.environ.get("A2_NOVST", "0") == "1"
        k = self
        ld_sem = lambda tg: (k.s_la if tg % 2 == 0 else k.s_lb)
        ld_done = [0] * 16
        mm_done = [[0] * 12 for _ in range(16)]
        evac_done = [[0] * 12 for _ in range(16)]
        vtp_done = [[0] * 4 for _ in range(16)]
        vte_done = [[0] * 4 for _ in range(16)]
        st_qk = [[0] * 8 for _ in range(16)]
        st_v = [[0] * 4 for _ in range(16)]
        for tg in range(NTG):
            ld_sem(tg).cnt += 16 * 16
            ld_done[tg] = ld_sem(tg).cnt
            for out in range(NOUT):
                k.s_pe.cnt += 1
                mm_done[tg][out] = k.s_pe.cnt
                k.s_act2.cnt += 1
                evac_done[tg][out] = k.s_act2.cnt
                if out < 8:
                    k.st_s[out % 4].cnt += 16
                    st_qk[tg][out] = k.st_s[out % 4].cnt
            if not NOV:
                for hl in range(4):
                    k.s_pe2.cnt += 4
                    vtp_done[tg][hl] = k.s_pe2.cnt
                    k.s_act3.cnt += 1
                    vte_done[tg][hl] = k.s_act3.cnt
                    k.st_s[4 + hl % 2].cnt += 16
                    st_v[tg][hl] = k.st_s[4 + hl % 2].cnt
        self.a2_st_tots = [(s.h, s.cnt) for s in self.st_s[0:6]]

        def a2g_body(g):
            for hh, cc in self.a0_spill_tots:
                g.wait_ge(hh, cc)
            for kk in range(4):
                if kk >= 1:
                    g.wait_ge(self.cc_gather[kk - 1].h, 1)
                g.collective_compute(
                    "AllGather", ALU.bypass, replica_groups=GROUPS,
                    ins=[k.d_xlnT_own[kk].ap().opt()],
                    outs=[k.d_xlnT_g[kk].ap().opt()],
                ).then_inc(self.cc_gather[kk].h)
        self.on("gpsimd", a2g_body)

        def sy_body(sy):
            for tg in range(NTG):
                kk, r = tg // 4, tg % 4
                sy.wait_ge(self.cc_gather[kk].h, 1)
                half = (tg % 2) * 8192
                if tg >= 2:
                    sy.wait_ge(k.s_pe.h, mm_done[tg - 2][NOUT - 1])
                for dc in range(16):
                    sy.dma_start(
                        k.BIGA[:, half + 512 * dc:half + 512 * (dc + 1)],
                        k.d_xlnT_g[kk][2048 * r + 128 * dc:
                                       2048 * r + 128 * (dc + 1), :],
                    ).then_inc(ld_sem(tg).h, 16)
        self.on("sync", sy_body)

        def pe_body(t):
            for tg in range(NTG):
                half = (tg % 2) * 8192
                t.wait_ge(ld_sem(tg).h, ld_done[tg])
                for out in range(NOUT):
                    wi, hl = divmod(out, 4)
                    bank = out % 4
                    gi = tg * NOUT + out
                    if gi >= 4:
                        pt, po = divmod(gi - 4, NOUT)
                        t.wait_ge(k.s_act2.h, evac_done[pt][po])
                    for dc in range(16):
                        mm = t.matmul(
                            k.ps[bank][:, :],
                            k.BIGB[:, 8192 * wi + 512 * dc + 128 * hl:
                                   8192 * wi + 512 * dc + 128 * (hl + 1)],
                            k.BIGA[:, half + 512 * dc:half + 512 * (dc + 1)],
                            start=(dc == 0), stop=(dc == 15))
                        if dc == 15:
                            mm.then_inc(k.s_pe.h, 1)
                for hl in range(0 if NOV else 4):
                    t.wait_ge(k.s_act2.h, evac_done[tg][8 + hl])
                    gi = tg * 4 + hl
                    if gi >= 1:
                        pt, ph = divmod(gi - 1, 4)
                        t.wait_ge(k.s_act3.h, vte_done[pt][ph])
                    vsl = 4096 + 512 * (hl % 2)
                    for i in range(4):
                        t.transpose(k.psT[:, 128 * i:128 * (i + 1)],
                                    k.STAGE[:, vsl + 128 * i:vsl + 128 * (i + 1)],
                                    k.ident[:, :]).then_inc(k.s_pe2.h, 1)
        self.on("tensor", pe_body)

        def sc_body(sc):
            for tg in range(NTG):
                kk, r = tg // 4, tg % 4
                tok = r * 2048 + 512 * kk
                def evac_v(hl):
                    bank = hl
                    vsl = 4096 + 512 * (hl % 2)
                    sc.wait_ge(k.s_pe.h, mm_done[tg][8 + hl])
                    if tg * 4 + hl >= 2:
                        pidx = tg * 4 + hl - 2
                        pt, ph = divmod(pidx, 4)
                        sc.wait_ge(k.s_pe2.h, vtp_done[pt][ph])
                    sc.copy(k.STAGE[:, vsl:vsl + 512],
                            k.ps[bank][:, :]).then_inc(k.s_act2.h, 1)
                for out in range(min(NOUT, 8)):
                    wi, hl = divmod(out, 4)
                    bank = out % 4
                    sc.wait_ge(k.s_pe.h, mm_done[tg][out])
                    sl = 512 * (out % 4)
                    if tg * 8 + out >= 4:
                        pidx = tg * 8 + out - 4
                        pt, po = divmod(pidx, 8)
                        sc.wait_ge(k.st_s[out % 4].h, st_qk[pt][po])
                    sc.copy(k.STAGE[:, sl:sl + 512],
                            k.ps[bank][:, :]).then_inc(k.s_act2.h, 1)
                    sc.wait_ge(k.s_act2.h, evac_done[tg][out])
                    dstt = (k.d_q if wi == 0 else k.d_k)[hl]
                    sc.dma_start(dstt[:, tok:tok + 512],
                                 k.STAGE[:, sl:sl + 512]
                                 ).then_inc(k.st_s[out % 4].h, 16)
                if NOUT > 8 and not NOV:
                    evac_v(0)
                    evac_v(1)
                for hl in range(0 if (NOV or NOUT <= 8) else 4):
                    sc.wait_ge(k.s_pe2.h, vtp_done[tg][hl])
                    osl = 6144 + 512 * (hl % 2)
                    if tg * 4 + hl >= 2:
                        pidx = tg * 4 + hl - 2
                        pt, ph = divmod(pidx, 4)
                        sc.wait_ge(k.st_s[4 + hl % 2].h, st_v[pt][ph])
                    sc.copy(k.STAGE[:, osl:osl + 512],
                            k.psT[:, :]).then_inc(k.s_act3.h, 1)
                    if hl + 2 <= 3:
                        evac_v(hl + 2)
                    if NOVST:
                        sc.dma_start(k.d_v[hl][tok:tok + 512, :],
                                     k.STAGE[:, osl:osl + 512].rearrange(
                                         "p (i c) -> (i p) c", i=4)
                                     ).then_inc(k.st_s[4 + hl % 2].h, 16) if False else None
                        sc.nop().then_inc(k.st_s[4 + hl % 2].h, 16)
                    else:
                        sc.wait_ge(k.s_act3.h, vte_done[tg][hl])
                        dst = k.d_v[hl][tok:tok + 512, :].rearrange(
                            "(i p) c -> p i c", p=128)
                        srcv = k.STAGE[:, osl:osl + 512].rearrange(
                            "p (i c) -> p i c", i=4)
                        sc.dma_start(dst, srcv).then_inc(k.st_s[4 + hl % 2].h, 16)
        self.on("scalar", sc_body)
        self.a2_mm_tot = (k.s_pe.h, k.s_pe.cnt)

    def phase_A3(self):
        k = self
        steps_all = []
        for gg in range(16):
            for tt in range(4 * gg + 4):
                steps_all.append((gg, tt))
        sidx = {s: i for i, s in enumerate(steps_all)}
        NS = len(steps_all)
        SB = [0, 1, 5]
        CB = [4, 2]

        ld_sem = lambda hh: (k.s_la if hh % 2 == 0 else k.s_lb)
        qkv_ld = [0] * MYH
        smm, expd, maskd, ladd, ctxd = {}, {}, {}, {}, {}
        lsum_d, recip_d, bcast_d, bccp_d, norm_d, stout_d = {}, {}, {}, {}, {}, {}
        head_done = [0] * MYH
        NG = MYH * 16

        def walk():
            for hh in range(MYH):
                ld_sem(hh).cnt += 16 * 3
                qkv_ld[hh] = ld_sem(hh).cnt
                for g in range(16):
                    gi = hh * 16 + g
                    K = 4 * g + 4
                    for kt in range(K):
                        k.s_pe.cnt += 1
                        smm[(hh, g, kt)] = k.s_pe.cnt
                        if kt == 0 and gi >= 1:
                            k.s_pe2.cnt += 1
                            bcast_d[gi - 1] = k.s_pe2.cnt
                        k.s_act.cnt += 1
                        expd[(hh, g, kt)] = k.s_act.cnt
                        if kt == 1 and gi >= 1:
                            k.s_act2.cnt += 1
                            bccp_d[gi - 1] = k.s_act2.cnt
                        if kt == min(5, K - 1) and gi >= 1:
                            k.st_s[6].cnt += 16
                            stout_d[gi - 1] = k.st_s[6].cnt
                        if kt >= 4 * g:
                            k.s_dve.cnt += 1
                            maskd[(hh, g, kt)] = k.s_dve.cnt
                        if kt % 2 == 1:
                            k.s_dve.cnt += 1
                            ladd[(hh, g, kt)] = k.s_dve.cnt
                            ladd[(hh, g, kt - 1)] = k.s_dve.cnt
                        k.s_pe2.cnt += 1
                        ctxd[(hh, g, kt)] = k.s_pe2.cnt
                    k.s_pe2.cnt += 1
                    lsum_d[gi] = k.s_pe2.cnt
                    k.s_dve.cnt += 1
                    recip_d[gi] = k.s_dve.cnt
                    k.s_dve.cnt += 1
                    norm_d[gi] = k.s_dve.cnt
                head_done[hh] = k.s_pe2.cnt
            k.s_pe2.cnt += 1
            bcast_d[NG - 1] = k.s_pe2.cnt
            k.s_act2.cnt += 1
            bccp_d[NG - 1] = k.s_act2.cnt
            k.st_s[6].cnt += 16
            stout_d[NG - 1] = k.st_s[6].cnt
        walk()
        a2_gate = self.a2_st_tots
        ctx_st_tot = k.st_s[6].cnt

        def prev_step(hh, g, kt, back):
            i = sidx[(g, kt)]
            if i >= back:
                pg, pk = steps_all[i - back]
                return (hh, pg, pk)
            if hh > 0:
                pg, pk = steps_all[NS - back + i]
                return (hh - 1, pg, pk)
            return None

        def pt_ready(hh, g, kt):
            if kt >= 4 * g:
                return (k.s_dve.h, maskd[(hh, g, kt)])
            return (k.s_act.h, expd[(hh, g, kt)])

        def sy_body(sy):
            for hh, cc in a2_gate:
                sy.wait_ge(hh, cc)
            sy.wait_ge(self.a2_mm_tot[0], self.a2_mm_tot[1])
            for hh in range(MYH):
                BIG = k.BIGA if hh % 2 == 0 else k.BIGB
                if hh >= 2:
                    sy.wait_ge(k.s_pe2.h, head_done[hh - 2])
                sy.dma_start(BIG[:, 0:8192], k.d_q[hh][:, :]
                             ).then_inc(ld_sem(hh).h, 16)
                sy.dma_start(BIG[:, 8192:16384], k.d_k[hh][:, :]
                             ).then_inc(ld_sem(hh).h, 16)
                sy.dma_start(
                    BIG[:, 16384:24576].rearrange("p (kt c) -> p kt c", c=128),
                    k.d_v[hh].rearrange("(kt p) c -> p kt c", p=128),
                ).then_inc(ld_sem(hh).h, 16)
        self.on("sync", sy_body)

        def emit_bcast(t, gi):
            t.wait_ge(k.s_dve.h, recip_d[gi])
            t.matmul(k.ps[3][:, :], k.ones_fr[:, :], k.recip_row[:, :],
                     start=True, stop=True, skip_group_check=True
                     ).then_inc(k.s_pe2.h, 1)

        def pe_body(t):
            for hh in range(MYH):
                BIG = k.BIGA if hh % 2 == 0 else k.BIGB
                t.wait_ge(ld_sem(hh).h, qkv_ld[hh])
                for g in range(16):
                    gi = hh * 16 + g
                    K = 4 * g + 4
                    for step in range(K + 1):
                        if step < K:
                            kt = step
                            ps3b = prev_step(hh, g, kt, 3)
                            if ps3b is not None:
                                t.wait_ge(k.s_act.h, expd[ps3b])
                            t.matmul(k.ps[SB[kt % 3]][:, :],
                                     BIG[:, 8192 + 128 * kt:8192 + 128 * (kt + 1)],
                                     BIG[:, 512 * g:512 * (g + 1)],
                                     start=True, stop=True,
                                     skip_group_check=True
                                     ).then_inc(k.s_pe.h, 1)
                            if kt == 0 and gi >= 1:
                                emit_bcast(t, gi - 1)
                        if step >= 1:
                            kt = step - 1
                            sh, cn = pt_ready(hh, g, kt)
                            t.wait_ge(sh, cn)
                            if kt == 0 and gi >= 2:
                                t.wait_ge(k.s_dve.h, norm_d[gi - 2])
                            t.matmul(k.ps[CB[gi % 2]][:, :],
                                     BIG[:, 16384 + 128 * kt:16384 + 128 * (kt + 1)],
                                     k.STAGE[:, 512 * (kt % 4):512 * (kt % 4) + 512],
                                     start=(kt == 0), stop=(kt == K - 1),
                                     skip_group_check=True
                                     ).then_inc(k.s_pe2.h, 1)
                    t.wait_ge(k.s_dve.h, ladd[(hh, g, K - 1)])
                    if gi >= 1:
                        t.wait_ge(k.s_act2.h, bccp_d[gi - 1])
                    t.matmul(k.ps[3][0:1, :], k.ones_fc[:, :],
                             k.evf[:, 0:512],
                             start=True, stop=False, skip_group_check=True)
                    t.matmul(k.ps[3][0:1, :], k.ones_fc[:, :],
                             k.evf[:, 512:1024],
                             start=False, stop=True, skip_group_check=True
                             ).then_inc(k.s_pe2.h, 1)
            emit_bcast(t, NG - 1)
        self.on("tensor", pe_body)

        def emit_bccp(sc, gi):
            sc.wait_ge(k.s_pe2.h, bcast_d[gi])
            if gi >= 1:
                sc.wait_ge(k.s_dve.h, norm_d[gi - 1])
            sc.copy(k.sbf1[:, 0:512], k.ps[3][:, :]).then_inc(k.s_act2.h, 1)

        def emit_store(sc, gi):
            hh, g = divmod(gi, 16)
            sc.wait_ge(k.s_dve.h, norm_d[gi])
            if gi >= 1:
                sc.wait_ge(k.st_s[6].h, stout_d[gi - 1])
            dst = k.d_ctx_own[hh][:, 512 * g:512 * (g + 1)]
            sc.dma_start(dst, k.STAGE[:, 2048 + 512 * (gi % 2):
                                      2560 + 512 * (gi % 2)]
                         ).then_inc(k.st_s[6].h, 16)

        def sc_body(sc):
            for hh, cc in a2_gate:
                sc.wait_ge(hh, cc)
            for hh in range(MYH):
                for g in range(16):
                    gi = hh * 16 + g
                    K = 4 * g + 4
                    for kt in range(K):
                        sc.wait_ge(k.s_pe.h, smm[(hh, g, kt)])
                        ps4b = prev_step(hh, g, kt, 4)
                        if ps4b is not None:
                            sc.wait_ge(k.s_pe2.h, ctxd[ps4b])
                            sc.wait_ge(k.s_dve.h, ladd[ps4b])
                        sc.activation(
                            k.STAGE[:, 512 * (kt % 4):512 * (kt % 4) + 512],
                            k.ps[SB[kt % 3]][:, :], AF.Exp,
                            scale=ISQ).then_inc(k.s_act.h, 1)
                        if kt == 1 and gi >= 1:
                            emit_bccp(sc, gi - 1)
                        if kt == min(5, K - 1) and gi >= 1:
                            emit_store(sc, gi - 1)
            emit_bccp(sc, NG - 1)
            emit_store(sc, NG - 1)
        self.on("scalar", sc_body)

        def ve_body(v):
            for hh, cc in a2_gate:
                v.wait_ge(hh, cc)
            for hh in range(MYH):
                for g in range(16):
                    gi = hh * 16 + g
                    K = 4 * g + 4
                    for kt in range(K):
                        v.wait_ge(k.s_act.h, expd[(hh, g, kt)])
                        sl = 512 * (kt % 4)
                        if kt >= 4 * g:
                            r = kt - 4 * g
                            v.tensor_mul(k.STAGE[:, sl:sl + 512],
                                         k.STAGE[:, sl:sl + 512],
                                         k.masks[:, 512 * r:512 * (r + 1)]
                                         ).then_inc(k.s_dve.h, 1)
                            v.wait_ge(k.s_dve.h, maskd[(hh, g, kt)])
                        if kt % 2 == 1:
                            psl = 512 * ((kt - 1) % 4)
                            if kt == 1:
                                if gi >= 1:
                                    v.wait_ge(k.s_pe2.h, lsum_d[gi - 1])
                                v.tensor_copy(k.evf[:, 0:1024],
                                              k.STAGE[:, psl:psl + 1024]
                                              ).then_inc(k.s_dve.h, 1)
                            else:
                                v.wait_ge(k.s_dve.h, ladd[(hh, g, kt - 2)])
                                v.scalar_tensor_tensor(
                                    k.evf[:, 0:1024],
                                    k.STAGE[:, psl:psl + 1024], 1.0,
                                    k.evf[:, 0:1024], ALU.mult, ALU.add
                                ).then_inc(k.s_dve.h, 1)
                    v.wait_ge(k.s_pe2.h, lsum_d[gi])
                    v.reciprocal(k.recip_row[:, :], k.ps[3][0:1, :]
                                 ).then_inc(k.s_dve.h, 1)
                    v.wait_ge(k.s_act2.h, bccp_d[gi])
                    v.wait_ge(k.s_dve.h, recip_d[gi])
                    if gi >= 2:
                        v.wait_ge(k.st_s[6].h, stout_d[gi - 2])
                    v.tensor_tensor(k.STAGE[:, 2048 + 512 * (gi % 2):
                                            2560 + 512 * (gi % 2)],
                                    k.ps[CB[gi % 2]][:, :],
                                    k.sbf1[:, 0:512], ALU.mult
                                    ).then_inc(k.s_dve.h, 1)
        self.on("vector", ve_body)

        cc_a2a = [self.new_cc() for _ in range(MYH)]
        self.cc_a2a = cc_a2a
        head_st = [stout_d[16 * hh + 15] for hh in range(MYH)]

        def g_body(g):
            for hh in range(MYH):
                g.wait_ge(k.st_s[6].h, head_st[hh])
                if hh >= 1:
                    g.wait_ge(cc_a2a[hh - 1].h, 1)
                g.collective_compute(
                    "AllGather", ALU.bypass, replica_groups=GROUPS,
                    ins=[k.d_ctx_own[hh].ap().opt()],
                    outs=[k.d_ctx_g[hh].ap().opt()],
                ).then_inc(cc_a2a[hh].h)
        self.on("gpsimd", g_body)
        self.flush("A2A3")

    def phase_A5(self):
        k = self
        gd_w = k.s_gd.cnt + 16 * 16
        k.s_gd.cnt = gd_w
        self.a5_gdw = gd_w

        def g_body(g):
            g.wait_ge(self.cc_a2a[0].h, 1)
            for dc in range(16):
                g.dma_start(k.BIGA[:, 2048 * dc:2048 * (dc + 1)],
                            k.woT[128 * dc:128 * (dc + 1), :]
                            ).then_inc(k.s_gd.h, 16)
        self.on("gpsimd", g_body)

        ld_sem = lambda ti: (k.s_la if ti % 2 == 0 else k.s_lb)
        e_ln = [[0] * 3 for _ in range(NT)]
        ld_ctx = [0] * NT
        ld_x = [0] * NT
        o_done = [[0] * 4 for _ in range(NT)]
        x1p_done = [[0] * 4 for _ in range(NT)]
        xbo_done = [0] * NT
        m = {}
        xln3_done = [0] * NT
        x1st_done = [0] * NT
        x3st_done = [0] * NT
        tp_done = [[0] * 4 for _ in range(NT)]
        tpe_done = [[0] * 4 for _ in range(NT)]
        st_sp = [[0] * 4 for _ in range(NT)]
        rt_done = [0] * NT
        act_last = [0] * NT

        for ti in range(NT):
            ld_sem(ti).cnt += 16 * 4
            ld_ctx[ti] = ld_sem(ti).cnt
            k.s_lx.cnt += 16
            ld_x[ti] = k.s_lx.cnt
            k.s_dve.cnt += 1
            xbo_done[ti] = k.s_dve.cnt
            for jg in range(4):
                k.s_pe.cnt += 1
                o_done[ti][jg] = k.s_pe.cnt
                k.s_dve.cnt += 1
                x1p_done[ti][jg] = k.s_dve.cnt
            k.st_s[7].cnt += 16
            x1st_done[ti] = k.st_s[7].cnt
            k.s_dve.cnt += 1
            m[("red2", ti)] = k.s_dve.cnt
            k.s_act.cnt += 2
            m[("ms2", ti)] = k.s_act.cnt
            k.s_dve.cnt += 3
            m[("var2", ti)] = k.s_dve.cnt
            k.s_act3.cnt += 1
            e_ln[ti][0] = k.s_act3.cnt
            k.s_act.cnt += 1
            m[("rstd2", ti)] = k.s_act.cnt
            k.s_dve.cnt += 1
            m[("xln2", ti)] = k.s_dve.cnt
            k.s_dve.cnt += 1
            m[("red3", ti)] = k.s_dve.cnt
            k.s_act.cnt += 2
            m[("ms3", ti)] = k.s_act.cnt
            k.s_dve.cnt += 3
            m[("var3", ti)] = k.s_dve.cnt
            k.s_act3.cnt += 1
            e_ln[ti][1] = k.s_act3.cnt
            k.s_act.cnt += 1
            m[("rstd3", ti)] = k.s_act.cnt
            k.s_dve.cnt += 3
            m[("x3f", ti)] = k.s_dve.cnt
            k.s_dve.cnt += 1
            xln3_done[ti] = k.s_dve.cnt
            k.st_s[5].cnt += 16
            x3st_done[ti] = k.st_s[5].cnt
            k.s_act.cnt += 1
            m[("rt_sq", ti)] = k.s_act.cnt
            k.s_act3.cnt += 1
            e_ln[ti][2] = k.s_act3.cnt
            k.s_act.cnt += 1
            m[("rt_act", ti)] = k.s_act.cnt
            act_last[ti] = k.s_act.cnt
            k.s_dve.cnt += 2
            rt_done[ti] = k.s_dve.cnt
            for q in range(4):
                k.s_pe2.cnt += 4
                tp_done[ti][q] = k.s_pe2.cnt
                k.s_act2.cnt += 1
                tpe_done[ti][q] = k.s_act2.cnt
                k.st_s[q].cnt += 16
                st_sp[ti][q] = k.st_s[q].cnt
        self.a5_sp_tots = [(k.st_s[q].h, k.st_s[q].cnt) for q in range(4)]
        self.a5_x3_tot = (k.st_s[5].h, k.st_s[5].cnt)
        self.a5_x1_tot = (k.st_s[7].h, k.st_s[7].cnt)

        def sy_body(sy):
            for cc in self.cc_a2a:
                sy.wait_ge(cc.h, 1)
            tok0 = (sy.partition_id() % 4) * 2048
            ctx3d = [k.d_ctx_g[hh].ap().rearrange("(r p) t -> p r t", p=128)
                     for hh in range(MYH)]
            for ti in range(NT):
                half = (ti % 2) * 2048
                if ti >= 2:
                    sy.wait_ge(k.s_pe.h, o_done[ti - 2][3])
                for hh in range(MYH):
                    srcv = ctx3d[hh][:, :, 128 * ti:][:, :, bass.ds(tok0, 128)]
                    dst = k.BIGB[:, half + 512 * hh:half + 512 * (hh + 1)
                                 ].rearrange("p (r t) -> p r t", t=128)
                    sy.dma_start(dst, srcv).then_inc(ld_sem(ti).h, 16)
                if ti >= 1:
                    sy.wait_ge(k.s_dve.h, xbo_done[ti - 1])
                    sy.wait_ge(k.s_act.h, m[("ms2", ti - 1)])
                sy.dma_start(k.sbx[:, :], k.x_home[128 * ti:128 * (ti + 1), :]
                             ).then_inc(k.s_lx.h, 16)
        self.on("sync", sy_body)

        def pe_body(t):
            t.wait_ge(k.s_gd.h, gd_w)
            for ti in range(NT):
                half = (ti % 2) * 2048
                t.wait_ge(ld_sem(ti).h, ld_ctx[ti])
                for jg in range(4):
                    if ti >= 1:
                        t.wait_ge(k.s_dve.h, x1p_done[ti - 1][jg])
                    for li in range(16):
                        hh4, r = divmod(li, 4)
                        mc = 4 * r + hh4
                        mm = t.matmul(
                            k.ps[jg][:, :],
                            k.BIGB[:, half + 128 * li:half + 128 * (li + 1)],
                            k.BIGA[:, 2048 * mc + 512 * jg:
                                   2048 * mc + 512 * (jg + 1)],
                            start=(li == 0), stop=(li == 15),
                            skip_group_check=True)
                        if li == 15:
                            mm.then_inc(k.s_pe.h, 1)
                t.wait_ge(k.s_dve.h, xln3_done[ti])
                for q in range(4):
                    gi = ti * 4 + q
                    if gi >= 1:
                        pti, pq = divmod(gi - 1, 4)
                        t.wait_ge(k.s_act2.h, tpe_done[pti][pq])
                    for i in range(4):
                        dc = 4 * q + i
                        t.transpose(
                            k.psT[:, 128 * i:128 * (i + 1)],
                            k.STAGE[:, 2048 + 128 * dc:2048 + 128 * (dc + 1)],
                            k.ident[:, :]).then_inc(k.s_pe2.h, 1)
        self.on("tensor", pe_body)

        def ve_body(v):
            for ti in range(NT):
                v.wait_ge(k.s_lx.h, ld_x[ti])
                if ti >= 1:
                    v.wait_ge(k.s_act.h, act_last[ti - 1])
                v.tensor_add(k.sbf3[:, :], k.sbx[:, :],
                             k.bc[:, 2 * D:3 * D]).then_inc(k.s_dve.h, 1)
                v.wait_ge(k.s_dve.h, xbo_done[ti])
                for jg in range(4):
                    v.wait_ge(k.s_pe.h, o_done[ti][jg])
                    if ti >= 1 and jg == 0:
                        v.wait_ge(k.st_s[7].h, x1st_done[ti - 1])
                    v.scalar_tensor_tensor(
                        k.sbf1[:, 512 * jg:512 * (jg + 1)],
                        k.ps[jg][:, :], 1.0,
                        k.sbf3[:, 512 * jg:512 * (jg + 1)],
                        ALU.mult, ALU.add).then_inc(k.s_dve.h, 1)
                v.wait_ge(k.s_dve.h, x1p_done[ti][3])
                v.reduce_sum(k.c_tmp[:, :], k.sbf1[:, :],
                             axis=AX.X).then_inc(k.s_dve.h, 1)
                v.wait_ge(k.s_act.h, m[("ms2", ti)])
                v.tensor_scalar_mul(k.c_var[:, :], k.c_ssq[:, :],
                                    1.0 / D).then_inc(k.s_dve.h, 1)
                v.tensor_mul(k.c_tmp2[:, :], k.c_mean[:, :], k.c_mean[:, :]
                             ).then_inc(k.s_dve.h, 1)
                v.wait_ge(k.s_dve.h, m[("var2", ti)] - 1)
                v.tensor_sub(k.c_var[:, :], k.c_var[:, :], k.c_tmp2[:, :]
                             ).then_inc(k.s_dve.h, 1)
                v.wait_ge(k.s_act.h, m[("rstd2", ti)])
                v.tensor_scalar(k.sbf2[:, :], k.sbf1[:, :], k.c_mean[:, :],
                                k.c_rstd[:, :], ALU.subtract, ALU.mult
                                ).then_inc(k.s_dve.h, 1)
                v.wait_ge(k.s_dve.h, m[("xln2", ti)])
                v.reduce_sum(k.c_tmp[:, :], k.sbf2[:, :],
                             axis=AX.X).then_inc(k.s_dve.h, 1)
                v.wait_ge(k.s_act.h, m[("ms3", ti)])
                v.tensor_scalar_mul(k.c_var[:, :], k.c_ssq[:, :],
                                    1.0 / D).then_inc(k.s_dve.h, 1)
                v.tensor_mul(k.c_tmp2[:, :], k.c_mean[:, :], k.c_mean[:, :]
                             ).then_inc(k.s_dve.h, 1)
                v.wait_ge(k.s_dve.h, m[("var3", ti)] - 1)
                v.tensor_sub(k.c_var[:, :], k.c_var[:, :], k.c_tmp2[:, :]
                             ).then_inc(k.s_dve.h, 1)
                v.wait_ge(k.s_act.h, m[("rstd3", ti)])
                v.tensor_scalar(k.sbf3[:, :], k.sbf2[:, :], k.c_mean[:, :],
                                k.c_rstd[:, :], ALU.subtract, ALU.mult
                                ).then_inc(k.s_dve.h, 1)
                v.wait_ge(k.s_dve.h, m[("x3f", ti)] - 2)
                v.tensor_mul(k.sbf2[:, :], k.sbf3[:, :], k.bc[:, 0:D]
                             ).then_inc(k.s_dve.h, 1)
                v.wait_ge(k.s_dve.h, m[("x3f", ti)] - 1)
                v.tensor_add(k.sbf3[:, :], k.sbf2[:, :], k.bc[:, D:2 * D]
                             ).then_inc(k.s_dve.h, 1)
                v.wait_ge(k.s_dve.h, m[("x3f", ti)])
                if ti >= 1:
                    v.wait_ge(k.s_pe2.h, tp_done[ti - 1][3])
                    v.wait_ge(k.st_s[5].h, x3st_done[ti - 1])
                v.tensor_copy(k.STAGE[:, 2048:4096], k.sbf3[:, :]
                              ).then_inc(k.s_dve.h, 1)
                v.wait_ge(k.s_act.h, m[("rt_act", ti)])
                v.tensor_scalar_min(k.c_rstd[:, :], k.c_rstd[:, :], 1e12
                                    ).then_inc(k.s_dve.h, 1)
                v.wait_ge(k.s_dve.h, rt_done[ti] - 1)
                v.tensor_mul(k.r_all[:, ti:ti + 1], k.c_rstd[:, :],
                             k.scale_col[:, :]).then_inc(k.s_dve.h, 1)
        self.on("vector", ve_body)

        def sc_body(sc):
            for ti in range(NT):
                if ti >= 1:
                    sc.wait_ge(k.s_dve.h, rt_done[ti - 1])
                sc.wait_ge(k.s_dve.h, x1p_done[ti][3])
                sc.dma_start(k.d_x1[128 * ti:128 * (ti + 1), :], k.sbf1[:, :]
                             ).then_inc(k.st_s[7].h, 16)
                sc.activation(k.STAGE[:, 0:2048], k.sbf1[:, :], AF.Square,
                              accum_out=k.c_ssq[:, :]).then_inc(k.s_act.h, 1)
                sc.wait_ge(k.s_dve.h, m[("red2", ti)])
                sc.mul(k.c_mean[:, :], k.c_tmp[:, :], 1.0 / D
                       ).then_inc(k.s_act.h, 1)
                sc.wait_ge(k.s_dve.h, m[("var2", ti)])
                sc.activation(k.c_rstd[:, :], k.c_var[:, :], AF.Ln,
                              bias=k.eps_ln[:, :]).then_inc(k.s_act3.h, 1)
                sc.wait_ge(k.s_act3.h, e_ln[ti][0])
                sc.activation(k.c_rstd[:, :], k.c_rstd[:, :], AF.Exp,
                              scale=-0.5).then_inc(k.s_act.h, 1)
                sc.wait_ge(k.s_dve.h, m[("xln2", ti)])
                sc.activation(k.STAGE[:, 0:2048], k.sbf2[:, :], AF.Square,
                              accum_out=k.c_ssq[:, :]).then_inc(k.s_act.h, 1)
                sc.wait_ge(k.s_dve.h, m[("red3", ti)])
                sc.mul(k.c_mean[:, :], k.c_tmp[:, :], 1.0 / D
                       ).then_inc(k.s_act.h, 1)
                sc.wait_ge(k.s_dve.h, m[("var3", ti)])
                sc.activation(k.c_rstd[:, :], k.c_var[:, :], AF.Ln,
                              bias=k.eps_ln[:, :]).then_inc(k.s_act3.h, 1)
                sc.wait_ge(k.s_act3.h, e_ln[ti][1])
                sc.activation(k.c_rstd[:, :], k.c_rstd[:, :], AF.Exp,
                              scale=-0.5).then_inc(k.s_act.h, 1)
                sc.wait_ge(k.s_dve.h, xln3_done[ti])
                sc.dma_start(k.d_xln3[128 * ti:128 * (ti + 1), :],
                             k.STAGE[:, 2048:4096]).then_inc(k.st_s[5].h, 16)
                sc.activation(k.STAGE[:, 0:2048], k.sbf3[:, :], AF.Square,
                              accum_out=k.c_ssq[:, :]).then_inc(k.s_act.h, 1)
                sc.wait_ge(k.s_act.h, m[("rt_sq", ti)])
                sc.activation(k.c_rstd[:, :], k.c_ssq[:, :], AF.Ln
                              ).then_inc(k.s_act3.h, 1)
                sc.wait_ge(k.s_act3.h, e_ln[ti][2])
                sc.activation(k.c_rstd[:, :], k.c_rstd[:, :], AF.Exp,
                              scale=-0.5).then_inc(k.s_act.h, 1)
                for q in range(4):
                    sc.wait_ge(k.s_pe2.h, tp_done[ti][q])
                    sl = 4096 + 512 * q
                    if ti >= 1:
                        sc.wait_ge(k.st_s[q].h, st_sp[ti - 1][q])
                    sc.copy(k.STAGE[:, sl:sl + 512], k.psT[:, :]
                            ).then_inc(k.s_act2.h, 1)
                    sc.wait_ge(k.s_act2.h, tpe_done[ti][q])
                    dst = k.d_xln3T[512 * q:512 * (q + 1),
                                    128 * ti:128 * (ti + 1)]
                    dst = dst.rearrange("(i p) t -> p i t", p=128)
                    srcv = k.STAGE[:, sl:sl + 512].rearrange(
                        "p (i t) -> p i t", i=4)
                    sc.dma_start(dst, srcv).then_inc(k.st_s[q].h, 16)
        self.on("scalar", sc_body)
        self.a5x = {
            "o15": o_done[15][3], "xbo15": xbo_done[15],
            "x1p15": x1p_done[15][3], "red2_15": m[("red2", 15)],
            "x1st15": x1st_done[15], "tp15": tp_done[15][3],
            "tpe15": tpe_done[15][3], "x3st15": x3st_done[15],
            "rt_done": list(rt_done)}

    def phase_F0(self):
        k = self
        ld = [0] * 16
        sq_done = [0] * 16
        cast_done = [0] * 16
        mm_done = [0] * 16
        for dc in range(16):
            k.s_lx.cnt += 16
            ld[dc] = k.s_lx.cnt
            k.s_act.cnt += 1
            sq_done[dc] = k.s_act.cnt
            k.s_dve.cnt += 1
            cast_done[dc] = k.s_dve.cnt
            k.s_pe.cnt += 1
            mm_done[dc] = k.s_pe.cnt
        k.s_act.cnt += 4
        ln4 = k.s_act.cnt
        k.s_act.cnt += 1
        rn_act = k.s_act.cnt
        k.s_dve.cnt += 1
        rn_min = k.s_dve.cnt
        pe_b, ac_b = [], []
        for j in range(4):
            k.s_pe.cnt += 1
            pe_b.append(k.s_pe.cnt)
            k.s_act.cnt += 1
            ac_b.append(k.s_act.cnt)

        def sy_body(sy):
            sy.wait_ge(k.s_dve.h, self.a5x["xbo15"])
            for dc in range(16):
                if dc >= 1:
                    sy.wait_ge(k.s_act.h, sq_done[dc - 1])
                    sy.wait_ge(k.s_dve.h, cast_done[dc - 1])
                sy.dma_start(k.sbx[:, :], k.mu2d[128 * dc:128 * (dc + 1), :]
                             ).then_inc(k.s_lx.h, 16)
        self.on("sync", sy_body)

        def sc_body(sc):
            sc.wait_ge(k.s_dve.h, self.a5x["red2_15"])
            sc.wait_ge(k.st_s[7].h, self.a5x["x1st15"])
            for dc in range(16):
                sc.wait_ge(k.s_lx.h, ld[dc])
                if dc >= 1:
                    sc.wait_ge(k.s_pe.h, mm_done[dc - 1])
                sc.activation(k.sbf1[:, :], k.sbx[:, :], AF.Square
                              ).then_inc(k.s_act.h, 1)
            sc.wait_ge(k.s_pe.h, mm_done[15])
            for j in range(4):
                sc.activation(k.g_r[0:1, 512 * j:512 * (j + 1)],
                              k.ps[2 + j][0:1, :], AF.Ln
                              ).then_inc(k.s_act.h, 1)
            sc.wait_ge(k.s_act.h, ln4)
            sc.activation(k.g_r[0:1, 0:D], k.g_r[0:1, 0:D], AF.Exp,
                          scale=-0.5).then_inc(k.s_act.h, 1)
            for j in range(4):
                sc.wait_ge(k.s_pe.h, pe_b[j])
                sc.copy(k.bc[:, 2 * D + 512 * j:2 * D + 512 * (j + 1)],
                        k.psW[:, :]).then_inc(k.s_act.h, 1)
        self.on("scalar", sc_body)

        def ve_body(v):
            v.wait_ge(k.s_pe.h, self.a5x["o15"])
            for dc in range(16):
                v.wait_ge(k.s_lx.h, ld[dc])
                v.tensor_copy(k.BIGA[:, 2048 * dc:2048 * (dc + 1)], k.sbx[:, :]
                              ).then_inc(k.s_dve.h, 1)
            v.wait_ge(k.s_act.h, rn_act)
            v.tensor_scalar_min(k.g_r[0:1, 0:D], k.g_r[0:1, 0:D], 1e12
                                ).then_inc(k.s_dve.h, 1)
        self.on("vector", ve_body)

        def pe_body(t):
            t.wait_ge(k.s_dve.h, self.a5x["x1p15"])
            for dc in range(16):
                t.wait_ge(k.s_act.h, sq_done[dc])
                for j in range(4):
                    mm = t.matmul(k.ps[2 + j][0:1, :], k.ones_fc[:, :],
                                  k.sbf1[:, 512 * j:512 * (j + 1)],
                                  start=(dc == 0), stop=(dc == 15),
                                  skip_group_check=True)
                    if j == 3:
                        mm.then_inc(k.s_pe.h, 1)
            t.wait_ge(k.s_dve.h, rn_min)
            for j in range(4):
                if j > 0:
                    t.wait_ge(k.s_act.h, ac_b[j - 1])
                t.matmul(k.psW[:, :], k.ones_fr[:, :],
                         k.g_r[0:1, 512 * j:512 * (j + 1)],
                         start=True, stop=True, skip_group_check=True
                         ).then_inc(k.s_pe.h, 1)
        self.on("tensor", pe_body)
        self.f0x = {"mm15": mm_done[15], "rnact": rn_act, "acb3": ac_b[3],
                    "cast15": cast_done[15]}

    def phase_F2(self):
        k = self
        ld_sem = lambda ti: (k.s_la if ti % 2 == 0 else k.s_lb)
        ld = [0] * NT
        mm_done = [[0] * 4 for _ in range(NT)]
        stt_done = [[0] * 4 for _ in range(NT)]
        exp_done = [[0] * 4 for _ in range(NT)]
        tp_done = [[0] * 4 for _ in range(NT)]
        tpe_done = [[0] * 4 for _ in range(NT)]
        st_p2 = [[0] * 4 for _ in range(NT)]
        st_p2t = [[0] * 4 for _ in range(NT)]
        for ti in range(NT):
            ld_sem(ti).cnt += 16 * 16
            ld[ti] = ld_sem(ti).cnt
            for eg in range(4):
                k.s_pe.cnt += 1
                mm_done[ti][eg] = k.s_pe.cnt
                k.s_dve.cnt += 1
                stt_done[ti][eg] = k.s_dve.cnt
                k.s_act.cnt += 1
                exp_done[ti][eg] = k.s_act.cnt
                k.st_s[eg].cnt += 16
                st_p2[ti][eg] = k.st_s[eg].cnt
                k.s_pe2.cnt += 4
                tp_done[ti][eg] = k.s_pe2.cnt
                k.s_act2.cnt += 1
                tpe_done[ti][eg] = k.s_act2.cnt
                k.st_s[4 + eg % 2].cnt += 16
                st_p2t[ti][eg] = k.st_s[4 + eg % 2].cnt
        gate = self.a5_sp_tots
        self.f2_p2_tots = [(k.st_s[i].h, k.st_s[i].cnt) for i in range(4)]
        self.f2_p2t_tots = [(k.st_s[4 + i].h, k.st_s[4 + i].cnt) for i in range(2)]

        def sy_body(sy):
            for hh, cc in gate:
                sy.wait_ge(hh, cc)
            sy.wait_ge(k.s_pe.h, self.a5x["o15"])
            for ti in range(NT):
                half = (ti % 2) * 2048
                if ti >= 2:
                    sy.wait_ge(k.s_pe.h, mm_done[ti - 2][3])
                for dc in range(16):
                    srcv = k.d_xln3T[128 * dc:128 * (dc + 1),
                                     128 * ti:128 * (ti + 1)]
                    sy.dma_start(k.BIGB[:, half + 128 * dc:half + 128 * (dc + 1)],
                                 srcv).then_inc(ld_sem(ti).h, 16)
        self.on("sync", sy_body)

        def pe_body(t):
            t.wait_ge(k.s_act.h, self.f0x["rnact"])
            t.wait_ge(k.s_act2.h, self.a5x["tpe15"])
            t.wait_ge(k.s_dve.h, self.f0x["cast15"])
            for ti in range(NT):
                half = (ti % 2) * 2048
                t.wait_ge(ld_sem(ti).h, ld[ti])
                if ti >= 1:
                    t.wait_ge(k.s_dve.h, stt_done[ti - 1][3])
                for dc in range(16):
                    for eg in range(4):
                        mm = t.matmul(k.ps[eg][:, :],
                                      k.BIGB[:, half + 128 * dc:
                                             half + 128 * (dc + 1)],
                                      k.BIGA[:, 2048 * dc + 512 * eg:
                                             2048 * dc + 512 * (eg + 1)],
                                      start=(dc == 0), stop=(dc == 15),
                                      skip_group_check=True)
                        if dc == 15:
                            mm.then_inc(k.s_pe.h, 1)
                for eg in range(4):
                    t.wait_ge(k.s_act.h, exp_done[ti][eg])
                    gi = ti * 4 + eg
                    if gi >= 1:
                        pti, pq = divmod(gi - 1, 4)
                        t.wait_ge(k.s_act2.h, tpe_done[pti][pq])
                    sl = 512 * eg
                    for i in range(4):
                        t.transpose(k.psT[:, 128 * i:128 * (i + 1)],
                                    k.STAGE[:, sl + 128 * i:sl + 128 * (i + 1)],
                                    k.ident[:, :]).then_inc(k.s_pe2.h, 1)
        self.on("tensor", pe_body)

        def ve_body(v):
            v.wait_ge(k.s_pe.h, self.f0x["mm15"])
            v.wait_ge(k.s_act.h, self.f0x["acb3"])
            for ti in range(NT):
                v.wait_ge(k.s_dve.h, self.a5x["rt_done"][ti])
                for eg in range(4):
                    v.wait_ge(k.s_pe.h, mm_done[ti][eg])
                    if ti >= 1:
                        v.wait_ge(k.s_act.h, exp_done[ti - 1][eg])
                    v.scalar_tensor_tensor(
                        k.sbf1[:, 512 * eg:512 * (eg + 1)], k.ps[eg][:, :],
                        k.r_all[:, ti:ti + 1],
                        k.bc[:, 2 * D + 512 * eg:2 * D + 512 * (eg + 1)],
                        ALU.mult, ALU.mult).then_inc(k.s_dve.h, 1)
        self.on("vector", ve_body)

        def sc_body(sc):
            sc.wait_ge(k.s_pe2.h, self.a5x["tp15"])
            sc.wait_ge(k.st_s[5].h, self.a5x["x3st15"])
            for ti in range(NT):
                for eg in range(4):
                    sc.wait_ge(k.s_dve.h, stt_done[ti][eg])
                    sl = 512 * eg
                    if ti >= 1:
                        sc.wait_ge(k.s_pe2.h, tp_done[ti - 1][eg])
                        sc.wait_ge(k.st_s[eg].h, st_p2[ti - 1][eg])
                    sc.activation(k.STAGE[:, sl:sl + 512],
                                  k.sbf1[:, 512 * eg:512 * (eg + 1)],
                                  AF.Exp).then_inc(k.s_act.h, 1)
                    sc.wait_ge(k.s_act.h, exp_done[ti][eg])
                    sc.dma_start(
                        k.d_P2[128 * ti:128 * (ti + 1),
                               512 * eg:512 * (eg + 1)],
                        k.STAGE[:, sl:sl + 512]).then_inc(k.st_s[eg].h, 16)
                for eg in range(4):
                    sc.wait_ge(k.s_pe2.h, tp_done[ti][eg])
                    sl = 2048 + 512 * (eg % 2)
                    gi = ti * 4 + eg
                    if gi >= 2:
                        pti, pq = divmod(gi - 2, 4)
                        sc.wait_ge(k.st_s[4 + eg % 2].h, st_p2t[pti][pq])
                    sc.copy(k.STAGE[:, sl:sl + 512], k.psT[:, :]
                            ).then_inc(k.s_act2.h, 1)
                    sc.wait_ge(k.s_act2.h, tpe_done[ti][eg])
                    dst = k.d_P2T[512 * eg:512 * (eg + 1),
                                  128 * ti:128 * (ti + 1)]
                    dst = dst.rearrange("(i p) t -> p i t", p=128)
                    srcv = k.STAGE[:, sl:sl + 512].rearrange(
                        "p (i t) -> p i t", i=4)
                    sc.dma_start(dst, srcv).then_inc(k.st_s[4 + eg % 2].h, 16)
        self.on("scalar", sc_body)
        self.f2x = {"mm15": mm_done[15][3],
                    "stt15": stt_done[15][3]}

    def phase_F3(self):
        k = self
        ld_sem = lambda ec: (k.s_la if ec % 2 == 0 else k.s_lb)
        k.s_lc.cnt += 16 * NT
        lda_done = k.s_lc.cnt
        ORD = [4 * j + el for el in range(4) for j in range(4)]
        ld = [0] * 16
        mm_done = [0] * 16
        ev_done = [0] * 16
        st_done = [0] * 16
        for i in range(16):
            ld_sem(i).cnt += 16 * 16
            ld[i] = ld_sem(i).cnt
            k.s_pe.cnt += 1
            mm_done[i] = k.s_pe.cnt
            k.s_act.cnt += 5
            ev_done[i] = k.s_act.cnt
            k.st_s[6].cnt += 16
            st_done[i] = k.st_s[6].cnt
        gate = self.f2_p2_tots + [self.a5_x3_tot]
        rs_st = [st_done[4 * el + 3] for el in range(4)]

        def sy_body(sy):
            for hh, cc in gate:
                sy.wait_ge(hh, cc)
            sy.wait_ge(k.s_pe.h, self.f2x["mm15"])
            for tc in range(NT):
                sy.dma_start(k.BIGA[:, 2048 * tc:2048 * (tc + 1)],
                             k.d_xln3[128 * tc:128 * (tc + 1), :]
                             ).then_inc(k.s_lc.h, 16)
            for i in range(16):
                ec = ORD[i]
                half = (i % 2) * 2048
                if i >= 2:
                    sy.wait_ge(k.s_pe.h, mm_done[i - 2])
                for tc in range(16):
                    srcv = k.d_P2[128 * tc:128 * (tc + 1),
                                  128 * ec:128 * (ec + 1)]
                    sy.dma_start(k.BIGB[:, half + 128 * tc:half + 128 * (tc + 1)],
                                 srcv).then_inc(ld_sem(i).h, 16)
        self.on("sync", sy_body)

        def pe_body(t):
            t.wait_ge(k.s_dve.h, self.f2x["stt15"])
            t.wait_ge(k.s_act.h, self.f0x["acb3"])
            t.wait_ge(k.s_lc.h, lda_done)
            for i in range(16):
                ec = ORD[i]
                half = (i % 2) * 2048
                t.wait_ge(ld_sem(i).h, ld[i])
                if i >= 1:
                    t.wait_ge(k.s_act.h, ev_done[i - 1])
                for tc in range(16):
                    for dg in range(4):
                        t.matmul(k.ps[dg][:, :],
                                 k.BIGB[:, half + 128 * tc:half + 128 * (tc + 1)],
                                 k.BIGA[:, 2048 * tc + 512 * dg:
                                        2048 * tc + 512 * (dg + 1)],
                                 start=(tc == 0), stop=(tc == 15),
                                 skip_group_check=True)
                    mm = t.matmul(k.psW[:, 0:4],
                                  k.BIGB[:, half + 128 * tc:half + 128 * (tc + 1)],
                                  k.ones4[:, :],
                                  start=(tc == 0), stop=(tc == 15),
                                  skip_group_check=True)
                    if tc == 15:
                        mm.then_inc(k.s_pe.h, 1)
        self.on("tensor", pe_body)

        def sc_body(sc):
            for i in range(16):
                ec = ORD[i]
                el, j = ec % 4, ec // 4
                sc.wait_ge(k.s_pe.h, mm_done[i])
                if i >= 1:
                    sc.wait_ge(k.st_s[6].h, st_done[i - 1])
                for dg in range(4):
                    sc.copy(k.evf[:, 512 * dg:512 * (dg + 1)],
                            k.ps[dg][:, :]).then_inc(k.s_act.h, 1)
                sc.copy(k.evf[:, 2048:2052], k.psW[:, 0:4]
                        ).then_inc(k.s_act.h, 1)
                sc.wait_ge(k.s_act.h, ev_done[i])
                sc.dma_start(k.d_U_p[el][128 * j:128 * (j + 1), :],
                             k.evf[:, :]).then_inc(k.st_s[6].h, 16)
        self.on("scalar", sc_body)

        cc_rs = [self.new_cc() for _ in range(4)]
        self.cc_rs = cc_rs

        def g_body(g):
            g.wait_ge(k.s_gd.h, self.a5_gdw)
            for dc in range(16):
                g.dma_start(k.BIGB[:, 8192 + 512 * dc:8192 + 512 * (dc + 1)],
                            k.w1_my[0, 128 * dc:128 * (dc + 1), :]
                            ).then_inc(k.s_gd.h, 16)
            for hc in range(4):
                g.dma_start(
                    k.BIGB[:, 16384 + 2048 * hc:16384 + 2048 * (hc + 1)],
                    k.w2_my[0, 128 * hc:128 * (hc + 1), :]
                ).then_inc(k.s_wd.h, 16)
            for el in range(4):
                g.wait_ge(k.st_s[6].h, rs_st[el])
                if el >= 1:
                    g.wait_ge(cc_rs[el - 1].h, 1)
                g.collective_compute(
                    "ReduceScatter", ALU.add, replica_groups=GROUPS,
                    ins=[k.d_U_p[el].ap().opt()],
                    outs=[k.d_U_m[el].ap().opt()],
                ).then_inc(cc_rs[el].h)
        self.on("gpsimd", g_body)
        self.flush("A5F3")

    def phase_F4(self):
        k = self
        uld = [0] * 4
        b1ld = [0] * 4
        b2ld = [0] * 4
        w1ld = [0] * 4
        w2ld = [0] * 4
        recz = [0] * 4
        castu = [0] * 4
        tpu = [[0] * 4 for _ in range(4)]
        tpue = [[0] * 4 for _ in range(4)]
        b1b = [0] * 4
        b1bc = [0] * 4
        b2b = [[0] * 4 for _ in range(4)]
        b2bc = [[0] * 4 for _ in range(4)]
        h1m = [0] * 4
        hpre = [0] * 4
        hglu = [0] * 4
        tph = [0] * 4
        tphe = [0] * 4
        som = [[0] * 4 for _ in range(4)]
        sod = [[0] * 4 for _ in range(4)]
        stso = [0] * 4
        for el in range(4):
            k.s_lx.cnt += 16
            uld[el] = k.s_lx.cnt
            k.s_lr.cnt += 16
            b1ld[el] = k.s_lr.cnt
            k.s_lr.cnt += 16
            b2ld[el] = k.s_lr.cnt
            k.s_gd.cnt += 16 * 16
            w1ld[el] = k.s_gd.cnt
            k.s_wd.cnt += 16 * 4
            w2ld[el] = k.s_wd.cnt
            k.s_dve.cnt += 1
            recz[el] = k.s_dve.cnt
            k.s_dve.cnt += 1
            castu[el] = k.s_dve.cnt
            for q in range(4):
                k.s_pe.cnt += 4
                tpu[el][q] = k.s_pe.cnt
                k.s_act.cnt += 1
                tpue[el][q] = k.s_act.cnt
            k.s_pe.cnt += 1
            b1b[el] = k.s_pe.cnt
            k.s_act.cnt += 1
            b1bc[el] = k.s_act.cnt
            for j in range(4):
                k.s_pe.cnt += 1
                b2b[el][j] = k.s_pe.cnt
                k.s_act.cnt += 1
                b2bc[el][j] = k.s_act.cnt
            k.s_pe.cnt += 1
            h1m[el] = k.s_pe.cnt
            k.s_dve.cnt += 1
            hpre[el] = k.s_dve.cnt
            k.s_act.cnt += 1
            hglu[el] = k.s_act.cnt
            k.s_pe.cnt += 4
            tph[el] = k.s_pe.cnt
            k.s_act.cnt += 1
            tphe[el] = k.s_act.cnt
            for dg in range(4):
                k.s_pe.cnt += 1
                som[el][dg] = k.s_pe.cnt
                k.s_dve.cnt += 1
                sod[el][dg] = k.s_dve.cnt
            k.st_s[7].cnt += 16
            stso[el] = k.st_s[7].cnt
        slot_st_tot = k.st_s[7].cnt

        def sy_body(sy):
            for el in range(4):
                sy.wait_ge(self.cc_rs[el].h, 1)
                if el >= 1:
                    sy.wait_ge(k.s_dve.h, castu[el - 1])
                sy.dma_start(k.evf[:, :],
                             k.d_U_m[el][:, :]
                             ).then_inc(k.s_lx.h, 16)
                if el >= 1:
                    sy.wait_ge(k.s_pe.h, b2b[el - 1][3])
                sy.dma_start(k.g_r[0:1, 0:HFF],
                             k.b1_my[el:el + 1, :]).then_inc(k.s_lr.h, 16)
                sy.wait_ge(k.s_pe.h, b1b[el])
                sy.dma_start(k.g_r[0:1, 0:D],
                             k.b2_my[el:el + 1, :]).then_inc(k.s_lr.h, 16)
        self.on("sync", sy_body)

        def wload(g, el):
            if el >= 1:
                g.wait_ge(k.s_pe.h, h1m[el - 1])
            for dc in range(16):
                g.dma_start(k.BIGB[:, 8192 + 512 * dc:8192 + 512 * (dc + 1)],
                            k.w1_my[el, 128 * dc:128 * (dc + 1), :]
                            ).then_inc(k.s_gd.h, 16)
            if el >= 1:
                g.wait_ge(k.s_pe.h, som[el - 1][3])
            for hc in range(4):
                g.dma_start(
                    k.BIGB[:, 16384 + 2048 * hc:16384 + 2048 * (hc + 1)],
                    k.w2_my[el, 128 * hc:128 * (hc + 1), :]
                ).then_inc(k.s_wd.h, 16)
        self.f4_wload = wload

        def ve_body(v):
            for hh, cc in self.f2_p2_tots:
                v.wait_ge(hh, cc)
            for el in range(4):
                v.wait_ge(k.s_lx.h, uld[el])
                v.reciprocal(k.c_tmp[:, :], k.evf[:, 2048:2049]
                             ).then_inc(k.s_dve.h, 1)
                if el >= 1:
                    v.wait_ge(k.s_pe.h, tpu[el - 1][3])
                v.tensor_copy(k.STAGE[:, 0:2048], k.evf[:, 0:2048]
                              ).then_inc(k.s_dve.h, 1)
                v.wait_ge(k.s_pe.h, h1m[el])
                v.wait_ge(k.s_act.h, b1bc[el])
                v.wait_ge(k.s_dve.h, recz[el])
                if el >= 1:
                    v.wait_ge(k.s_act.h, hglu[el - 1])
                v.scalar_tensor_tensor(k.sbf1[:, 0:512], k.ps[0][:, :],
                                       k.c_tmp[:, :], k.bc[:, 0:512],
                                       ALU.mult, ALU.add).then_inc(k.s_dve.h, 1)
                for dg in range(4):
                    v.wait_ge(k.s_pe.h, som[el][dg])
                    v.wait_ge(k.s_act.h, b2bc[el][dg])
                    if el >= 1 and dg == 0:
                        v.wait_ge(k.st_s[7].h, stso[el - 1])
                    v.tensor_add(
                        k.STAGE[:, 5120 + 512 * dg:5120 + 512 * (dg + 1)],
                        k.ps[1 + dg][:, :],
                        k.bc[:, D + 512 * dg:D + 512 * (dg + 1)]
                    ).then_inc(k.s_dve.h, 1)
        self.on("vector", ve_body)

        def pe_body(t):
            for el in range(4):
                t.wait_ge(k.s_dve.h, castu[el])
                for q in range(4):
                    gi = el * 4 + q
                    if gi >= 1:
                        pel, pq = divmod(gi - 1, 4)
                        t.wait_ge(k.s_act.h, tpue[pel][pq])
                    for i in range(4):
                        dc = 4 * q + i
                        t.transpose(
                            k.psT[:, 128 * i:128 * (i + 1)],
                            k.STAGE[:, 128 * dc:128 * (dc + 1)],
                            k.ident[:, :]).then_inc(k.s_pe.h, 1)
                t.wait_ge(k.s_lr.h, b1ld[el])
                if el >= 1:
                    t.wait_ge(k.s_act.h, b1bc[el - 1])
                    t.wait_ge(k.s_dve.h, hpre[el - 1])
                t.matmul(k.psW[:, 0:HFF], k.ones_fr[:, :], k.g_r[0:1, 0:HFF],
                         start=True, stop=True, skip_group_check=True
                         ).then_inc(k.s_pe.h, 1)
                t.wait_ge(k.s_lr.h, b2ld[el])
                for j in range(4):
                    if el + j >= 1:
                        prev = b2bc[el][j - 1] if j >= 1 else b2bc[el - 1][3]
                        t.wait_ge(k.s_act.h, prev)
                        if j == 0 and el >= 1:
                            t.wait_ge(k.s_dve.h, sod[el - 1][3])
                    t.matmul(k.ps[5][:, :], k.ones_fr[:, :],
                             k.g_r[0:1, 512 * j:512 * (j + 1)],
                             start=True, stop=True, skip_group_check=True
                             ).then_inc(k.s_pe.h, 1)
                t.wait_ge(k.s_act.h, tpue[el][3])
                t.wait_ge(k.s_gd.h, w1ld[el])
                if el >= 1:
                    t.wait_ge(k.s_dve.h, hpre[el - 1])
                for dc in range(16):
                    mm = t.matmul(
                        k.ps[0][:, :],
                        k.STAGE[:, 2048 + 128 * dc:2048 + 128 * (dc + 1)],
                        k.BIGB[:, 8192 + 512 * dc:8192 + 512 * (dc + 1)],
                        start=(dc == 0), stop=(dc == 15),
                        skip_group_check=True)
                    if dc == 15:
                        mm.then_inc(k.s_pe.h, 1)
                t.wait_ge(k.s_act.h, hglu[el])
                if el >= 1:
                    t.wait_ge(k.s_act.h, tphe[el - 1])
                for i in range(4):
                    t.transpose(k.psT[:, 128 * i:128 * (i + 1)],
                                k.STAGE[:, 4096 + 128 * i:4096 + 128 * (i + 1)],
                                k.ident[:, :]).then_inc(k.s_pe.h, 1)
                t.wait_ge(k.s_act.h, tphe[el])
                t.wait_ge(k.s_wd.h, w2ld[el])
                for dg in range(4):
                    if el >= 1:
                        t.wait_ge(k.s_dve.h, sod[el - 1][dg])
                    for hc in range(4):
                        mm = t.matmul(
                            k.ps[1 + dg][:, :],
                            k.STAGE[:, 4608 + 128 * hc:4608 + 128 * (hc + 1)],
                            k.BIGB[:, 16384 + 2048 * hc + 512 * dg:
                                   16384 + 2048 * hc + 512 * (dg + 1)],
                            start=(hc == 0), stop=(hc == 3),
                            skip_group_check=True)
                        if hc == 3:
                            mm.then_inc(k.s_pe.h, 1)
        self.on("tensor", pe_body)

        def sc_body(sc):
            for hh, cc in self.f2_p2t_tots:
                sc.wait_ge(hh, cc)
            for el in range(4):
                for q in range(4):
                    sc.wait_ge(k.s_pe.h, tpu[el][q])
                    sl = 2048 + 512 * q
                    sc.copy(k.STAGE[:, sl:sl + 512], k.psT[:, :]
                            ).then_inc(k.s_act.h, 1)
                sc.wait_ge(k.s_pe.h, b1b[el])
                if el >= 1:
                    sc.wait_ge(k.s_dve.h, hpre[el - 1])
                sc.copy(k.bc[:, 0:512], k.psW[:, 0:HFF]).then_inc(k.s_act.h, 1)
                for j in range(4):
                    sc.wait_ge(k.s_pe.h, b2b[el][j])
                    if el >= 1:
                        sc.wait_ge(k.s_dve.h, sod[el - 1][j])
                    sc.copy(k.bc[:, D + 512 * j:D + 512 * (j + 1)],
                            k.ps[5][:, :]).then_inc(k.s_act.h, 1)
                sc.wait_ge(k.s_dve.h, hpre[el])
                sc.activation(k.STAGE[:, 4096:4608], k.sbf1[:, 0:512],
                              AF.Gelu).then_inc(k.s_act.h, 1)
                sc.wait_ge(k.s_pe.h, tph[el])
                sc.copy(k.STAGE[:, 4608:5120], k.psT[:, :]
                        ).then_inc(k.s_act.h, 1)
                sc.wait_ge(k.s_dve.h, sod[el][3])
                sc.dma_start(k.d_sp[el][:, :],
                             k.STAGE[:, 5120:7168]).then_inc(k.st_s[7].h, 16)
        self.on("scalar", sc_body)

        cc_sg = [self.new_cc() for _ in range(4)]
        self.cc_sg = cc_sg

        def g_body2(g):
            for el in range(4):
                if el + 1 <= 3:
                    self.f4_wload(g, el + 1)
                if el == 0:
                    g.wait_ge(self.cc_rs[3].h, 1)
                g.wait_ge(k.st_s[7].h, stso[el])
                if el >= 1:
                    g.wait_ge(cc_sg[el - 1].h, 1)
                g.collective_compute(
                    "AllGather", ALU.bypass, replica_groups=GROUPS,
                    ins=[k.d_sp[el].ap().opt()],
                    outs=[k.d_sg[el].ap().opt()],
                ).then_inc(cc_sg[el].h)
        self.on("gpsimd", g_body2)
        self.f4x = {"tphe3": tphe[3], "sod33": sod[3][3],
                    "stso3": stso[3]}

    def phase_F5(self):
        k = self
        ld_sem = lambda ti: (k.s_la if ti % 2 == 0 else k.s_lb)
        k.s_lc.cnt += 16 * 16
        lda = k.s_lc.cnt
        ld = [0] * NT
        ldx = [0] * NT
        mm_done = [0] * NT
        rec_done = [0] * NT
        out_done = [[0] * 4 for _ in range(NT)]
        st_y = [0] * NT
        for ti in range(NT):
            ld_sem(ti).cnt += 16 * 16
            ld[ti] = ld_sem(ti).cnt
            k.s_lx.cnt += 16
            ldx[ti] = k.s_lx.cnt
            k.s_pe.cnt += 1
            mm_done[ti] = k.s_pe.cnt
            k.s_dve.cnt += 1
            rec_done[ti] = k.s_dve.cnt
            for dg in range(4):
                k.s_dve.cnt += 1
                out_done[ti][dg] = k.s_dve.cnt
            k.st_s[6].cnt += 16
            st_y[ti] = k.st_s[6].cnt
        gate = self.f2_p2t_tots + [self.a5_x1_tot]

        def sy_body(sy):
            for hh, cc in gate:
                sy.wait_ge(hh, cc)
            for el in range(4):
                sy.wait_ge(self.cc_sg[el].h, 1)
                for j in range(4):
                    ec = 4 * j + el
                    sy.dma_start(k.BIGA[:, 2048 * ec:2048 * (ec + 1)],
                                 k.d_sg[el][128 * j:128 * (j + 1), :]
                                 ).then_inc(k.s_lc.h, 16)
            for ti in range(NT):
                half = (ti % 2) * 2048
                if ti >= 2:
                    sy.wait_ge(k.s_pe.h, mm_done[ti - 2])
                for ec in range(16):
                    srcv = k.d_P2T[128 * ec:128 * (ec + 1),
                                   128 * ti:128 * (ti + 1)]
                    sy.dma_start(k.BIGB[:, half + 128 * ec:half + 128 * (ec + 1)],
                                 srcv).then_inc(ld_sem(ti).h, 16)
                if ti >= 1:
                    sy.wait_ge(k.s_dve.h, out_done[ti - 1][3])
                sy.dma_start(k.sbx[:, :], k.d_x1[128 * ti:128 * (ti + 1), :]
                             ).then_inc(k.s_lx.h, 16)
        self.on("sync", sy_body)

        def pe_body(t):
            t.wait_ge(k.s_act.h, self.f4x["tphe3"])
            t.wait_ge(k.s_dve.h, self.f4x["sod33"])
            t.wait_ge(k.s_lc.h, lda)
            for ti in range(NT):
                half = (ti % 2) * 2048
                t.wait_ge(ld_sem(ti).h, ld[ti])
                if ti >= 1:
                    t.wait_ge(k.s_dve.h, out_done[ti - 1][3])
                    t.wait_ge(k.s_dve.h, rec_done[ti - 1])
                for ec in range(16):
                    for dg in range(4):
                        t.matmul(k.ps[dg][:, :],
                                 k.BIGB[:, half + 128 * ec:half + 128 * (ec + 1)],
                                 k.BIGA[:, 2048 * ec + 512 * dg:
                                        2048 * ec + 512 * (dg + 1)],
                                 start=(ec == 0), stop=(ec == 15),
                                 skip_group_check=True)
                    mm = t.matmul(k.psW[:, 0:1],
                                  k.BIGB[:, half + 128 * ec:half + 128 * (ec + 1)],
                                  k.ones_bc[:, :],
                                  start=(ec == 0), stop=(ec == 15),
                                  skip_group_check=True)
                    if ec == 15:
                        mm.then_inc(k.s_pe.h, 1)
        self.on("tensor", pe_body)

        def ve_body(v):
            v.wait_ge(k.s_act.h, self.f4x["tphe3"])
            v.wait_ge(k.st_s[7].h, self.f4x["stso3"])
            for ti in range(NT):
                v.wait_ge(k.s_pe.h, mm_done[ti])
                v.wait_ge(k.s_lx.h, ldx[ti])
                v.reciprocal(k.c_tmp[:, :], k.psW[:, 0:1]
                             ).then_inc(k.s_dve.h, 1)
                if ti >= 1:
                    v.wait_ge(k.st_s[6].h, st_y[ti - 1])
                v.wait_ge(k.s_dve.h, rec_done[ti])
                for dg in range(4):
                    v.scalar_tensor_tensor(
                        k.sbf1[:, 512 * dg:512 * (dg + 1)], k.ps[dg][:, :],
                        k.c_tmp[:, :], k.sbx[:, 512 * dg:512 * (dg + 1)],
                        ALU.mult, ALU.add).then_inc(k.s_dve.h, 1)
        self.on("vector", ve_body)

        def sc_body(sc):
            for ti in range(NT):
                sc.wait_ge(k.s_dve.h, out_done[ti][3])
                sc.dma_start(k.y[128 * ti:128 * (ti + 1), :], k.sbf1[:, :]
                             ).then_inc(k.st_s[6].h, 16)
        self.on("scalar", sc_body)
        self.flush("F4F5")


def build_nc(stage=99, debug=False):
    return KB(stage=stage, debug=debug).build()


_NC_CACHE = {}


def _get_nc():
    if "nc" not in _NC_CACHE:
        _NC_CACHE["nc"] = build_nc()
    return _NC_CACHE["nc"]


def make_in_maps(inputs):
    ins = {kk: np.asarray(v) for kk, v in inputs.items()}
    x = ins["L0_attn_x"].astype(np.float32, copy=False)
    wq, wk, wv = ins["L0_attn_wq"], ins["L0_attn_wk"], ins["L0_attn_wv"]
    wo, bo = ins["L0_attn_wo"], ins["L0_attn_bo"]
    gamma, beta = ins["L0_ffn_gamma"], ins["L0_ffn_beta"]
    mu, scale = ins["L0_ffn_mu"], ins["L0_ffn_scale"]
    W1, b1 = ins["L0_ffn_W1"], ins["L0_ffn_b1"]
    W2, b2 = ins["L0_ffn_W2"], ins["L0_ffn_b2"]
    wqT = np.ascontiguousarray(wq.T)
    wkT = np.ascontiguousarray(wk.T)
    wvT = np.ascontiguousarray(wv.T)
    woT = np.ascontiguousarray(wo.T)
    mu2d = np.ascontiguousarray(mu.reshape(D, ES))
    in_maps = []
    for c in range(8):
        bg, j = divmod(c, 4)
        in_maps.append({
            "x_home": np.ascontiguousarray(x[bg, TH * j:TH * (j + 1), :]),
            "wq_my": np.ascontiguousarray(wqT[:, 512 * j:512 * (j + 1)]),
            "wk_my": np.ascontiguousarray(wkT[:, 512 * j:512 * (j + 1)]),
            "wv_my": np.ascontiguousarray(wvT[:, 512 * j:512 * (j + 1)]),
            "woT": woT,
            "bo": np.ascontiguousarray(bo.reshape(1, D)).astype(np.float32),
            "gamma": np.ascontiguousarray(gamma.reshape(1, D)).astype(np.float32),
            "beta": np.ascontiguousarray(beta.reshape(1, D)).astype(np.float32),
            "scale_in": np.ascontiguousarray(scale.reshape(1, 1)).astype(np.float32),
            "mu2d": mu2d,
            "w1_my": np.ascontiguousarray(W1[4 * j:4 * (j + 1)]),
            "b1_my": np.ascontiguousarray(b1[4 * j:4 * (j + 1)]),
            "w2_my": np.ascontiguousarray(W2[4 * j:4 * (j + 1)]),
            "b2_my": np.ascontiguousarray(b2[4 * j:4 * (j + 1)]),
        })
    return in_maps


def kernel(**inputs):
    nc = _get_nc()
    in_maps = make_in_maps(inputs)
    res = run_bass_kernel_spmd(nc, in_maps, core_ids=list(range(8)))
    out = np.empty((B, S, D), np.float32)
    for c in range(8):
        bg, j = divmod(c, 4)
        out[bg, TH * j:TH * (j + 1), :] = res.results[c]["y"]
    return out


# version 12 (final)
# speedup vs baseline: 1.3089x; 1.0451x over previous
import math
import os as _os
from contextlib import ExitStack

import numpy as np

import concourse.bass as bass
import concourse.mybir as mybir
from concourse.bass_utils import run_bass_kernel_spmd

F32 = mybir.dt.float32
BF16 = mybir.dt.bfloat16
AF = mybir.ActivationFunctionType
ALU = mybir.AluOpType
AX = mybir.AxisListType

B, S, D, H, HD = 2, 8192, 2048, 16, 128
GROUPS = [[0, 1, 2, 3], [4, 5, 6, 7]]
TH = 2048
NT = TH // 128
E, SLOT, HFF = 16, 128, 512
ES = E * SLOT
MYH, MYE = 4, 4
ISQ = 1.0 / math.sqrt(HD)
LN_EPS = 1e-5
UW = D + 4

ENGS = ("sync", "tensor", "scalar", "vector", "gpsimd")


class Sem:
    def __init__(self, h):
        self.h = h
        self.cnt = 0


class KB:
    def __init__(self, stage=99, debug=False):
        self.stage = stage
        self.debug = debug
        self.ops = {e: [] for e in ENGS}

    def on(self, eng_name, fn):
        self.ops[eng_name].append(fn)

    def flush(self, name=None):
        ops, self.ops = self.ops, {e: [] for e in ENGS}
        with self.nc.Block() as blk:
            for e in ENGS:
                if ops[e]:
                    def body(eng, _ops=tuple(ops[e])):
                        for f in _ops:
                            f(eng)
                    getattr(blk, e)(body)

    def new_cc(self):
        s = self.cc_sems[self.cc_used]
        self.cc_used += 1
        return s

    def build(self):
        nc = bass.Bass("TRN2", target_bir_lowering=False)
        self.nc = nc

        self.x_home = nc.dram_tensor("x_home", [TH, D], F32, kind="ExternalInput")
        self.wq_my = nc.dram_tensor("wq_my", [D, MYH * HD], F32, kind="ExternalInput")
        self.wk_my = nc.dram_tensor("wk_my", [D, MYH * HD], F32, kind="ExternalInput")
        self.wv_my = nc.dram_tensor("wv_my", [D, MYH * HD], F32, kind="ExternalInput")
        self.woT = nc.dram_tensor("woT", [D, D], F32, kind="ExternalInput")
        self.bo = nc.dram_tensor("bo", [1, D], F32, kind="ExternalInput")
        self.gamma = nc.dram_tensor("gamma", [1, D], F32, kind="ExternalInput")
        self.beta = nc.dram_tensor("beta", [1, D], F32, kind="ExternalInput")
        self.scale_in = nc.dram_tensor("scale_in", [1, 1], F32, kind="ExternalInput")
        self.mu2d = nc.dram_tensor("mu2d", [D, ES], F32, kind="ExternalInput")
        self.w1_my = nc.dram_tensor("w1_my", [MYE, D, HFF], F32, kind="ExternalInput")
        self.b1_my = nc.dram_tensor("b1_my", [MYE, HFF], F32, kind="ExternalInput")
        self.w2_my = nc.dram_tensor("w2_my", [MYE, HFF, D], F32, kind="ExternalInput")
        self.b2_my = nc.dram_tensor("b2_my", [MYE, D], F32, kind="ExternalInput")
        self.y = nc.dram_tensor("y", [TH, D], F32, kind="ExternalOutput")

        self.d_xlnT_own = [nc.dram_tensor(f"d_xlnT_own{k}", [D, 512], BF16)
                           for k in range(4)]
        self.d_xlnT_g = [nc.dram_tensor(f"d_xlnT_g{k}", [4 * D, 512], BF16)
                         for k in range(4)]
        self.d_q = [nc.dram_tensor(f"d_q{h}", [HD, S], BF16) for h in range(MYH)]
        self.d_k = [nc.dram_tensor(f"d_k{h}", [HD, S], BF16) for h in range(MYH)]
        self.d_v = [nc.dram_tensor(f"d_v{h}", [S, HD], BF16) for h in range(MYH)]
        self.d_ctx_own = [nc.dram_tensor(f"d_ctx_own{h}", [HD, S], BF16)
                          for h in range(MYH)]
        self.d_ctx_g = [nc.dram_tensor(f"d_ctx_g{h}", [4 * HD, S], BF16)
                        for h in range(MYH)]
        self.d_x1 = nc.dram_tensor("d_x1", [TH, D], F32)
        self.d_xln3 = nc.dram_tensor("d_xln3", [TH, D], BF16)
        self.d_xln3T = nc.dram_tensor("d_xln3T", [D, TH], BF16)
        self.d_P2 = nc.dram_tensor("d_P2", [TH, ES], BF16)
        self.d_P2T = nc.dram_tensor("d_P2T", [ES, TH], BF16)
        self.d_U_p = [nc.dram_tensor(f"d_U_p{e}", [4 * SLOT, UW], F32)
                      for e in range(MYE)]
        self.d_U_m = [nc.dram_tensor(f"d_U_m{e}", [SLOT, UW], F32)
                      for e in range(MYE)]
        self.d_sp = [nc.dram_tensor(f"d_sp{e}", [SLOT, D], BF16)
                     for e in range(MYE)]
        self.d_sg = [nc.dram_tensor(f"d_sg{e}", [4 * SLOT, D], BF16)
                     for e in range(MYE)]

        self.dbg = {}
        if self.debug:
            def mkdbg(name, src, shape, dt):
                t = nc.dram_tensor("dbg_" + name, shape, dt, kind="ExternalOutput")
                self.dbg[name] = (t, src)
            for kk in range(4):
                mkdbg(f"xlnT_g{kk}", self.d_xlnT_g[kk], [4 * D, 512], BF16)
                mkdbg(f"xlnT_o{kk}", self.d_xlnT_own[kk], [D, 512], BF16)
            for h in range(MYH):
                mkdbg(f"q{h}", self.d_q[h], [HD, S], BF16)
                mkdbg(f"k{h}", self.d_k[h], [HD, S], BF16)
                mkdbg(f"v{h}", self.d_v[h], [S, HD], BF16)
            mkdbg("x1", self.d_x1, [TH, D], F32)
            mkdbg("xln3", self.d_xln3, [TH, D], BF16)
            mkdbg("xln3T", self.d_xln3T, [D, TH], BF16)
            mkdbg("P2", self.d_P2, [TH, ES], BF16)
            mkdbg("P2T", self.d_P2T, [ES, TH], BF16)

        with ExitStack() as ctx:
            en = ctx.enter_context
            self.BIGA = en(nc.sbuf_tensor("BIGA", [128, 32768], BF16))
            self.BIGB = en(nc.sbuf_tensor("BIGB", [128, 24576], BF16))
            self.STAGE = en(nc.sbuf_tensor("STAGE", [128, 7168], BF16))
            self.sbx = en(nc.sbuf_tensor("sbx", [128, D], F32))
            self.sbf1 = en(nc.sbuf_tensor("sbf1", [128, D], F32))
            self.sbf2 = en(nc.sbuf_tensor("sbf2", [128, D], F32))
            self.sbf3 = en(nc.sbuf_tensor("sbf3", [128, D], F32))
            self.evf = en(nc.sbuf_tensor("evf", [128, UW], F32))
            self.bc = en(nc.sbuf_tensor("bc_pool", [128, 3 * D], F32))
            self.Lt = en(nc.sbuf_tensor("Lt", [128, 512], F32))
            self.masks = en(nc.sbuf_tensor("masks", [128, 2048], BF16))
            self.ident = en(nc.sbuf_tensor("ident", [128, 128], BF16))
            self.ones_fc = en(nc.sbuf_tensor("ones_fc", [128, 1], F32))
            self.ones_fr = en(nc.sbuf_tensor("ones_fr", [1, 128], F32))
            self.ones_bc = en(nc.sbuf_tensor("ones_bc", [128, 1], BF16))
            self.ones4 = en(nc.sbuf_tensor("ones4", [128, 4], BF16))
            self.eps_ln = en(nc.sbuf_tensor("eps_ln", [128, 1], F32))
            self.scale_col = en(nc.sbuf_tensor("scale_col", [128, 1], F32))
            self.r_all = en(nc.sbuf_tensor("r_all", [128, NT], F32))
            self.c_mean = en(nc.sbuf_tensor("c_mean", [128, 1], F32))
            self.c_ssq = en(nc.sbuf_tensor("c_ssq", [128, 1], F32))
            self.c_var = en(nc.sbuf_tensor("c_var", [128, 1], F32))
            self.c_rstd = en(nc.sbuf_tensor("c_rstd", [128, 1], F32))
            self.c_tmp = en(nc.sbuf_tensor("c_tmp", [128, 1], F32))
            self.c_tmp2 = en(nc.sbuf_tensor("c_tmp2", [128, 1], F32))
            self.recip_row = en(nc.sbuf_tensor("recip_row", [1, 512], F32))
            self.g_r = en(nc.sbuf_tensor("g_row", [1, D], F32))
            self.ps = [en(nc.psum_tensor(f"ps{i}", [128, 512], F32))
                       for i in range(6)]
            self.psT = en(nc.psum_tensor("psT", [128, 512], BF16))
            self.psW = en(nc.psum_tensor("psW", [128, 512], F32))
            for nm in ("gd", "gd2", "wd", "pe", "act", "dve", "pe2", "act2", "act3",
                       "la", "lb", "lc", "lx", "lr"):
                setattr(self, "s_" + nm, Sem(en(nc.semaphore("s_" + nm))))
            self.st_s = [Sem(en(nc.semaphore(f"s_sts{i}"))) for i in range(8)]
            self.cc_sems = [Sem(en(nc.semaphore(f"s_cc{i}"))) for i in range(16)]
            self.cc_used = 0

            self.phase0()
            if self.stage >= 1:
                self.phase_A0()
            if self.stage >= 2:
                self.phase_A2()
            if self.stage == 2:
                self.flush("A2only")
            if self.stage >= 3:
                self.phase_A3()
            if self.stage >= 4:
                self.phase_A5()
            if self.stage >= 5:
                self.phase_F0()
                self.phase_F2()
            if self.stage >= 6:
                self.phase_F3()
            if 4 <= self.stage < 6:
                self.flush("A5F3partial")
            if self.stage >= 7:
                self.phase_F4()
            if self.stage == 7:
                self.flush("F4only")
            if self.stage >= 8:
                self.phase_F5()

            if self.debug:
                k = self
                gd_now = k.s_gd.cnt + 16 * len(self.dbg)
                k.s_gd.cnt = gd_now
                st_tots_d = [(s.h, s.cnt) for s in self.st_s]
                cc_n_d = self.cc_used

                def g_dbg(g):
                    for i in range(cc_n_d):
                        g.wait_ge(self.cc_sems[i].h, 1)
                    for hh, cc in st_tots_d:
                        g.wait_ge(hh, cc)
                    for name, (t, src) in k.dbg.items():
                        g.dma_start(t.ap(), src.ap()).then_inc(k.s_gd.h, 16)
                    g.wait_ge(k.s_gd.h, gd_now)
                self.on("gpsimd", g_dbg)
                self.flush("dbg")

            st_tots = [(s.h, s.cnt) for s in self.st_s]
            gd_f, gd2_f, wd_f = self.s_gd.cnt, self.s_gd2.cnt, self.s_wd.cnt
            cc_n = self.cc_used

            def fin(sy):
                for hh, cc in st_tots:
                    sy.wait_ge(hh, cc)
                sy.wait_ge(self.s_gd.h, gd_f)
                sy.wait_ge(self.s_gd2.h, gd2_f)
                sy.wait_ge(self.s_wd.h, wd_f)
                for i in range(cc_n):
                    sy.wait_ge(self.cc_sems[i].h, 1)
            self.on("sync", fin)
            self.flush("fin")
        return nc

    def phase0(self):
        k = self
        k.s_gd2.cnt += 7
        gd_target = k.s_gd.cnt + 16 * 48

        def g_body(g):
            g.memset(k.masks[:, :], 1.0).then_inc(k.s_gd2.h, 1)
            g.wait_ge(k.s_gd2.h, 1)
            for r in range(4):
                g.affine_select(out=k.masks[:, 512 * r:512 * (r + 1)],
                                in_=k.masks[:, 512 * r:512 * (r + 1)],
                                compare_op=ALU.is_ge, fill=0.0, base=-128 * r,
                                pattern=[[1, 512]], channel_multiplier=-1
                                ).then_inc(k.s_gd2.h, 1)
            g.memset(k.ident[:, :], 0.0).then_inc(k.s_gd2.h, 1)
            g.wait_ge(k.s_gd2.h, 6)
            g.affine_select(out=k.ident[:, :], in_=k.ident[:, :],
                            compare_op=ALU.not_equal, fill=1.0, base=0,
                            pattern=[[-1, 128]], channel_multiplier=1)
            g.memset(k.ones_fc[:, :], 1.0)
            g.memset(k.ones_fr[:, :], 1.0)
            g.memset(k.ones_bc[:, :], 1.0)
            g.memset(k.ones4[:, :], 0.0).then_inc(k.s_gd2.h, 1)
            g.wait_ge(k.s_gd2.h, 7)
            g.memset(k.ones4[:, 0:1], 1.0)
            g.memset(k.eps_ln[:, :], LN_EPS)
            if _os.environ.get("SIMINIT", "0") == "1":
                g.memset(k.BIGA[:, :], 0.0)
                g.memset(k.STAGE[:, :], 0.0)
            for wi, w in enumerate((k.wq_my, k.wk_my, k.wv_my)):
                for dc in range(16):
                    g.dma_start(
                        k.BIGB[:, 8192 * wi + 512 * dc:8192 * wi + 512 * (dc + 1)],
                        w[128 * dc:128 * (dc + 1), :],
                    ).then_inc(k.s_gd.h, 16)
            g.wait_ge(k.s_gd.h, gd_target)
        k.s_gd.cnt += 16 * 48
        self.on("gpsimd", g_body)
        self.flush("p0")

        for src_ap, width, dst_fn in (
            (self.gamma[:, :], D, lambda j: self.bc[:, 512 * j:512 * (j + 1)]),
            (self.beta[:, :], D, lambda j: self.bc[:, D + 512 * j:D + 512 * (j + 1)]),
            (self.bo[:, :], D, lambda j: self.bc[:, 2 * D + 512 * j:2 * D + 512 * (j + 1)]),
            (self.scale_in[:, :], 1, lambda j: self.scale_col[:, :]),
        ):
            nsl = (width + 511) // 512
            k.s_lr.cnt += 16
            ld_at = k.s_lr.cnt
            pe_at, ac_at = [], []
            for j in range(nsl):
                k.s_pe.cnt += 1
                pe_at.append(k.s_pe.cnt)
                k.s_act.cnt += 1
                ac_at.append(k.s_act.cnt)

            def sy_body(sy, src_ap=src_ap, width=width):
                sy.dma_start(k.g_r[:, 0:width], src_ap).then_inc(k.s_lr.h, 16)
            self.on("sync", sy_body)

            def pe_body(t, nsl=nsl, width=width, ld_at=ld_at, ac_at=tuple(ac_at)):
                t.wait_ge(k.s_lr.h, ld_at)
                for j in range(nsl):
                    w = min(512, width - 512 * j)
                    if j > 0:
                        t.wait_ge(k.s_act.h, ac_at[j - 1])
                    t.matmul(k.psW[:, 0:w], k.ones_fr[:, :],
                             k.g_r[:, 512 * j:512 * j + w],
                             start=True, stop=True).then_inc(k.s_pe.h, 1)
            self.on("tensor", pe_body)

            def sc_body(sc, nsl=nsl, width=width, dst_fn=dst_fn,
                        pe_at=tuple(pe_at)):
                for j in range(nsl):
                    w = min(512, width - 512 * j)
                    sc.wait_ge(k.s_pe.h, pe_at[j])
                    sc.copy(dst_fn(j), k.psW[:, 0:w]).then_inc(k.s_act.h, 1)
            self.on("scalar", sc_body)
            self.flush("bcast")

    def phase_A0(self):
        k = self
        k.e_a0_ln = [0] * NT
        ld_x = [0] * NT
        red_done = [0] * NT
        meansq_done = [0] * NT
        var_done = [0] * NT
        xln_done = [0] * NT
        ssq_done = [0] * NT
        mean_done = [0] * NT
        rstd_done = [0] * NT
        tp_done = [[0] * 4 for _ in range(NT)]
        evac_done = [[0] * 4 for _ in range(NT)]
        st_done = [[0] * 4 for _ in range(NT)]

        for ti in range(NT):
            k.s_lx.cnt += 16
            ld_x[ti] = k.s_lx.cnt
            k.s_dve.cnt += 1
            red_done[ti] = k.s_dve.cnt
            k.s_act.cnt += 1
            ssq_done[ti] = k.s_act.cnt
            k.s_act.cnt += 1
            mean_done[ti] = k.s_act.cnt
            k.s_dve.cnt += 2
            meansq_done[ti] = k.s_dve.cnt
            k.s_dve.cnt += 1
            var_done[ti] = k.s_dve.cnt
            k.s_act3.cnt += 1
            k.e_a0_ln[ti] = k.s_act3.cnt
            k.s_act.cnt += 1
            rstd_done[ti] = k.s_act.cnt
            k.s_dve.cnt += 1
            xln_done[ti] = k.s_dve.cnt
            for q in range(4):
                k.s_pe.cnt += 4
                tp_done[ti][q] = k.s_pe.cnt
                k.s_act2.cnt += 1
                evac_done[ti][q] = k.s_act2.cnt
                k.st_s[q].cnt += 16
                st_done[ti][q] = k.st_s[q].cnt
        cc_gather = [self.new_cc() for _ in range(4)]
        self.cc_gather = cc_gather
        st_tot = [(k.st_s[q].h, k.st_s[q].cnt) for q in range(4)]

        def sy_body(sy):
            for ti in range(NT):
                if ti >= 1:
                    sy.wait_ge(k.s_dve.h, xln_done[ti - 1])
                    sy.wait_ge(k.s_act.h, ssq_done[ti - 1])
                sy.dma_start(k.sbx[:, :],
                             k.x_home[128 * ti:128 * (ti + 1), :]
                             ).then_inc(k.s_lx.h, 16)
        self.on("sync", sy_body)

        def ve_body(v):
            for ti in range(NT):
                v.wait_ge(k.s_lx.h, ld_x[ti])
                if ti >= 1:
                    v.wait_ge(k.s_act.h, rstd_done[ti - 1])
                    v.wait_ge(k.s_pe.h, tp_done[ti - 1][3])
                v.reduce_sum(k.c_tmp[:, :], k.sbx[:, :],
                             axis=AX.X).then_inc(k.s_dve.h, 1)
                v.wait_ge(k.s_act.h, mean_done[ti])
                v.tensor_scalar_mul(k.c_var[:, :], k.c_ssq[:, :],
                                    1.0 / D).then_inc(k.s_dve.h, 1)
                v.tensor_mul(k.c_tmp2[:, :], k.c_mean[:, :],
                             k.c_mean[:, :]).then_inc(k.s_dve.h, 1)
                v.wait_ge(k.s_dve.h, meansq_done[ti])
                v.tensor_sub(k.c_var[:, :], k.c_var[:, :],
                             k.c_tmp2[:, :]).then_inc(k.s_dve.h, 1)
                v.wait_ge(k.s_act.h, rstd_done[ti])
                v.tensor_scalar(k.STAGE[:, 0:2048], k.sbx[:, :],
                                k.c_mean[:, :], k.c_rstd[:, :],
                                ALU.subtract, ALU.mult).then_inc(k.s_dve.h, 1)
        self.on("vector", ve_body)

        def sc_body(sc):
            for ti in range(NT):
                sc.wait_ge(k.s_lx.h, ld_x[ti])
                if ti >= 1:
                    sc.wait_ge(k.s_dve.h, xln_done[ti - 1])
                sc.activation(k.sbf1[:, :], k.sbx[:, :], AF.Square,
                              accum_out=k.c_ssq[:, :]).then_inc(k.s_act.h, 1)
                sc.wait_ge(k.s_dve.h, red_done[ti])
                sc.mul(k.c_mean[:, :], k.c_tmp[:, :],
                       1.0 / D).then_inc(k.s_act.h, 1)
                sc.wait_ge(k.s_dve.h, var_done[ti])
                sc.activation(k.c_rstd[:, :], k.c_var[:, :], AF.Ln,
                              bias=k.eps_ln[:, :]).then_inc(k.s_act3.h, 1)
                sc.wait_ge(k.s_act3.h, k.e_a0_ln[ti])
                sc.activation(k.c_rstd[:, :], k.c_rstd[:, :], AF.Exp,
                              scale=-0.5).then_inc(k.s_act.h, 1)
                for q in range(4):
                    sc.wait_ge(k.s_pe.h, tp_done[ti][q])
                    sl = 2048 + 512 * q
                    if ti >= 1:
                        sc.wait_ge(k.st_s[q].h, st_done[ti - 1][q])
                    sc.copy(k.STAGE[:, sl:sl + 512],
                            k.psT[:, :]).then_inc(k.s_act2.h, 1)
                    sc.wait_ge(k.s_act2.h, evac_done[ti][q])
                    kk = ti // 4
                    dst = k.d_xlnT_own[kk][512 * q:512 * (q + 1),
                                           128 * (ti % 4):128 * (ti % 4) + 128]
                    dst = dst.rearrange("(i p) t -> p i t", p=128)
                    srcv = k.STAGE[:, sl:sl + 512].rearrange(
                        "p (i t) -> p i t", i=4)
                    sc.dma_start(dst, srcv).then_inc(k.st_s[q].h, 16)
        self.on("scalar", sc_body)

        def pe_body(t):
            for ti in range(NT):
                t.wait_ge(k.s_dve.h, xln_done[ti])
                for q in range(4):
                    gi = ti * 4 + q
                    if gi >= 1:
                        pti, pq = divmod(gi - 1, 4)
                        t.wait_ge(k.s_act2.h, evac_done[pti][pq])
                    for i in range(4):
                        dc = 4 * q + i
                        t.transpose(k.psT[:, 128 * i:128 * (i + 1)],
                                    k.STAGE[:, 128 * dc:128 * (dc + 1)],
                                    k.ident[:, :]).then_inc(k.s_pe.h, 1)
        self.on("tensor", pe_body)

        self.a0_spill_tots = st_tot
        self.flush("A0")

    def phase_A2(self):
        NTG = int(_os.environ.get("A2_TG", "16"))
        NOUT = int(_os.environ.get("A2_OUT", "12"))
        NOV = _os.environ.get("A2_NOV", "0") == "1"
        NOVST = _os.environ.get("A2_NOVST", "0") == "1"
        k = self
        ld_sem = lambda tg: (k.s_la if tg % 2 == 0 else k.s_lb)
        ld_done = [0] * 16
        mm_done = [[0] * 12 for _ in range(16)]
        evac_done = [[0] * 12 for _ in range(16)]
        vtp_done = [[0] * 4 for _ in range(16)]
        vte_done = [[0] * 4 for _ in range(16)]
        st_qk = [[0] * 8 for _ in range(16)]
        st_v = [[0] * 4 for _ in range(16)]
        for tg in range(NTG):
            ld_sem(tg).cnt += 16 * 16
            ld_done[tg] = ld_sem(tg).cnt
            for out in range(NOUT):
                k.s_pe.cnt += 1
                mm_done[tg][out] = k.s_pe.cnt
                k.s_act2.cnt += 1
                evac_done[tg][out] = k.s_act2.cnt
                if out < 8:
                    k.st_s[out % 4].cnt += 16
                    st_qk[tg][out] = k.st_s[out % 4].cnt
            if not NOV:
                for hl in range(4):
                    k.s_pe2.cnt += 4
                    vtp_done[tg][hl] = k.s_pe2.cnt
                    k.s_act3.cnt += 1
                    vte_done[tg][hl] = k.s_act3.cnt
                    k.st_s[4 + hl % 2].cnt += 16
                    st_v[tg][hl] = k.st_s[4 + hl % 2].cnt
        self.a2_st_tots = [(s.h, s.cnt) for s in self.st_s[0:6]]

        def a2g_body(g):
            for hh, cc in self.a0_spill_tots:
                g.wait_ge(hh, cc)
            for kk in range(4):
                if kk >= 1:
                    g.wait_ge(self.cc_gather[kk - 1].h, 1)
                g.collective_compute(
                    "AllGather", ALU.bypass, replica_groups=GROUPS,
                    ins=[k.d_xlnT_own[kk].ap().opt()],
                    outs=[k.d_xlnT_g[kk].ap().opt()],
                ).then_inc(self.cc_gather[kk].h)
        self.on("gpsimd", a2g_body)

        def sy_body(sy):
            for tg in range(NTG):
                kk, r = tg // 4, tg % 4
                sy.wait_ge(self.cc_gather[kk].h, 1)
                half = (tg % 2) * 8192
                if tg >= 2:
                    sy.wait_ge(k.s_pe.h, mm_done[tg - 2][NOUT - 1])
                for dc in range(16):
                    sy.dma_start(
                        k.BIGA[:, half + 512 * dc:half + 512 * (dc + 1)],
                        k.d_xlnT_g[kk][2048 * r + 128 * dc:
                                       2048 * r + 128 * (dc + 1), :],
                    ).then_inc(ld_sem(tg).h, 16)
        self.on("sync", sy_body)

        def pe_body(t):
            for tg in range(NTG):
                half = (tg % 2) * 8192
                t.wait_ge(ld_sem(tg).h, ld_done[tg])
                for out in range(NOUT):
                    wi, hl = divmod(out, 4)
                    bank = out % 4
                    gi = tg * NOUT + out
                    if gi >= 4:
                        pt, po = divmod(gi - 4, NOUT)
                        t.wait_ge(k.s_act2.h, evac_done[pt][po])
                    for dc in range(16):
                        mm = t.matmul(
                            k.ps[bank][:, :],
                            k.BIGB[:, 8192 * wi + 512 * dc + 128 * hl:
                                   8192 * wi + 512 * dc + 128 * (hl + 1)],
                            k.BIGA[:, half + 512 * dc:half + 512 * (dc + 1)],
                            start=(dc == 0), stop=(dc == 15))
                        if dc == 15:
                            mm.then_inc(k.s_pe.h, 1)
                for hl in range(0 if NOV else 4):
                    t.wait_ge(k.s_act2.h, evac_done[tg][8 + hl])
                    gi = tg * 4 + hl
                    if gi >= 1:
                        pt, ph = divmod(gi - 1, 4)
                        t.wait_ge(k.s_act3.h, vte_done[pt][ph])
                    vsl = 4096 + 512 * (hl % 2)
                    for i in range(4):
                        t.transpose(k.psT[:, 128 * i:128 * (i + 1)],
                                    k.STAGE[:, vsl + 128 * i:vsl + 128 * (i + 1)],
                                    k.ident[:, :]).then_inc(k.s_pe2.h, 1)
        self.on("tensor", pe_body)

        def sc_body(sc):
            for tg in range(NTG):
                kk, r = tg // 4, tg % 4
                tok = r * 2048 + 512 * kk
                def evac_v(hl):
                    bank = hl
                    vsl = 4096 + 512 * (hl % 2)
                    sc.wait_ge(k.s_pe.h, mm_done[tg][8 + hl])
                    if tg * 4 + hl >= 2:
                        pidx = tg * 4 + hl - 2
                        pt, ph = divmod(pidx, 4)
                        sc.wait_ge(k.s_pe2.h, vtp_done[pt][ph])
                    sc.copy(k.STAGE[:, vsl:vsl + 512],
                            k.ps[bank][:, :]).then_inc(k.s_act2.h, 1)
                for out in range(min(NOUT, 8)):
                    wi, hl = divmod(out, 4)
                    bank = out % 4
                    sc.wait_ge(k.s_pe.h, mm_done[tg][out])
                    sl = 512 * (out % 4)
                    if tg * 8 + out >= 4:
                        pidx = tg * 8 + out - 4
                        pt, po = divmod(pidx, 8)
                        sc.wait_ge(k.st_s[out % 4].h, st_qk[pt][po])
                    sc.copy(k.STAGE[:, sl:sl + 512],
                            k.ps[bank][:, :]).then_inc(k.s_act2.h, 1)
                    sc.wait_ge(k.s_act2.h, evac_done[tg][out])
                    dstt = (k.d_q if wi == 0 else k.d_k)[hl]
                    sc.dma_start(dstt[:, tok:tok + 512],
                                 k.STAGE[:, sl:sl + 512]
                                 ).then_inc(k.st_s[out % 4].h, 16)
                if NOUT > 8 and not NOV:
                    evac_v(0)
                    evac_v(1)
                for hl in range(0 if (NOV or NOUT <= 8) else 4):
                    sc.wait_ge(k.s_pe2.h, vtp_done[tg][hl])
                    osl = 6144 + 512 * (hl % 2)
                    if tg * 4 + hl >= 2:
                        pidx = tg * 4 + hl - 2
                        pt, ph = divmod(pidx, 4)
                        sc.wait_ge(k.st_s[4 + hl % 2].h, st_v[pt][ph])
                    sc.copy(k.STAGE[:, osl:osl + 512],
                            k.psT[:, :]).then_inc(k.s_act3.h, 1)
                    if hl + 2 <= 3:
                        evac_v(hl + 2)
                    if NOVST:
                        sc.dma_start(k.d_v[hl][tok:tok + 512, :],
                                     k.STAGE[:, osl:osl + 512].rearrange(
                                         "p (i c) -> (i p) c", i=4)
                                     ).then_inc(k.st_s[4 + hl % 2].h, 16) if False else None
                        sc.nop().then_inc(k.st_s[4 + hl % 2].h, 16)
                    else:
                        sc.wait_ge(k.s_act3.h, vte_done[tg][hl])
                        dst = k.d_v[hl][tok:tok + 512, :].rearrange(
                            "(i p) c -> p i c", p=128)
                        srcv = k.STAGE[:, osl:osl + 512].rearrange(
                            "p (i c) -> p i c", i=4)
                        sc.dma_start(dst, srcv).then_inc(k.st_s[4 + hl % 2].h, 16)
        self.on("scalar", sc_body)
        self.a2_mm_tot = (k.s_pe.h, k.s_pe.cnt)

    def phase_A3(self):
        k = self
        steps_all = []
        for gg in range(16):
            for tt in range(4 * gg + 4):
                steps_all.append((gg, tt))
        sidx = {s: i for i, s in enumerate(steps_all)}
        NS = len(steps_all)
        SB = [0, 1, 5]
        CB = [4, 2]

        ld_sem = lambda hh: (k.s_la if hh % 2 == 0 else k.s_lb)
        qkv_ld = [0] * MYH
        smm, expd, maskd, ladd, ctxd = {}, {}, {}, {}, {}
        lsum_d, recip_d, bcast_d, bccp_d, norm_d, stout_d = {}, {}, {}, {}, {}, {}
        head_done = [0] * MYH
        NG = MYH * 16

        def walk():
            for hh in range(MYH):
                ld_sem(hh).cnt += 16 * 3
                qkv_ld[hh] = ld_sem(hh).cnt
                for g in range(16):
                    gi = hh * 16 + g
                    K = 4 * g + 4
                    for kt in range(K):
                        k.s_pe.cnt += 1
                        smm[(hh, g, kt)] = k.s_pe.cnt
                        if kt == 0 and gi >= 1:
                            k.s_pe2.cnt += 1
                            bcast_d[gi - 1] = k.s_pe2.cnt
                        k.s_act.cnt += 1
                        expd[(hh, g, kt)] = k.s_act.cnt
                        if kt == 1 and gi >= 1:
                            k.s_act2.cnt += 1
                            bccp_d[gi - 1] = k.s_act2.cnt
                        if kt == min(5, K - 1) and gi >= 1:
                            k.st_s[6].cnt += 16
                            stout_d[gi - 1] = k.st_s[6].cnt
                        if kt >= 4 * g:
                            k.s_dve.cnt += 1
                            maskd[(hh, g, kt)] = k.s_dve.cnt
                        if kt % 2 == 1:
                            k.s_dve.cnt += 1
                            ladd[(hh, g, kt)] = k.s_dve.cnt
                            ladd[(hh, g, kt - 1)] = k.s_dve.cnt
                        k.s_pe2.cnt += 1
                        ctxd[(hh, g, kt)] = k.s_pe2.cnt
                    k.s_pe2.cnt += 1
                    lsum_d[gi] = k.s_pe2.cnt
                    k.s_dve.cnt += 1
                    recip_d[gi] = k.s_dve.cnt
                    k.s_dve.cnt += 1
                    norm_d[gi] = k.s_dve.cnt
                head_done[hh] = k.s_pe2.cnt
            k.s_pe2.cnt += 1
            bcast_d[NG - 1] = k.s_pe2.cnt
            k.s_act2.cnt += 1
            bccp_d[NG - 1] = k.s_act2.cnt
            k.st_s[6].cnt += 16
            stout_d[NG - 1] = k.st_s[6].cnt
        walk()
        a2_gate = self.a2_st_tots
        ctx_st_tot = k.st_s[6].cnt

        def prev_step(hh, g, kt, back):
            i = sidx[(g, kt)]
            if i >= back:
                pg, pk = steps_all[i - back]
                return (hh, pg, pk)
            if hh > 0:
                pg, pk = steps_all[NS - back + i]
                return (hh - 1, pg, pk)
            return None

        def pt_ready(hh, g, kt):
            if kt >= 4 * g:
                return (k.s_dve.h, maskd[(hh, g, kt)])
            return (k.s_act.h, expd[(hh, g, kt)])

        def sy_body(sy):
            for hh, cc in a2_gate:
                sy.wait_ge(hh, cc)
            sy.wait_ge(self.a2_mm_tot[0], self.a2_mm_tot[1])
            for hh in range(MYH):
                BIG = k.BIGA if hh % 2 == 0 else k.BIGB
                if hh >= 2:
                    sy.wait_ge(k.s_pe2.h, head_done[hh - 2])
                sy.dma_start(BIG[:, 0:8192], k.d_q[hh][:, :]
                             ).then_inc(ld_sem(hh).h, 16)
                sy.dma_start(BIG[:, 8192:16384], k.d_k[hh][:, :]
                             ).then_inc(ld_sem(hh).h, 16)
                sy.dma_start(
                    BIG[:, 16384:24576].rearrange("p (kt c) -> p kt c", c=128),
                    k.d_v[hh].rearrange("(kt p) c -> p kt c", p=128),
                ).then_inc(ld_sem(hh).h, 16)
        self.on("sync", sy_body)

        def emit_bcast(t, gi):
            t.wait_ge(k.s_dve.h, recip_d[gi])
            t.matmul(k.ps[3][:, :], k.ones_fr[:, :], k.recip_row[:, :],
                     start=True, stop=True, skip_group_check=True
                     ).then_inc(k.s_pe2.h, 1)

        def pe_body(t):
            for hh in range(MYH):
                BIG = k.BIGA if hh % 2 == 0 else k.BIGB
                t.wait_ge(ld_sem(hh).h, qkv_ld[hh])
                for g in range(16):
                    gi = hh * 16 + g
                    K = 4 * g + 4
                    for step in range(K + 2):
                        if step < K:
                            kt = step
                            ps3b = prev_step(hh, g, kt, 3)
                            if ps3b is not None:
                                t.wait_ge(k.s_act.h, expd[ps3b])
                            t.matmul(k.ps[SB[kt % 3]][:, :],
                                     BIG[:, 8192 + 128 * kt:8192 + 128 * (kt + 1)],
                                     BIG[:, 512 * g:512 * (g + 1)],
                                     start=True, stop=True,
                                     skip_group_check=True
                                     ).then_inc(k.s_pe.h, 1)
                            if kt == 0 and gi >= 1:
                                emit_bcast(t, gi - 1)
                        if step >= 2:
                            kt = step - 2
                            sh, cn = pt_ready(hh, g, kt)
                            t.wait_ge(sh, cn)
                            if kt == 0 and gi >= 2:
                                t.wait_ge(k.s_dve.h, norm_d[gi - 2])
                            t.matmul(k.ps[CB[gi % 2]][:, :],
                                     BIG[:, 16384 + 128 * kt:16384 + 128 * (kt + 1)],
                                     k.STAGE[:, 512 * (kt % 4):512 * (kt % 4) + 512],
                                     start=(kt == 0), stop=(kt == K - 1),
                                     skip_group_check=True
                                     ).then_inc(k.s_pe2.h, 1)
                    t.wait_ge(k.s_dve.h, ladd[(hh, g, K - 1)])
                    if gi >= 1:
                        t.wait_ge(k.s_act2.h, bccp_d[gi - 1])
                    t.matmul(k.ps[3][0:1, :], k.ones_fc[:, :],
                             k.evf[:, 0:512],
                             start=True, stop=False, skip_group_check=True)
                    t.matmul(k.ps[3][0:1, :], k.ones_fc[:, :],
                             k.evf[:, 512:1024],
                             start=False, stop=True, skip_group_check=True
                             ).then_inc(k.s_pe2.h, 1)
            emit_bcast(t, NG - 1)
        self.on("tensor", pe_body)

        def emit_bccp(sc, gi):
            sc.wait_ge(k.s_pe2.h, bcast_d[gi])
            if gi >= 1:
                sc.wait_ge(k.s_dve.h, norm_d[gi - 1])
            sc.copy(k.sbf1[:, 0:512], k.ps[3][:, :]).then_inc(k.s_act2.h, 1)

        def emit_store(sc, gi):
            hh, g = divmod(gi, 16)
            sc.wait_ge(k.s_dve.h, norm_d[gi])
            if gi >= 1:
                sc.wait_ge(k.st_s[6].h, stout_d[gi - 1])
            dst = k.d_ctx_own[hh][:, 512 * g:512 * (g + 1)]
            sc.dma_start(dst, k.STAGE[:, 2048 + 512 * (gi % 2):
                                      2560 + 512 * (gi % 2)]
                         ).then_inc(k.st_s[6].h, 16)

        def sc_body(sc):
            for hh, cc in a2_gate:
                sc.wait_ge(hh, cc)
            for hh in range(MYH):
                for g in range(16):
                    gi = hh * 16 + g
                    K = 4 * g + 4
                    for kt in range(K):
                        sc.wait_ge(k.s_pe.h, smm[(hh, g, kt)])
                        ps4b = prev_step(hh, g, kt, 4)
                        if ps4b is not None:
                            sc.wait_ge(k.s_pe2.h, ctxd[ps4b])
                            sc.wait_ge(k.s_dve.h, ladd[ps4b])
                        sc.activation(
                            k.STAGE[:, 512 * (kt % 4):512 * (kt % 4) + 512],
                            k.ps[SB[kt % 3]][:, :], AF.Exp,
                            scale=ISQ).then_inc(k.s_act.h, 1)
                        if kt == 1 and gi >= 1:
                            emit_bccp(sc, gi - 1)
                        if kt == min(5, K - 1) and gi >= 1:
                            emit_store(sc, gi - 1)
            emit_bccp(sc, NG - 1)
            emit_store(sc, NG - 1)
        self.on("scalar", sc_body)

        def ve_body(v):
            for hh, cc in a2_gate:
                v.wait_ge(hh, cc)
            for hh in range(MYH):
                for g in range(16):
                    gi = hh * 16 + g
                    K = 4 * g + 4
                    for kt in range(K):
                        v.wait_ge(k.s_act.h, expd[(hh, g, kt)])
                        sl = 512 * (kt % 4)
                        if kt >= 4 * g:
                            r = kt - 4 * g
                            v.tensor_mul(k.STAGE[:, sl:sl + 512],
                                         k.STAGE[:, sl:sl + 512],
                                         k.masks[:, 512 * r:512 * (r + 1)]
                                         ).then_inc(k.s_dve.h, 1)
                            v.wait_ge(k.s_dve.h, maskd[(hh, g, kt)])
                        if kt % 2 == 1:
                            psl = 512 * ((kt - 1) % 4)
                            if kt == 1:
                                if gi >= 1:
                                    v.wait_ge(k.s_pe2.h, lsum_d[gi - 1])
                                v.tensor_copy(k.evf[:, 0:1024],
                                              k.STAGE[:, psl:psl + 1024]
                                              ).then_inc(k.s_dve.h, 1)
                            else:
                                v.wait_ge(k.s_dve.h, ladd[(hh, g, kt - 2)])
                                v.scalar_tensor_tensor(
                                    k.evf[:, 0:1024],
                                    k.STAGE[:, psl:psl + 1024], 1.0,
                                    k.evf[:, 0:1024], ALU.mult, ALU.add
                                ).then_inc(k.s_dve.h, 1)
                    v.wait_ge(k.s_pe2.h, lsum_d[gi])
                    v.reciprocal(k.recip_row[:, :], k.ps[3][0:1, :]
                                 ).then_inc(k.s_dve.h, 1)
                    v.wait_ge(k.s_act2.h, bccp_d[gi])
                    v.wait_ge(k.s_dve.h, recip_d[gi])
                    if gi >= 2:
                        v.wait_ge(k.st_s[6].h, stout_d[gi - 2])
                    v.tensor_tensor(k.STAGE[:, 2048 + 512 * (gi % 2):
                                            2560 + 512 * (gi % 2)],
                                    k.ps[CB[gi % 2]][:, :],
                                    k.sbf1[:, 0:512], ALU.mult
                                    ).then_inc(k.s_dve.h, 1)
        self.on("vector", ve_body)

        cc_a2a = [self.new_cc() for _ in range(MYH)]
        self.cc_a2a = cc_a2a
        head_st = [stout_d[16 * hh + 15] for hh in range(MYH)]

        def g_body(g):
            for hh in range(MYH):
                g.wait_ge(k.st_s[6].h, head_st[hh])
                if hh >= 1:
                    g.wait_ge(cc_a2a[hh - 1].h, 1)
                g.collective_compute(
                    "AllGather", ALU.bypass, replica_groups=GROUPS,
                    ins=[k.d_ctx_own[hh].ap().opt()],
                    outs=[k.d_ctx_g[hh].ap().opt()],
                ).then_inc(cc_a2a[hh].h)
        self.on("gpsimd", g_body)
        self.flush("A2A3")

    def phase_A5(self):
        k = self
        gd_w = k.s_gd.cnt + 16 * 16
        k.s_gd.cnt = gd_w
        self.a5_gdw = gd_w

        def g_body(g):
            g.wait_ge(self.cc_a2a[0].h, 1)
            for dc in range(16):
                g.dma_start(k.BIGA[:, 2048 * dc:2048 * (dc + 1)],
                            k.woT[128 * dc:128 * (dc + 1), :]
                            ).then_inc(k.s_gd.h, 16)
        self.on("gpsimd", g_body)

        ld_sem = lambda ti: (k.s_la if ti % 2 == 0 else k.s_lb)
        e_ln = [[0] * 3 for _ in range(NT)]
        ld_ctx = [0] * NT
        ld_x = [0] * NT
        o_done = [[0] * 4 for _ in range(NT)]
        x1p_done = [[0] * 4 for _ in range(NT)]
        xbo_done = [0] * NT
        m = {}
        xln3_done = [0] * NT
        x1st_done = [0] * NT
        x3st_done = [0] * NT
        tp_done = [[0] * 4 for _ in range(NT)]
        tpe_done = [[0] * 4 for _ in range(NT)]
        st_sp = [[0] * 4 for _ in range(NT)]
        rt_done = [0] * NT
        act_last = [0] * NT

        for ti in range(NT):
            ld_sem(ti).cnt += 16 * 4
            ld_ctx[ti] = ld_sem(ti).cnt
            k.s_lx.cnt += 16
            ld_x[ti] = k.s_lx.cnt
            k.s_dve.cnt += 1
            xbo_done[ti] = k.s_dve.cnt
            for jg in range(4):
                k.s_pe.cnt += 1
                o_done[ti][jg] = k.s_pe.cnt
                k.s_dve.cnt += 1
                x1p_done[ti][jg] = k.s_dve.cnt
            k.st_s[7].cnt += 16
            x1st_done[ti] = k.st_s[7].cnt
            k.s_dve.cnt += 1
            m[("red2", ti)] = k.s_dve.cnt
            k.s_act.cnt += 2
            m[("ms2", ti)] = k.s_act.cnt
            k.s_dve.cnt += 3
            m[("var2", ti)] = k.s_dve.cnt
            k.s_act3.cnt += 1
            e_ln[ti][0] = k.s_act3.cnt
            k.s_act.cnt += 1
            m[("rstd2", ti)] = k.s_act.cnt
            k.s_dve.cnt += 1
            m[("xln2", ti)] = k.s_dve.cnt
            k.s_dve.cnt += 1
            m[("red3", ti)] = k.s_dve.cnt
            k.s_act.cnt += 2
            m[("ms3", ti)] = k.s_act.cnt
            k.s_dve.cnt += 3
            m[("var3", ti)] = k.s_dve.cnt
            k.s_act3.cnt += 1
            e_ln[ti][1] = k.s_act3.cnt
            k.s_act.cnt += 1
            m[("rstd3", ti)] = k.s_act.cnt
            k.s_dve.cnt += 3
            m[("x3f", ti)] = k.s_dve.cnt
            k.s_dve.cnt += 1
            xln3_done[ti] = k.s_dve.cnt
            k.st_s[5].cnt += 16
            x3st_done[ti] = k.st_s[5].cnt
            k.s_act.cnt += 1
            m[("rt_sq", ti)] = k.s_act.cnt
            k.s_act3.cnt += 1
            e_ln[ti][2] = k.s_act3.cnt
            k.s_act.cnt += 1
            m[("rt_act", ti)] = k.s_act.cnt
            act_last[ti] = k.s_act.cnt
            k.s_dve.cnt += 2
            rt_done[ti] = k.s_dve.cnt
            for q in range(4):
                k.s_pe2.cnt += 4
                tp_done[ti][q] = k.s_pe2.cnt
                k.s_act2.cnt += 1
                tpe_done[ti][q] = k.s_act2.cnt
                k.st_s[q].cnt += 16
                st_sp[ti][q] = k.st_s[q].cnt
        self.a5_sp_tots = [(k.st_s[q].h, k.st_s[q].cnt) for q in range(4)]
        self.a5_x3_tot = (k.st_s[5].h, k.st_s[5].cnt)
        self.a5_x1_tot = (k.st_s[7].h, k.st_s[7].cnt)

        def sy_body(sy):
            for cc in self.cc_a2a:
                sy.wait_ge(cc.h, 1)
            tok0 = (sy.partition_id() % 4) * 2048
            ctx3d = [k.d_ctx_g[hh].ap().rearrange("(r p) t -> p r t", p=128)
                     for hh in range(MYH)]
            for ti in range(NT):
                half = (ti % 2) * 2048
                if ti >= 2:
                    sy.wait_ge(k.s_pe.h, o_done[ti - 2][3])
                for hh in range(MYH):
                    srcv = ctx3d[hh][:, :, 128 * ti:][:, :, bass.ds(tok0, 128)]
                    dst = k.BIGB[:, half + 512 * hh:half + 512 * (hh + 1)
                                 ].rearrange("p (r t) -> p r t", t=128)
                    sy.dma_start(dst, srcv).then_inc(ld_sem(ti).h, 16)
                if ti >= 1:
                    sy.wait_ge(k.s_dve.h, xbo_done[ti - 1])
                    sy.wait_ge(k.s_act.h, m[("ms2", ti - 1)])
                sy.dma_start(k.sbx[:, :], k.x_home[128 * ti:128 * (ti + 1), :]
                             ).then_inc(k.s_lx.h, 16)
        self.on("sync", sy_body)

        def pe_body(t):
            t.wait_ge(k.s_gd.h, gd_w)
            for ti in range(NT):
                half = (ti % 2) * 2048
                t.wait_ge(ld_sem(ti).h, ld_ctx[ti])
                for jg in range(4):
                    if ti >= 1:
                        t.wait_ge(k.s_dve.h, x1p_done[ti - 1][jg])
                    for li in range(16):
                        hh4, r = divmod(li, 4)
                        mc = 4 * r + hh4
                        mm = t.matmul(
                            k.ps[jg][:, :],
                            k.BIGB[:, half + 128 * li:half + 128 * (li + 1)],
                            k.BIGA[:, 2048 * mc + 512 * jg:
                                   2048 * mc + 512 * (jg + 1)],
                            start=(li == 0), stop=(li == 15),
                            skip_group_check=True)
                        if li == 15:
                            mm.then_inc(k.s_pe.h, 1)
                t.wait_ge(k.s_dve.h, xln3_done[ti])
                for q in range(4):
                    gi = ti * 4 + q
                    if gi >= 1:
                        pti, pq = divmod(gi - 1, 4)
                        t.wait_ge(k.s_act2.h, tpe_done[pti][pq])
                    for i in range(4):
                        dc = 4 * q + i
                        t.transpose(
                            k.psT[:, 128 * i:128 * (i + 1)],
                            k.STAGE[:, 2048 + 128 * dc:2048 + 128 * (dc + 1)],
                            k.ident[:, :]).then_inc(k.s_pe2.h, 1)
        self.on("tensor", pe_body)

        def ve_body(v):
            for ti in range(NT):
                v.wait_ge(k.s_lx.h, ld_x[ti])
                if ti >= 1:
                    v.wait_ge(k.s_act.h, act_last[ti - 1])
                v.tensor_add(k.sbf3[:, :], k.sbx[:, :],
                             k.bc[:, 2 * D:3 * D]).then_inc(k.s_dve.h, 1)
                v.wait_ge(k.s_dve.h, xbo_done[ti])
                for jg in range(4):
                    v.wait_ge(k.s_pe.h, o_done[ti][jg])
                    if ti >= 1 and jg == 0:
                        v.wait_ge(k.st_s[7].h, x1st_done[ti - 1])
                    v.scalar_tensor_tensor(
                        k.sbf1[:, 512 * jg:512 * (jg + 1)],
                        k.ps[jg][:, :], 1.0,
                        k.sbf3[:, 512 * jg:512 * (jg + 1)],
                        ALU.mult, ALU.add).then_inc(k.s_dve.h, 1)
                v.wait_ge(k.s_dve.h, x1p_done[ti][3])
                v.reduce_sum(k.c_tmp[:, :], k.sbf1[:, :],
                             axis=AX.X).then_inc(k.s_dve.h, 1)
                v.wait_ge(k.s_act.h, m[("ms2", ti)])
                v.tensor_scalar_mul(k.c_var[:, :], k.c_ssq[:, :],
                                    1.0 / D).then_inc(k.s_dve.h, 1)
                v.tensor_mul(k.c_tmp2[:, :], k.c_mean[:, :], k.c_mean[:, :]
                             ).then_inc(k.s_dve.h, 1)
                v.wait_ge(k.s_dve.h, m[("var2", ti)] - 1)
                v.tensor_sub(k.c_var[:, :], k.c_var[:, :], k.c_tmp2[:, :]
                             ).then_inc(k.s_dve.h, 1)
                v.wait_ge(k.s_act.h, m[("rstd2", ti)])
                v.tensor_scalar(k.sbf2[:, :], k.sbf1[:, :], k.c_mean[:, :],
                                k.c_rstd[:, :], ALU.subtract, ALU.mult
                                ).then_inc(k.s_dve.h, 1)
                v.wait_ge(k.s_dve.h, m[("xln2", ti)])
                v.reduce_sum(k.c_tmp[:, :], k.sbf2[:, :],
                             axis=AX.X).then_inc(k.s_dve.h, 1)
                v.wait_ge(k.s_act.h, m[("ms3", ti)])
                v.tensor_scalar_mul(k.c_var[:, :], k.c_ssq[:, :],
                                    1.0 / D).then_inc(k.s_dve.h, 1)
                v.tensor_mul(k.c_tmp2[:, :], k.c_mean[:, :], k.c_mean[:, :]
                             ).then_inc(k.s_dve.h, 1)
                v.wait_ge(k.s_dve.h, m[("var3", ti)] - 1)
                v.tensor_sub(k.c_var[:, :], k.c_var[:, :], k.c_tmp2[:, :]
                             ).then_inc(k.s_dve.h, 1)
                v.wait_ge(k.s_act.h, m[("rstd3", ti)])
                v.tensor_scalar(k.sbf3[:, :], k.sbf2[:, :], k.c_mean[:, :],
                                k.c_rstd[:, :], ALU.subtract, ALU.mult
                                ).then_inc(k.s_dve.h, 1)
                v.wait_ge(k.s_dve.h, m[("x3f", ti)] - 2)
                v.tensor_mul(k.sbf2[:, :], k.sbf3[:, :], k.bc[:, 0:D]
                             ).then_inc(k.s_dve.h, 1)
                v.wait_ge(k.s_dve.h, m[("x3f", ti)] - 1)
                v.tensor_add(k.sbf3[:, :], k.sbf2[:, :], k.bc[:, D:2 * D]
                             ).then_inc(k.s_dve.h, 1)
                v.wait_ge(k.s_dve.h, m[("x3f", ti)])
                if ti >= 1:
                    v.wait_ge(k.s_pe2.h, tp_done[ti - 1][3])
                    v.wait_ge(k.st_s[5].h, x3st_done[ti - 1])
                v.tensor_copy(k.STAGE[:, 2048:4096], k.sbf3[:, :]
                              ).then_inc(k.s_dve.h, 1)
                v.wait_ge(k.s_act.h, m[("rt_act", ti)])
                v.tensor_scalar_min(k.c_rstd[:, :], k.c_rstd[:, :], 1e12
                                    ).then_inc(k.s_dve.h, 1)
                v.wait_ge(k.s_dve.h, rt_done[ti] - 1)
                v.tensor_mul(k.r_all[:, ti:ti + 1], k.c_rstd[:, :],
                             k.scale_col[:, :]).then_inc(k.s_dve.h, 1)
        self.on("vector", ve_body)

        def sc_body(sc):
            for ti in range(NT):
                if ti >= 1:
                    sc.wait_ge(k.s_dve.h, rt_done[ti - 1])
                sc.wait_ge(k.s_dve.h, x1p_done[ti][3])
                sc.dma_start(k.d_x1[128 * ti:128 * (ti + 1), :], k.sbf1[:, :]
                             ).then_inc(k.st_s[7].h, 16)
                sc.activation(k.STAGE[:, 0:2048], k.sbf1[:, :], AF.Square,
                              accum_out=k.c_ssq[:, :]).then_inc(k.s_act.h, 1)
                sc.wait_ge(k.s_dve.h, m[("red2", ti)])
                sc.mul(k.c_mean[:, :], k.c_tmp[:, :], 1.0 / D
                       ).then_inc(k.s_act.h, 1)
                sc.wait_ge(k.s_dve.h, m[("var2", ti)])
                sc.activation(k.c_rstd[:, :], k.c_var[:, :], AF.Ln,
                              bias=k.eps_ln[:, :]).then_inc(k.s_act3.h, 1)
                sc.wait_ge(k.s_act3.h, e_ln[ti][0])
                sc.activation(k.c_rstd[:, :], k.c_rstd[:, :], AF.Exp,
                              scale=-0.5).then_inc(k.s_act.h, 1)
                sc.wait_ge(k.s_dve.h, m[("xln2", ti)])
                sc.activation(k.STAGE[:, 0:2048], k.sbf2[:, :], AF.Square,
                              accum_out=k.c_ssq[:, :]).then_inc(k.s_act.h, 1)
                sc.wait_ge(k.s_dve.h, m[("red3", ti)])
                sc.mul(k.c_mean[:, :], k.c_tmp[:, :], 1.0 / D
                       ).then_inc(k.s_act.h, 1)
                sc.wait_ge(k.s_dve.h, m[("var3", ti)])
                sc.activation(k.c_rstd[:, :], k.c_var[:, :], AF.Ln,
                              bias=k.eps_ln[:, :]).then_inc(k.s_act3.h, 1)
                sc.wait_ge(k.s_act3.h, e_ln[ti][1])
                sc.activation(k.c_rstd[:, :], k.c_rstd[:, :], AF.Exp,
                              scale=-0.5).then_inc(k.s_act.h, 1)
                sc.wait_ge(k.s_dve.h, xln3_done[ti])
                sc.dma_start(k.d_xln3[128 * ti:128 * (ti + 1), :],
                             k.STAGE[:, 2048:4096]).then_inc(k.st_s[5].h, 16)
                sc.activation(k.STAGE[:, 0:2048], k.sbf3[:, :], AF.Square,
                              accum_out=k.c_ssq[:, :]).then_inc(k.s_act.h, 1)
                sc.wait_ge(k.s_act.h, m[("rt_sq", ti)])
                sc.activation(k.c_rstd[:, :], k.c_ssq[:, :], AF.Ln
                              ).then_inc(k.s_act3.h, 1)
                sc.wait_ge(k.s_act3.h, e_ln[ti][2])
                sc.activation(k.c_rstd[:, :], k.c_rstd[:, :], AF.Exp,
                              scale=-0.5).then_inc(k.s_act.h, 1)
                for q in range(4):
                    sc.wait_ge(k.s_pe2.h, tp_done[ti][q])
                    sl = 4096 + 512 * q
                    if ti >= 1:
                        sc.wait_ge(k.st_s[q].h, st_sp[ti - 1][q])
                    sc.copy(k.STAGE[:, sl:sl + 512], k.psT[:, :]
                            ).then_inc(k.s_act2.h, 1)
                    sc.wait_ge(k.s_act2.h, tpe_done[ti][q])
                    dst = k.d_xln3T[512 * q:512 * (q + 1),
                                    128 * ti:128 * (ti + 1)]
                    dst = dst.rearrange("(i p) t -> p i t", p=128)
                    srcv = k.STAGE[:, sl:sl + 512].rearrange(
                        "p (i t) -> p i t", i=4)
                    sc.dma_start(dst, srcv).then_inc(k.st_s[q].h, 16)
        self.on("scalar", sc_body)
        self.a5x = {
            "o15": o_done[15][3], "xbo15": xbo_done[15],
            "x1p15": x1p_done[15][3], "red2_15": m[("red2", 15)],
            "x1st15": x1st_done[15], "tp15": tp_done[15][3],
            "tpe15": tpe_done[15][3], "x3st15": x3st_done[15],
            "rt_done": list(rt_done)}

    def phase_F0(self):
        k = self
        ld = [0] * 16
        sq_done = [0] * 16
        cast_done = [0] * 16
        mm_done = [0] * 16
        for dc in range(16):
            k.s_lx.cnt += 16
            ld[dc] = k.s_lx.cnt
            k.s_act.cnt += 1
            sq_done[dc] = k.s_act.cnt
            k.s_dve.cnt += 1
            cast_done[dc] = k.s_dve.cnt
            k.s_pe.cnt += 1
            mm_done[dc] = k.s_pe.cnt
        k.s_act.cnt += 4
        ln4 = k.s_act.cnt
        k.s_act.cnt += 1
        rn_act = k.s_act.cnt
        k.s_dve.cnt += 1
        rn_min = k.s_dve.cnt
        pe_b, ac_b = [], []
        for j in range(4):
            k.s_pe.cnt += 1
            pe_b.append(k.s_pe.cnt)
            k.s_act.cnt += 1
            ac_b.append(k.s_act.cnt)

        def sy_body(sy):
            sy.wait_ge(k.s_dve.h, self.a5x["xbo15"])
            for dc in range(16):
                if dc >= 1:
                    sy.wait_ge(k.s_act.h, sq_done[dc - 1])
                    sy.wait_ge(k.s_dve.h, cast_done[dc - 1])
                sy.dma_start(k.sbx[:, :], k.mu2d[128 * dc:128 * (dc + 1), :]
                             ).then_inc(k.s_lx.h, 16)
        self.on("sync", sy_body)

        def sc_body(sc):
            sc.wait_ge(k.s_dve.h, self.a5x["red2_15"])
            sc.wait_ge(k.st_s[7].h, self.a5x["x1st15"])
            for dc in range(16):
                sc.wait_ge(k.s_lx.h, ld[dc])
                if dc >= 1:
                    sc.wait_ge(k.s_pe.h, mm_done[dc - 1])
                sc.activation(k.sbf1[:, :], k.sbx[:, :], AF.Square
                              ).then_inc(k.s_act.h, 1)
            sc.wait_ge(k.s_pe.h, mm_done[15])
            for j in range(4):
                sc.activation(k.g_r[0:1, 512 * j:512 * (j + 1)],
                              k.ps[2 + j][0:1, :], AF.Ln
                              ).then_inc(k.s_act.h, 1)
            sc.wait_ge(k.s_act.h, ln4)
            sc.activation(k.g_r[0:1, 0:D], k.g_r[0:1, 0:D], AF.Exp,
                          scale=-0.5).then_inc(k.s_act.h, 1)
            for j in range(4):
                sc.wait_ge(k.s_pe.h, pe_b[j])
                sc.copy(k.bc[:, 2 * D + 512 * j:2 * D + 512 * (j + 1)],
                        k.psW[:, :]).then_inc(k.s_act.h, 1)
        self.on("scalar", sc_body)

        def ve_body(v):
            v.wait_ge(k.s_pe.h, self.a5x["o15"])
            for dc in range(16):
                v.wait_ge(k.s_lx.h, ld[dc])
                v.tensor_copy(k.BIGA[:, 2048 * dc:2048 * (dc + 1)], k.sbx[:, :]
                              ).then_inc(k.s_dve.h, 1)
            v.wait_ge(k.s_act.h, rn_act)
            v.tensor_scalar_min(k.g_r[0:1, 0:D], k.g_r[0:1, 0:D], 1e12
                                ).then_inc(k.s_dve.h, 1)
        self.on("vector", ve_body)

        def pe_body(t):
            t.wait_ge(k.s_dve.h, self.a5x["x1p15"])
            for dc in range(16):
                t.wait_ge(k.s_act.h, sq_done[dc])
                for j in range(4):
                    mm = t.matmul(k.ps[2 + j][0:1, :], k.ones_fc[:, :],
                                  k.sbf1[:, 512 * j:512 * (j + 1)],
                                  start=(dc == 0), stop=(dc == 15),
                                  skip_group_check=True)
                    if j == 3:
                        mm.then_inc(k.s_pe.h, 1)
            t.wait_ge(k.s_dve.h, rn_min)
            for j in range(4):
                if j > 0:
                    t.wait_ge(k.s_act.h, ac_b[j - 1])
                t.matmul(k.psW[:, :], k.ones_fr[:, :],
                         k.g_r[0:1, 512 * j:512 * (j + 1)],
                         start=True, stop=True, skip_group_check=True
                         ).then_inc(k.s_pe.h, 1)
        self.on("tensor", pe_body)
        self.f0x = {"mm15": mm_done[15], "rnact": rn_act, "acb3": ac_b[3],
                    "cast15": cast_done[15]}

    def phase_F2(self):
        k = self
        ld_sem = lambda ti: (k.s_la if ti % 2 == 0 else k.s_lb)
        ld = [0] * NT
        mm_done = [[0] * 4 for _ in range(NT)]
        stt_done = [[0] * 4 for _ in range(NT)]
        exp_done = [[0] * 4 for _ in range(NT)]
        tp_done = [[0] * 4 for _ in range(NT)]
        tpe_done = [[0] * 4 for _ in range(NT)]
        st_p2 = [[0] * 4 for _ in range(NT)]
        st_p2t = [[0] * 4 for _ in range(NT)]
        for ti in range(NT):
            ld_sem(ti).cnt += 16 * 16
            ld[ti] = ld_sem(ti).cnt
            for eg in range(4):
                k.s_pe.cnt += 1
                mm_done[ti][eg] = k.s_pe.cnt
                k.s_dve.cnt += 1
                stt_done[ti][eg] = k.s_dve.cnt
                k.s_act.cnt += 1
                exp_done[ti][eg] = k.s_act.cnt
                k.st_s[eg].cnt += 16
                st_p2[ti][eg] = k.st_s[eg].cnt
                k.s_pe2.cnt += 4
                tp_done[ti][eg] = k.s_pe2.cnt
                k.s_act2.cnt += 1
                tpe_done[ti][eg] = k.s_act2.cnt
                k.st_s[4 + eg % 2].cnt += 16
                st_p2t[ti][eg] = k.st_s[4 + eg % 2].cnt
        gate = self.a5_sp_tots
        self.f2_p2_tots = [(k.st_s[i].h, k.st_s[i].cnt) for i in range(4)]
        self.f2_p2t_tots = [(k.st_s[4 + i].h, k.st_s[4 + i].cnt) for i in range(2)]

        def sy_body(sy):
            for hh, cc in gate:
                sy.wait_ge(hh, cc)
            sy.wait_ge(k.s_pe.h, self.a5x["o15"])
            for ti in range(NT):
                half = (ti % 2) * 2048
                if ti >= 2:
                    sy.wait_ge(k.s_pe.h, mm_done[ti - 2][3])
                for dc in range(16):
                    srcv = k.d_xln3T[128 * dc:128 * (dc + 1),
                                     128 * ti:128 * (ti + 1)]
                    sy.dma_start(k.BIGB[:, half + 128 * dc:half + 128 * (dc + 1)],
                                 srcv).then_inc(ld_sem(ti).h, 16)
        self.on("sync", sy_body)

        def pe_body(t):
            t.wait_ge(k.s_act.h, self.f0x["rnact"])
            t.wait_ge(k.s_act2.h, self.a5x["tpe15"])
            t.wait_ge(k.s_dve.h, self.f0x["cast15"])
            for ti in range(NT):
                half = (ti % 2) * 2048
                t.wait_ge(ld_sem(ti).h, ld[ti])
                if ti >= 1:
                    t.wait_ge(k.s_dve.h, stt_done[ti - 1][3])
                for dc in range(16):
                    for eg in range(4):
                        mm = t.matmul(k.ps[eg][:, :],
                                      k.BIGB[:, half + 128 * dc:
                                             half + 128 * (dc + 1)],
                                      k.BIGA[:, 2048 * dc + 512 * eg:
                                             2048 * dc + 512 * (eg + 1)],
                                      start=(dc == 0), stop=(dc == 15),
                                      skip_group_check=True)
                        if dc == 15:
                            mm.then_inc(k.s_pe.h, 1)
                for eg in range(4):
                    t.wait_ge(k.s_act.h, exp_done[ti][eg])
                    gi = ti * 4 + eg
                    if gi >= 1:
                        pti, pq = divmod(gi - 1, 4)
                        t.wait_ge(k.s_act2.h, tpe_done[pti][pq])
                    sl = 512 * eg
                    for i in range(4):
                        t.transpose(k.psT[:, 128 * i:128 * (i + 1)],
                                    k.STAGE[:, sl + 128 * i:sl + 128 * (i + 1)],
                                    k.ident[:, :]).then_inc(k.s_pe2.h, 1)
        self.on("tensor", pe_body)

        def ve_body(v):
            v.wait_ge(k.s_pe.h, self.f0x["mm15"])
            v.wait_ge(k.s_act.h, self.f0x["acb3"])
            for ti in range(NT):
                v.wait_ge(k.s_dve.h, self.a5x["rt_done"][ti])
                for eg in range(4):
                    v.wait_ge(k.s_pe.h, mm_done[ti][eg])
                    if ti >= 1:
                        v.wait_ge(k.s_act.h, exp_done[ti - 1][eg])
                    v.scalar_tensor_tensor(
                        k.sbf1[:, 512 * eg:512 * (eg + 1)], k.ps[eg][:, :],
                        k.r_all[:, ti:ti + 1],
                        k.bc[:, 2 * D + 512 * eg:2 * D + 512 * (eg + 1)],
                        ALU.mult, ALU.mult).then_inc(k.s_dve.h, 1)
        self.on("vector", ve_body)

        def sc_body(sc):
            sc.wait_ge(k.s_pe2.h, self.a5x["tp15"])
            sc.wait_ge(k.st_s[5].h, self.a5x["x3st15"])
            for ti in range(NT):
                for eg in range(4):
                    sc.wait_ge(k.s_dve.h, stt_done[ti][eg])
                    sl = 512 * eg
                    if ti >= 1:
                        sc.wait_ge(k.s_pe2.h, tp_done[ti - 1][eg])
                        sc.wait_ge(k.st_s[eg].h, st_p2[ti - 1][eg])
                    sc.activation(k.STAGE[:, sl:sl + 512],
                                  k.sbf1[:, 512 * eg:512 * (eg + 1)],
                                  AF.Exp).then_inc(k.s_act.h, 1)
                    sc.wait_ge(k.s_act.h, exp_done[ti][eg])
                    sc.dma_start(
                        k.d_P2[128 * ti:128 * (ti + 1),
                               512 * eg:512 * (eg + 1)],
                        k.STAGE[:, sl:sl + 512]).then_inc(k.st_s[eg].h, 16)
                for eg in range(4):
                    sc.wait_ge(k.s_pe2.h, tp_done[ti][eg])
                    sl = 2048 + 512 * (eg % 2)
                    gi = ti * 4 + eg
                    if gi >= 2:
                        pti, pq = divmod(gi - 2, 4)
                        sc.wait_ge(k.st_s[4 + eg % 2].h, st_p2t[pti][pq])
                    sc.copy(k.STAGE[:, sl:sl + 512], k.psT[:, :]
                            ).then_inc(k.s_act2.h, 1)
                    sc.wait_ge(k.s_act2.h, tpe_done[ti][eg])
                    dst = k.d_P2T[512 * eg:512 * (eg + 1),
                                  128 * ti:128 * (ti + 1)]
                    dst = dst.rearrange("(i p) t -> p i t", p=128)
                    srcv = k.STAGE[:, sl:sl + 512].rearrange(
                        "p (i t) -> p i t", i=4)
                    sc.dma_start(dst, srcv).then_inc(k.st_s[4 + eg % 2].h, 16)
        self.on("scalar", sc_body)
        self.f2x = {"mm15": mm_done[15][3],
                    "stt15": stt_done[15][3]}

    def phase_F3(self):
        k = self
        ld_sem = lambda ec: (k.s_la if ec % 2 == 0 else k.s_lb)
        k.s_lc.cnt += 16 * NT
        lda_done = k.s_lc.cnt
        ORD = [4 * j + el for el in range(4) for j in range(4)]
        ld = [0] * 16
        mm_done = [0] * 16
        ev_done = [0] * 16
        st_done = [0] * 16
        for i in range(16):
            ld_sem(i).cnt += 16 * 16
            ld[i] = ld_sem(i).cnt
            k.s_pe.cnt += 1
            mm_done[i] = k.s_pe.cnt
            k.s_act.cnt += 5
            ev_done[i] = k.s_act.cnt
            k.st_s[6].cnt += 16
            st_done[i] = k.st_s[6].cnt
        gate = self.f2_p2_tots + [self.a5_x3_tot]
        rs_st = [st_done[4 * el + 3] for el in range(4)]

        def sy_body(sy):
            for hh, cc in gate:
                sy.wait_ge(hh, cc)
            sy.wait_ge(k.s_pe.h, self.f2x["mm15"])
            for tc in range(NT):
                sy.dma_start(k.BIGA[:, 2048 * tc:2048 * (tc + 1)],
                             k.d_xln3[128 * tc:128 * (tc + 1), :]
                             ).then_inc(k.s_lc.h, 16)
            for i in range(16):
                ec = ORD[i]
                half = (i % 2) * 2048
                if i >= 2:
                    sy.wait_ge(k.s_pe.h, mm_done[i - 2])
                for tc in range(16):
                    srcv = k.d_P2[128 * tc:128 * (tc + 1),
                                  128 * ec:128 * (ec + 1)]
                    sy.dma_start(k.BIGB[:, half + 128 * tc:half + 128 * (tc + 1)],
                                 srcv).then_inc(ld_sem(i).h, 16)
        self.on("sync", sy_body)

        def pe_body(t):
            t.wait_ge(k.s_dve.h, self.f2x["stt15"])
            t.wait_ge(k.s_act.h, self.f0x["acb3"])
            t.wait_ge(k.s_lc.h, lda_done)
            for i in range(16):
                ec = ORD[i]
                half = (i % 2) * 2048
                t.wait_ge(ld_sem(i).h, ld[i])
                if i >= 1:
                    t.wait_ge(k.s_act.h, ev_done[i - 1])
                for tc in range(16):
                    for dg in range(4):
                        t.matmul(k.ps[dg][:, :],
                                 k.BIGB[:, half + 128 * tc:half + 128 * (tc + 1)],
                                 k.BIGA[:, 2048 * tc + 512 * dg:
                                        2048 * tc + 512 * (dg + 1)],
                                 start=(tc == 0), stop=(tc == 15),
                                 skip_group_check=True)
                    mm = t.matmul(k.psW[:, 0:4],
                                  k.BIGB[:, half + 128 * tc:half + 128 * (tc + 1)],
                                  k.ones4[:, :],
                                  start=(tc == 0), stop=(tc == 15),
                                  skip_group_check=True)
                    if tc == 15:
                        mm.then_inc(k.s_pe.h, 1)
        self.on("tensor", pe_body)

        def sc_body(sc):
            for i in range(16):
                ec = ORD[i]
                el, j = ec % 4, ec // 4
                sc.wait_ge(k.s_pe.h, mm_done[i])
                if i >= 1:
                    sc.wait_ge(k.st_s[6].h, st_done[i - 1])
                for dg in range(4):
                    sc.copy(k.evf[:, 512 * dg:512 * (dg + 1)],
                            k.ps[dg][:, :]).then_inc(k.s_act.h, 1)
                sc.copy(k.evf[:, 2048:2052], k.psW[:, 0:4]
                        ).then_inc(k.s_act.h, 1)
                sc.wait_ge(k.s_act.h, ev_done[i])
                sc.dma_start(k.d_U_p[el][128 * j:128 * (j + 1), :],
                             k.evf[:, :]).then_inc(k.st_s[6].h, 16)
        self.on("scalar", sc_body)

        cc_rs = [self.new_cc() for _ in range(4)]
        self.cc_rs = cc_rs

        def g_body(g):
            g.wait_ge(k.s_gd.h, self.a5_gdw)
            for dc in range(16):
                g.dma_start(k.BIGB[:, 8192 + 512 * dc:8192 + 512 * (dc + 1)],
                            k.w1_my[0, 128 * dc:128 * (dc + 1), :]
                            ).then_inc(k.s_gd.h, 16)
            for hc in range(4):
                g.dma_start(
                    k.BIGB[:, 16384 + 2048 * hc:16384 + 2048 * (hc + 1)],
                    k.w2_my[0, 128 * hc:128 * (hc + 1), :]
                ).then_inc(k.s_wd.h, 16)
            for el in range(4):
                g.wait_ge(k.st_s[6].h, rs_st[el])
                if el >= 1:
                    g.wait_ge(cc_rs[el - 1].h, 1)
                g.collective_compute(
                    "ReduceScatter", ALU.add, replica_groups=GROUPS,
                    ins=[k.d_U_p[el].ap().opt()],
                    outs=[k.d_U_m[el].ap().opt()],
                ).then_inc(cc_rs[el].h)
        self.on("gpsimd", g_body)
        self.flush("A5F3")

    def phase_F4(self):
        k = self
        uld = [0] * 4
        b1ld = [0] * 4
        b2ld = [0] * 4
        w1ld = [0] * 4
        w2ld = [0] * 4
        recz = [0] * 4
        castu = [0] * 4
        tpu = [[0] * 4 for _ in range(4)]
        tpue = [[0] * 4 for _ in range(4)]
        b1b = [0] * 4
        b1bc = [0] * 4
        b2b = [[0] * 4 for _ in range(4)]
        b2bc = [[0] * 4 for _ in range(4)]
        h1m = [0] * 4
        hpre = [0] * 4
        hglu = [0] * 4
        tph = [0] * 4
        tphe = [0] * 4
        som = [[0] * 4 for _ in range(4)]
        sod = [[0] * 4 for _ in range(4)]
        stso = [0] * 4
        for el in range(4):
            k.s_lx.cnt += 16
            uld[el] = k.s_lx.cnt
            k.s_lr.cnt += 16
            b1ld[el] = k.s_lr.cnt
            k.s_lr.cnt += 16
            b2ld[el] = k.s_lr.cnt
            k.s_gd.cnt += 16 * 16
            w1ld[el] = k.s_gd.cnt
            k.s_wd.cnt += 16 * 4
            w2ld[el] = k.s_wd.cnt
            k.s_dve.cnt += 1
            recz[el] = k.s_dve.cnt
            k.s_dve.cnt += 1
            castu[el] = k.s_dve.cnt
            for q in range(4):
                k.s_pe.cnt += 4
                tpu[el][q] = k.s_pe.cnt
                k.s_act.cnt += 1
                tpue[el][q] = k.s_act.cnt
            k.s_pe.cnt += 1
            b1b[el] = k.s_pe.cnt
            k.s_act.cnt += 1
            b1bc[el] = k.s_act.cnt
            for j in range(4):
                k.s_pe.cnt += 1
                b2b[el][j] = k.s_pe.cnt
                k.s_act.cnt += 1
                b2bc[el][j] = k.s_act.cnt
            k.s_pe.cnt += 1
            h1m[el] = k.s_pe.cnt
            k.s_dve.cnt += 1
            hpre[el] = k.s_dve.cnt
            k.s_act.cnt += 1
            hglu[el] = k.s_act.cnt
            k.s_pe.cnt += 4
            tph[el] = k.s_pe.cnt
            k.s_act.cnt += 1
            tphe[el] = k.s_act.cnt
            for dg in range(4):
                k.s_pe.cnt += 1
                som[el][dg] = k.s_pe.cnt
                k.s_dve.cnt += 1
                sod[el][dg] = k.s_dve.cnt
            k.st_s[7].cnt += 16
            stso[el] = k.st_s[7].cnt
        slot_st_tot = k.st_s[7].cnt

        def sy_body(sy):
            for el in range(4):
                sy.wait_ge(self.cc_rs[el].h, 1)
                if el >= 1:
                    sy.wait_ge(k.s_dve.h, castu[el - 1])
                sy.dma_start(k.evf[:, :],
                             k.d_U_m[el][:, :]
                             ).then_inc(k.s_lx.h, 16)
                if el >= 1:
                    sy.wait_ge(k.s_pe.h, b2b[el - 1][3])
                sy.dma_start(k.g_r[0:1, 0:HFF],
                             k.b1_my[el:el + 1, :]).then_inc(k.s_lr.h, 16)
                sy.wait_ge(k.s_pe.h, b1b[el])
                sy.dma_start(k.g_r[0:1, 0:D],
                             k.b2_my[el:el + 1, :]).then_inc(k.s_lr.h, 16)
        self.on("sync", sy_body)

        def wload(g, el):
            if el >= 1:
                g.wait_ge(k.s_pe.h, h1m[el - 1])
            for dc in range(16):
                g.dma_start(k.BIGB[:, 8192 + 512 * dc:8192 + 512 * (dc + 1)],
                            k.w1_my[el, 128 * dc:128 * (dc + 1), :]
                            ).then_inc(k.s_gd.h, 16)
            if el >= 1:
                g.wait_ge(k.s_pe.h, som[el - 1][3])
            for hc in range(4):
                g.dma_start(
                    k.BIGB[:, 16384 + 2048 * hc:16384 + 2048 * (hc + 1)],
                    k.w2_my[el, 128 * hc:128 * (hc + 1), :]
                ).then_inc(k.s_wd.h, 16)
        self.f4_wload = wload

        def ve_body(v):
            for hh, cc in self.f2_p2_tots:
                v.wait_ge(hh, cc)
            for el in range(4):
                v.wait_ge(k.s_lx.h, uld[el])
                v.reciprocal(k.c_tmp[:, :], k.evf[:, 2048:2049]
                             ).then_inc(k.s_dve.h, 1)
                if el >= 1:
                    v.wait_ge(k.s_pe.h, tpu[el - 1][3])
                v.tensor_copy(k.STAGE[:, 0:2048], k.evf[:, 0:2048]
                              ).then_inc(k.s_dve.h, 1)
                v.wait_ge(k.s_pe.h, h1m[el])
                v.wait_ge(k.s_act.h, b1bc[el])
                v.wait_ge(k.s_dve.h, recz[el])
                if el >= 1:
                    v.wait_ge(k.s_act.h, hglu[el - 1])
                v.scalar_tensor_tensor(k.sbf1[:, 0:512], k.ps[0][:, :],
                                       k.c_tmp[:, :], k.bc[:, 0:512],
                                       ALU.mult, ALU.add).then_inc(k.s_dve.h, 1)
                for dg in range(4):
                    v.wait_ge(k.s_pe.h, som[el][dg])
                    v.wait_ge(k.s_act.h, b2bc[el][dg])
                    if el >= 1 and dg == 0:
                        v.wait_ge(k.st_s[7].h, stso[el - 1])
                    v.tensor_add(
                        k.STAGE[:, 5120 + 512 * dg:5120 + 512 * (dg + 1)],
                        k.ps[1 + dg][:, :],
                        k.bc[:, D + 512 * dg:D + 512 * (dg + 1)]
                    ).then_inc(k.s_dve.h, 1)
        self.on("vector", ve_body)

        def pe_body(t):
            for el in range(4):
                t.wait_ge(k.s_dve.h, castu[el])
                for q in range(4):
                    gi = el * 4 + q
                    if gi >= 1:
                        pel, pq = divmod(gi - 1, 4)
                        t.wait_ge(k.s_act.h, tpue[pel][pq])
                    for i in range(4):
                        dc = 4 * q + i
                        t.transpose(
                            k.psT[:, 128 * i:128 * (i + 1)],
                            k.STAGE[:, 128 * dc:128 * (dc + 1)],
                            k.ident[:, :]).then_inc(k.s_pe.h, 1)
                t.wait_ge(k.s_lr.h, b1ld[el])
                if el >= 1:
                    t.wait_ge(k.s_act.h, b1bc[el - 1])
                    t.wait_ge(k.s_dve.h, hpre[el - 1])
                t.matmul(k.psW[:, 0:HFF], k.ones_fr[:, :], k.g_r[0:1, 0:HFF],
                         start=True, stop=True, skip_group_check=True
                         ).then_inc(k.s_pe.h, 1)
                t.wait_ge(k.s_lr.h, b2ld[el])
                for j in range(4):
                    if el + j >= 1:
                        prev = b2bc[el][j - 1] if j >= 1 else b2bc[el - 1][3]
                        t.wait_ge(k.s_act.h, prev)
                        if j == 0 and el >= 1:
                            t.wait_ge(k.s_dve.h, sod[el - 1][3])
                    t.matmul(k.ps[5][:, :], k.ones_fr[:, :],
                             k.g_r[0:1, 512 * j:512 * (j + 1)],
                             start=True, stop=True, skip_group_check=True
                             ).then_inc(k.s_pe.h, 1)
                t.wait_ge(k.s_act.h, tpue[el][3])
                t.wait_ge(k.s_gd.h, w1ld[el])
                if el >= 1:
                    t.wait_ge(k.s_dve.h, hpre[el - 1])
                for dc in range(16):
                    mm = t.matmul(
                        k.ps[0][:, :],
                        k.STAGE[:, 2048 + 128 * dc:2048 + 128 * (dc + 1)],
                        k.BIGB[:, 8192 + 512 * dc:8192 + 512 * (dc + 1)],
                        start=(dc == 0), stop=(dc == 15),
                        skip_group_check=True)
                    if dc == 15:
                        mm.then_inc(k.s_pe.h, 1)
                t.wait_ge(k.s_act.h, hglu[el])
                if el >= 1:
                    t.wait_ge(k.s_act.h, tphe[el - 1])
                for i in range(4):
                    t.transpose(k.psT[:, 128 * i:128 * (i + 1)],
                                k.STAGE[:, 4096 + 128 * i:4096 + 128 * (i + 1)],
                                k.ident[:, :]).then_inc(k.s_pe.h, 1)
                t.wait_ge(k.s_act.h, tphe[el])
                t.wait_ge(k.s_wd.h, w2ld[el])
                for dg in range(4):
                    if el >= 1:
                        t.wait_ge(k.s_dve.h, sod[el - 1][dg])
                    for hc in range(4):
                        mm = t.matmul(
                            k.ps[1 + dg][:, :],
                            k.STAGE[:, 4608 + 128 * hc:4608 + 128 * (hc + 1)],
                            k.BIGB[:, 16384 + 2048 * hc + 512 * dg:
                                   16384 + 2048 * hc + 512 * (dg + 1)],
                            start=(hc == 0), stop=(hc == 3),
                            skip_group_check=True)
                        if hc == 3:
                            mm.then_inc(k.s_pe.h, 1)
        self.on("tensor", pe_body)

        def sc_body(sc):
            for hh, cc in self.f2_p2t_tots:
                sc.wait_ge(hh, cc)
            for el in range(4):
                for q in range(4):
                    sc.wait_ge(k.s_pe.h, tpu[el][q])
                    sl = 2048 + 512 * q
                    sc.copy(k.STAGE[:, sl:sl + 512], k.psT[:, :]
                            ).then_inc(k.s_act.h, 1)
                sc.wait_ge(k.s_pe.h, b1b[el])
                if el >= 1:
                    sc.wait_ge(k.s_dve.h, hpre[el - 1])
                sc.copy(k.bc[:, 0:512], k.psW[:, 0:HFF]).then_inc(k.s_act.h, 1)
                for j in range(4):
                    sc.wait_ge(k.s_pe.h, b2b[el][j])
                    if el >= 1:
                        sc.wait_ge(k.s_dve.h, sod[el - 1][j])
                    sc.copy(k.bc[:, D + 512 * j:D + 512 * (j + 1)],
                            k.ps[5][:, :]).then_inc(k.s_act.h, 1)
                sc.wait_ge(k.s_dve.h, hpre[el])
                sc.activation(k.STAGE[:, 4096:4608], k.sbf1[:, 0:512],
                              AF.Gelu).then_inc(k.s_act.h, 1)
                sc.wait_ge(k.s_pe.h, tph[el])
                sc.copy(k.STAGE[:, 4608:5120], k.psT[:, :]
                        ).then_inc(k.s_act.h, 1)
                sc.wait_ge(k.s_dve.h, sod[el][3])
                sc.dma_start(k.d_sp[el][:, :],
                             k.STAGE[:, 5120:7168]).then_inc(k.st_s[7].h, 16)
        self.on("scalar", sc_body)

        cc_sg = [self.new_cc() for _ in range(4)]
        self.cc_sg = cc_sg

        def g_body2(g):
            for el in range(4):
                if el + 1 <= 3:
                    self.f4_wload(g, el + 1)
                if el == 0:
                    g.wait_ge(self.cc_rs[3].h, 1)
                g.wait_ge(k.st_s[7].h, stso[el])
                if el >= 1:
                    g.wait_ge(cc_sg[el - 1].h, 1)
                g.collective_compute(
                    "AllGather", ALU.bypass, replica_groups=GROUPS,
                    ins=[k.d_sp[el].ap().opt()],
                    outs=[k.d_sg[el].ap().opt()],
                ).then_inc(cc_sg[el].h)
        self.on("gpsimd", g_body2)
        self.f4x = {"tphe3": tphe[3], "sod33": sod[3][3],
                    "stso3": stso[3]}

    def phase_F5(self):
        k = self
        ld_sem = lambda ti: (k.s_la if ti % 2 == 0 else k.s_lb)
        k.s_lc.cnt += 16 * 16
        lda = k.s_lc.cnt
        ld = [0] * NT
        ldx = [0] * NT
        mm_done = [0] * NT
        rec_done = [0] * NT
        out_done = [[0] * 4 for _ in range(NT)]
        st_y = [0] * NT
        for ti in range(NT):
            ld_sem(ti).cnt += 16 * 16
            ld[ti] = ld_sem(ti).cnt
            k.s_lx.cnt += 16
            ldx[ti] = k.s_lx.cnt
            k.s_pe.cnt += 1
            mm_done[ti] = k.s_pe.cnt
            k.s_dve.cnt += 1
            rec_done[ti] = k.s_dve.cnt
            for dg in range(4):
                k.s_dve.cnt += 1
                out_done[ti][dg] = k.s_dve.cnt
            k.st_s[6].cnt += 16
            st_y[ti] = k.st_s[6].cnt
        gate = self.f2_p2t_tots + [self.a5_x1_tot]

        def sy_body(sy):
            for hh, cc in gate:
                sy.wait_ge(hh, cc)
            for el in range(4):
                sy.wait_ge(self.cc_sg[el].h, 1)
                for j in range(4):
                    ec = 4 * j + el
                    sy.dma_start(k.BIGA[:, 2048 * ec:2048 * (ec + 1)],
                                 k.d_sg[el][128 * j:128 * (j + 1), :]
                                 ).then_inc(k.s_lc.h, 16)
            for ti in range(NT):
                half = (ti % 2) * 2048
                if ti >= 2:
                    sy.wait_ge(k.s_pe.h, mm_done[ti - 2])
                for ec in range(16):
                    srcv = k.d_P2T[128 * ec:128 * (ec + 1),
                                   128 * ti:128 * (ti + 1)]
                    sy.dma_start(k.BIGB[:, half + 128 * ec:half + 128 * (ec + 1)],
                                 srcv).then_inc(ld_sem(ti).h, 16)
                if ti >= 1:
                    sy.wait_ge(k.s_dve.h, out_done[ti - 1][3])
                sy.dma_start(k.sbx[:, :], k.d_x1[128 * ti:128 * (ti + 1), :]
                             ).then_inc(k.s_lx.h, 16)
        self.on("sync", sy_body)

        def pe_body(t):
            t.wait_ge(k.s_act.h, self.f4x["tphe3"])
            t.wait_ge(k.s_dve.h, self.f4x["sod33"])
            t.wait_ge(k.s_lc.h, lda)
            for ti in range(NT):
                half = (ti % 2) * 2048
                t.wait_ge(ld_sem(ti).h, ld[ti])
                if ti >= 1:
                    t.wait_ge(k.s_dve.h, out_done[ti - 1][3])
                    t.wait_ge(k.s_dve.h, rec_done[ti - 1])
                for ec in range(16):
                    for dg in range(4):
                        t.matmul(k.ps[dg][:, :],
                                 k.BIGB[:, half + 128 * ec:half + 128 * (ec + 1)],
                                 k.BIGA[:, 2048 * ec + 512 * dg:
                                        2048 * ec + 512 * (dg + 1)],
                                 start=(ec == 0), stop=(ec == 15),
                                 skip_group_check=True)
                    mm = t.matmul(k.psW[:, 0:1],
                                  k.BIGB[:, half + 128 * ec:half + 128 * (ec + 1)],
                                  k.ones_bc[:, :],
                                  start=(ec == 0), stop=(ec == 15),
                                  skip_group_check=True)
                    if ec == 15:
                        mm.then_inc(k.s_pe.h, 1)
        self.on("tensor", pe_body)

        def ve_body(v):
            v.wait_ge(k.s_act.h, self.f4x["tphe3"])
            v.wait_ge(k.st_s[7].h, self.f4x["stso3"])
            for ti in range(NT):
                v.wait_ge(k.s_pe.h, mm_done[ti])
                v.wait_ge(k.s_lx.h, ldx[ti])
                v.reciprocal(k.c_tmp[:, :], k.psW[:, 0:1]
                             ).then_inc(k.s_dve.h, 1)
                if ti >= 1:
                    v.wait_ge(k.st_s[6].h, st_y[ti - 1])
                v.wait_ge(k.s_dve.h, rec_done[ti])
                for dg in range(4):
                    v.scalar_tensor_tensor(
                        k.sbf1[:, 512 * dg:512 * (dg + 1)], k.ps[dg][:, :],
                        k.c_tmp[:, :], k.sbx[:, 512 * dg:512 * (dg + 1)],
                        ALU.mult, ALU.add).then_inc(k.s_dve.h, 1)
        self.on("vector", ve_body)

        def sc_body(sc):
            for ti in range(NT):
                sc.wait_ge(k.s_dve.h, out_done[ti][3])
                sc.dma_start(k.y[128 * ti:128 * (ti + 1), :], k.sbf1[:, :]
                             ).then_inc(k.st_s[6].h, 16)
        self.on("scalar", sc_body)
        self.flush("F4F5")


def build_nc(stage=99, debug=False):
    return KB(stage=stage, debug=debug).build()


_NC_CACHE = {}


def _get_nc():
    if "nc" not in _NC_CACHE:
        _NC_CACHE["nc"] = build_nc()
    return _NC_CACHE["nc"]


def make_in_maps(inputs):
    ins = {kk: np.asarray(v) for kk, v in inputs.items()}
    x = ins["L0_attn_x"].astype(np.float32, copy=False)
    wq, wk, wv = ins["L0_attn_wq"], ins["L0_attn_wk"], ins["L0_attn_wv"]
    wo, bo = ins["L0_attn_wo"], ins["L0_attn_bo"]
    gamma, beta = ins["L0_ffn_gamma"], ins["L0_ffn_beta"]
    mu, scale = ins["L0_ffn_mu"], ins["L0_ffn_scale"]
    W1, b1 = ins["L0_ffn_W1"], ins["L0_ffn_b1"]
    W2, b2 = ins["L0_ffn_W2"], ins["L0_ffn_b2"]
    wqT = np.ascontiguousarray(wq.T)
    wkT = np.ascontiguousarray(wk.T)
    wvT = np.ascontiguousarray(wv.T)
    woT = np.ascontiguousarray(wo.T)
    mu2d = np.ascontiguousarray(mu.reshape(D, ES))
    in_maps = []
    for c in range(8):
        bg, j = divmod(c, 4)
        in_maps.append({
            "x_home": np.ascontiguousarray(x[bg, TH * j:TH * (j + 1), :]),
            "wq_my": np.ascontiguousarray(wqT[:, 512 * j:512 * (j + 1)]),
            "wk_my": np.ascontiguousarray(wkT[:, 512 * j:512 * (j + 1)]),
            "wv_my": np.ascontiguousarray(wvT[:, 512 * j:512 * (j + 1)]),
            "woT": woT,
            "bo": np.ascontiguousarray(bo.reshape(1, D)).astype(np.float32),
            "gamma": np.ascontiguousarray(gamma.reshape(1, D)).astype(np.float32),
            "beta": np.ascontiguousarray(beta.reshape(1, D)).astype(np.float32),
            "scale_in": np.ascontiguousarray(scale.reshape(1, 1)).astype(np.float32),
            "mu2d": mu2d,
            "w1_my": np.ascontiguousarray(W1[4 * j:4 * (j + 1)]),
            "b1_my": np.ascontiguousarray(b1[4 * j:4 * (j + 1)]),
            "w2_my": np.ascontiguousarray(W2[4 * j:4 * (j + 1)]),
            "b2_my": np.ascontiguousarray(b2[4 * j:4 * (j + 1)]),
        })
    return in_maps


def kernel(**inputs):
    nc = _get_nc()
    in_maps = make_in_maps(inputs)
    res = run_bass_kernel_spmd(nc, in_maps, core_ids=list(range(8)))
    out = np.empty((B, S, D), np.float32)
    for c in range(8):
        bg, j = divmod(c, 4)
        out[bg, TH * j:TH * (j + 1), :] = res.results[c]["y"]
    return out
